# version 03
# baseline (speedup 1.0000x reference)
import numpy as np
from contextlib import ExitStack

import concourse.bass as bass
import concourse.mybir as mybir
from concourse.bass_utils import run_bass_kernel_spmd

F32 = mybir.dt.float32
BF16 = mybir.dt.bfloat16
AF = mybir.ActivationFunctionType
ALU = mybir.AluOpType
AX = mybir.AxisListType

D = 1024
KC = 8
L = 2048
TC = 256
NCH = L // TC
NST = 16
FF = 4096
FT = 512
NFT = L // FT
EPS = 1e-5
INV_D = 1.0 / D

SAMPLE_ITERS = 50
FULL_ITERS = 22


class _IH:

    __slots__ = ("incs", "pe_val")

    def __init__(self):
        self.incs = []
        self.pe_val = None

    def then_inc(self, sem, n):
        self.incs.append((sem, n))
        return self


class _Rec:

    def __init__(self, real, stream):
        self._real = real
        self._stream = stream

    def __getattr__(self, op):
        fn = getattr(self._real, op)

        def call(*a, **k):
            h = _IH()
            self._stream.append((fn, a, k, h))
            return h

        return call


def _sel_k(n):
    return max(1, int(n * 0.05))


def build(debug=(), phases=("P0", "ce2", "ce4", "P0w", "P0bar", "P1", "P2", "P3", "P4", "P5")):
    nc = bass.Bass("TRN2", target_bir_lowering=False, detect_race_conditions=False)
    dbg = set(debug)
    phset = set(phases)

    def din(name, shape, dt=F32):
        return nc.dram_tensor(name, shape, dt, kind="ExternalInput").ap()

    x_in = din("x", [L, D])
    ipw = din("ipw", [2 * D, D])
    ipb = din("ipb", [2 * D])
    cw = din("cw", [D, 4])
    cb = din("cb", [D])
    bcw = din("bcw", [2 * NST, D])
    bcb = din("bcb", [2 * NST])
    dtw = din("dtw", [D, D])
    dtb = din("dtb", [D])
    alog = din("alog", [D, NST])
    dvec = din("dvec", [D])
    ow = din("ow", [D, D])
    ob_ = din("ob", [D])
    lng = din("lng", [D])
    lnb = din("lnb", [D])
    w1 = din("w1", [D, FF])
    b1 = din("b1", [FF])
    w2 = din("w2", [FF, D])
    b2 = din("b2", [D])
    g2 = din("g2", [D])
    be2 = din("be2", [D])
    wr = din("wr", [D, 8])
    br = din("br", [8])

    out_d = nc.dram_tensor("out", [L, D], F32, kind="ExternalOutput").ap()

    ipwT_d = nc.dram_tensor("ipwT_d", [D, 2 * D], BF16, kind="Internal").ap()
    dtwT_d = nc.dram_tensor("dtwT_d", [D, D], BF16, kind="Internal").ap()
    owT_d = nc.dram_tensor("owT_d", [D, D], BF16, kind="Internal").ap()
    w1bf_d = nc.dram_tensor("w1bf_d", [D, FF], BF16, kind="Internal").ap()
    w1pbf_d = nc.dram_tensor("w1pbf_d", [D, FF], BF16, kind="Internal").ap()
    w2bf_d = nc.dram_tensor("w2bf_d", [FF, D], BF16, kind="Internal").ap()
    w2pbf_d = nc.dram_tensor("w2pbf_d", [FF, D], BF16, kind="Internal").ap()
    bc_d = nc.dram_tensor("bc_d", [2 * NST, L], F32, kind="Internal").ap()
    x2_d = nc.dram_tensor("x2_d", [D, L], F32, kind="Internal").ap()
    stat_d = nc.dram_tensor("stat_d", [1, 1024], F32, kind="Internal").ap()
    prow_d = nc.dram_tensor("prow_d", [1, L], F32, kind="Internal").ap()

    dbg_outs = {}
    tr_d = None

    def dout(name, shape, dt=F32):
        if name in dbg:
            dbg_outs[name] = nc.dram_tensor("dbg_" + name, shape, dt,
                                            kind="ExternalOutput").ap()
            return dbg_outs[name]
        return None

    with ExitStack() as ctx:
        e = ctx.enter_context

        sems = {}
        for s in ("pe", "ve", "po", "ac", "sy", "dpo", "st",
                  "d0", "dsl", "dt0", "dt1", "d1", "d2", "d2a", "d2b",
                  "dw0", "dw1", "dbs", "dbc0", "dbc1", "dx", "dx2", "dst",
                  "d4a", "d4b"):
            sems[s] = [e(nc.semaphore("s_" + s)), 0]

        def mark(inst, s, n=1):
            sems[s][1] += n
            inst.then_inc(sems[s][0], n)
            return sems[s][1]

        def wait(eng, s, v=None):
            if s[0] == "d" or s == "st":
                assert v is None or v == sems[s][1], (s, v, sems[s][1])
                v = None
            eng.wait_ge(sems[s][0], sems[s][1] if v is None else v)

        def dwait(eng, s, v):
            assert v == sems[s][1], (s, v, sems[s][1])
            eng.wait_ge(sems[s][0], v)

        def quiesce(eng):
            for s in ("d0", "dsl", "dt0", "dt1", "d1", "d2", "d2a", "d2b",
                      "dw0", "dw1", "dbs", "dbc0", "dbc1", "dx", "dx2",
                      "dst", "d4a", "d4b", "st"):
                if sems[s][1]:
                    eng.wait_ge(sems[s][0], sems[s][1])

        streams = {k: [] for k in ("pe", "ve", "po", "ac", "sy")}
        PE = _Rec(nc.tensor, streams["pe"])
        VE = _Rec(nc.vector, streams["ve"])
        PO = _Rec(nc.gpsimd, streams["po"])
        AC = _Rec(nc.scalar, streams["ac"])
        SY = _Rec(nc.sync, streams["sy"])

        def tr(out_ap, in_ap):
            p = in_ap.shape[0]
            PE.transpose(out_ap, in_ap, ident[0:p, 0:p])
        ENG = {"pe": PE, "ve": VE, "po": PO, "ac": AC, "sy": SY}

        def barrier():
            quiesce(ENG["sy"])
            if sems["dpo"][1]:
                ENG["po"].wait_ge(sems["dpo"][0], sems["dpo"][1])
            for nm, eng in ENG.items():
                if nm == "ac":
                    ac_mark()
                elif nm == "pe":
                    if streams["pe"] and streams["pe"][-1][0].__name__ in (
                            "matmul", "transpose"):
                        pe_mark()
                    else:
                        mark(eng.nop(), nm)
                elif nm in ("ve", "po"):
                    mark(eng.engine_nop(), nm)
                else:
                    mark(eng.nop(), nm)
            vals = {nm: sems[nm][1] for nm in ENG}
            for nm, eng in ENG.items():
                for nm2 in ENG:
                    if nm2 != nm:
                        wait(eng, nm2, vals[nm2])

        A_sb = e(nc.sbuf_tensor("A_sb", [128, KC, NST], F32))
        cw_sb = e(nc.sbuf_tensor("cw_sb", [128, KC, 4], F32))
        cb_sb = e(nc.sbuf_tensor("cb_sb", [128, KC], F32))
        dvec_sb = e(nc.sbuf_tensor("dvec_sb", [128, KC], F32))
        lng_sb = e(nc.sbuf_tensor("lng_sb", [128, KC], F32))
        lnb_sb = e(nc.sbuf_tensor("lnb_sb", [128, KC], F32))
        g2_sb = e(nc.sbuf_tensor("g2_sb", [128, KC], F32))
        be2_sb = e(nc.sbuf_tensor("be2_sb", [128, KC], F32))
        ipb_sb = e(nc.sbuf_tensor("ipb_sb", [128, 16], F32))
        dtb_sb = e(nc.sbuf_tensor("dtb_sb", [128, KC], F32))
        ob_sb = e(nc.sbuf_tensor("ob_sb", [128, KC], F32))
        bcbB_sb = e(nc.sbuf_tensor("bcbB_sb", [16, 1], F32))
        bcbC_sb = e(nc.sbuf_tensor("bcbC_sb", [16, 1], F32))
        bcb_sb = e(nc.sbuf_tensor("bcb_sb", [32, 1], F32))
        b1_sb = e(nc.sbuf_tensor("b1_sb", [128, 32], F32))
        b1p_sb = e(nc.sbuf_tensor("b1p_sb", [128, 32], F32))
        b2_sb = e(nc.sbuf_tensor("b2_sb", [128, KC], F32))
        b2p_sb = e(nc.sbuf_tensor("b2p_sb", [128, KC], F32))
        bcwT_sb = e(nc.sbuf_tensor("bcwT_sb", [128, KC, 32], BF16))
        wrg_sb = e(nc.sbuf_tensor("wrg_sb", [128, KC, 8], BF16))
        ceb_sb = e(nc.sbuf_tensor("ceb_sb", [128, 8], F32))
        bewb_sb = e(nc.sbuf_tensor("bewb_sb", [128, 8], F32))
        ones_sb = e(nc.sbuf_tensor("ones_sb", [128, 128], F32))
        onesr_sb = e(nc.sbuf_tensor("onesr_sb", [1, 128], F32))
        ident = e(nc.sbuf_tensor("ident", [128, 128], F32))
        half_sb = e(nc.sbuf_tensor("half_sb", [128, 1], F32))
        neg1_sb = e(nc.sbuf_tensor("neg1_sb", [128, 1], F32))
        negh_sb = e(nc.sbuf_tensor("negh_sb", [128, 1], F32))
        one_sb = e(nc.sbuf_tensor("one_sb", [128, 1], F32))
        eps_sb = e(nc.sbuf_tensor("eps_sb", [128, 1], F32))
        thr_sb = e(nc.sbuf_tensor("thr_sb", [128, 8], F32))
        dumA = e(nc.sbuf_tensor("dumA", [128, 4], F32))
        dumB = e(nc.sbuf_tensor("dumB", [128, 4], F32))
        acd1 = e(nc.sbuf_tensor("acd1", [128, 1], F32))
        acd2 = e(nc.sbuf_tensor("acd2", [128, 1], F32))
        vstage = e(nc.sbuf_tensor("vstage", [32, 128], F32))
        b1a_sb = e(nc.sbuf_tensor("b1a_sb", [128, 32], F32))
        b2a_sb = e(nc.sbuf_tensor("b2a_sb", [128, KC], F32))

        pb = [e(nc.psum_tensor(f"pb{i}", [128, 512], F32)) for i in range(8)]

        def bubble():
            VE.tensor_copy(dumA[:], dumB[:])

        def ac_mark(s="ac"):
            return mark(AC.activation(acd1[:], acd2[:], AF.Identity), s)

        def pe_mark(s="pe"):
            fn, a, k, h = streams["pe"][-1]
            assert fn.__name__ in ("matmul", "transpose"), fn.__name__
            if getattr(h, "pe_val", None) is not None:
                return h.pe_val
            sems[s][1] += 1
            h.incs.append((sems[s][0], 1))
            h.pe_val = sems[s][1]
            return sems[s][1]

        p1tr = nc.dram_tensor("dbg_p1trace", [SAMPLE_ITERS + 2 * FULL_ITERS, 24],
                              F32, kind="ExternalOutput").ap()             if "p1trace" in dbg else None
        if p1tr is not None:
            dbg_outs["p1trace"] = p1tr
        tr_i = [0]
        block = e(nc.Block())

        def _body(sync):
            VE.memset(ones_sb[:], 1.0)
            VE.memset(onesr_sb[:], 1.0)
            VE.memset(half_sb[:], 0.5)
            VE.memset(neg1_sb[:], -1.0)
            VE.memset(negh_sb[:], -0.5)
            VE.memset(one_sb[:], 1.0)
            VE.memset(eps_sb[:], EPS)
            VE.memset(dumB[:], 0.0)
            VE.memset(dumA[:], 0.0)
            VE.memset(acd1[:], 0.0)
            VE.memset(acd2[:], 0.0)
            v_const = mark(VE.engine_nop(), "ve")
            PO.memset(ident[:], 0.0)
            PO.affine_select(out=ident[:], in_=ident[:],
                             compare_op=ALU.not_equal, fill=1.0,
                             base=0, pattern=[[-1, 128]], channel_multiplier=1)
            p_ident = mark(PO.engine_nop(), "po")
            wait(PE, "po", p_ident)
            wait(PE, "ve", v_const)

            smalls = [] if "noSmalls" in phset else [
                      (ipb, ipb_sb, 16), (cb, cb_sb, 8), (dtb, dtb_sb, 8),
                      (ob_, ob_sb, 8), (dvec, dvec_sb, 8), (lng, lng_sb, 8),
                      (lnb, lnb_sb, 8), (g2, g2_sb, 8), (be2, be2_sb, 8),
                      (b2, b2_sb, 8), (b1, b1_sb, 32)]
            for si, (src, dst, w) in enumerate(smalls):
                for hw0 in range(0, w, 16):
                    hw = min(16, w - hw0)
                    d_ld = mark(SY.dma_start(
                        vstage[0:hw, :],
                        src[hw0 * 128:(hw0 + hw) * 128]
                        .rearrange("(a p) -> a p", p=128)), "d0", 16)
                    dwait(PE, "d0", d_ld)
                    tr(pb[0][:, 0:hw], vstage[0:hw, :])
                    t_v = pe_mark()
                    wait(VE, "pe", t_v)
                    VE.tensor_copy(dst[:, hw0:hw0 + hw], pb[0][:, 0:hw])
                    v_done = mark(VE.engine_nop(), "ve")
                    wait(PE, "ve", v_done)
                    wait(SY, "ve", v_done)

            if "noBcb" not in phset:
              mark(SY.dma_start(bcb_sb[:], bcb.unsqueeze(1)), "d0", 16)
              wait(PO, "d0")
              mark(PO.dma_start(bcbB_sb[:], bcb_sb[0:16, :]), "dpo", 16)
              mark(PO.dma_start(bcbC_sb[:], bcb_sb[16:32, :]), "dpo", 16)
              mark(SY.dma_start(cw_sb[:],
                                cw.rearrange("(c p) k -> p c k", p=128)), "d0", 16)
            if "noA" not in phset:
              d_alog = mark(SY.dma_start(
                  A_sb[:], alog.rearrange("(c p) n -> p c n", p=128)), "d0", 16)
              dwait(AC, "d0", d_alog)
              AC.activation(A_sb[:], A_sb[:], AF.Exp)
              AC.activation(A_sb[:], A_sb[:], AF.Identity, scale=neg1_sb[:])
              a_A = ac_mark()

            if "noCe" not in phset:
              with ExitStack() as p0ctx:
                e0 = p0ctx.enter_context
                wr_sb = e0(nc.sbuf_tensor("wr_sb", [128, KC, 8], F32))
                br_sb = e0(nc.sbuf_tensor("br_sb", [1, 8], F32))
                cerow = e0(nc.sbuf_tensor("cerow", [1, 16], F32))
                bcw_stage = e0(nc.sbuf_tensor("bcw_stage", [16, D], F32))
                bcw_stage2 = e0(nc.sbuf_tensor("bcw_stage2", [16, D], F32))

                d_wr = mark(SY.dma_start(
                    wr_sb[:], wr.rearrange("(c p) e -> p c e", p=128)), "d0", 16)
                d_br = mark(SY.dma_start(br_sb[:], br.unsqueeze(0)),
                            "d0", 16)
                dwait(VE, "d0", d_br)
                for c in range(KC):
                    VE.tensor_scalar(wrg_sb[:, c, :], wr_sb[:, c, :],
                                     g2_sb[:, c:c + 1], None, ALU.mult)
                v_wrg = mark(VE.engine_nop(), "ve")
                if "ce2" in phset or "ceAll" in phset:
                    gtmp = e0(nc.sbuf_tensor("gtmp", [128, KC, 8], F32))
                    btmp = e0(nc.sbuf_tensor("btmp", [128, KC, 8], F32))
                    cat = e0(nc.sbuf_tensor("cat", [128, 16], F32))
                    VE.tensor_tensor(gtmp[:], wr_sb[:],
                                     g2_sb[:].unsqueeze(2)
                                     .broadcast_to((128, KC, 8)), ALU.mult)
                    VE.tensor_tensor(btmp[:], wr_sb[:],
                                     be2_sb[:].unsqueeze(2)
                                     .broadcast_to((128, KC, 8)), ALU.mult)
                    bubble()
                    VE.reduce_sum(cat[:, 0:8], gtmp[:].transpose([0, 2, 1]),
                                  axis=AX.X)
                    VE.reduce_sum(cat[:, 8:16], btmp[:].transpose([0, 2, 1]),
                                  axis=AX.X)
                    v_rep = mark(VE.engine_nop(), "ve")
                    wait(PE, "ve", v_rep)
                    PE.matmul(pb[1][0:2, 0:16], ones_sb[:, 0:2], cat[:],
                              start=True, stop=True)
                    t_cb = pe_mark()
                    wait(VE, "pe", t_cb)
                    pass
                    VE.tensor_copy(cerow[0:1, 0:8], pb[1][0:1, 0:8])
                    VE.tensor_add(cerow[0:1, 8:16], pb[1][0:1, 8:16], br_sb[:])
                    v_ce = mark(VE.engine_nop(), "ve")
                    wait(SY, "ve", v_ce)
                    d_cw = mark(SY.dma_start(stat_d[0:1, 0:16], cerow[:]),
                                "dst", 16)
                    dwait(SY, "dst", d_cw)
                    mark(SY.dma_start(ceb_sb[:], stat_d[0:1, 0:8]
                                      .partition_broadcast(128).squeeze(1)),
                         "dst", 16)
                    d_ceb = mark(SY.dma_start(
                        bewb_sb[:], stat_d[0:1, 8:16]
                        .partition_broadcast(128).squeeze(1)), "dst", 16)
                    dwait(VE, "dst", d_ceb)
                    v_ceb = mark(VE.engine_nop(), "ve")
                if "ce4" in phset or "ceAll" in phset:
                    mark(SY.dma_start(bcw_stage[:], bcw[0:16, :]), "d0", 16)
                    d_bcw = mark(SY.dma_start(bcw_stage2[:], bcw[16:32, :]),
                                 "d0", 16)
                    dwait(PE, "d0", d_bcw)
                    wait(PE, "ve")
                    evs = {}
                    for c in range(KC):
                        bank = 2 + (c % 2)
                        if c >= 2:
                            wait(PE, "ve", evs[c - 2])
                        tr(pb[bank][:, 0:16],
                           bcw_stage[:, c * 128:(c + 1) * 128])
                        tr(pb[bank][:, 16:32],
                           bcw_stage2[:, c * 128:(c + 1) * 128])
                        t_t = pe_mark()
                        wait(VE, "pe", t_t)
                        VE.tensor_copy(bcwT_sb[:, c, :], pb[bank][:, 0:32])
                        evs[c] = mark(VE.engine_nop(), "ve")
                tsl = [e0(nc.sbuf_tensor(f"tsl{i}", [128, D], F32))
                       for i in range(2)]
                ttb = [e0(nc.sbuf_tensor(f"ttb{i}", [128, 128], BF16))
                       for i in range(2)]

                def transpose_weight(src_, dst_d, O):
                    n_o = O // 128
                    slab_rel = {}
                    tile_rel = {}
                    tile_st = {}
                    ti = 0
                    for obk in range(n_o):
                        r = obk % 2
                        if obk >= 2:
                            wait(SY, "ve", slab_rel[obk - 2])
                        d_sl = mark(SY.dma_start(
                            tsl[r][:], src_[obk * 128:(obk + 1) * 128, :]),
                            "dsl", 16)
                        dwait(PE, "dsl", d_sl)
                        for ib in range(KC):
                            trr = ti % 2
                            bank = 4 + trr
                            if ti >= 2:
                                wait(PE, "ve", tile_rel[ti - 2])
                            tr(pb[bank][:, 0:128],
                               tsl[r][:, ib * 128:(ib + 1) * 128])
                            t_tr = pe_mark()
                            wait(VE, "pe", t_tr)
                            if ti >= 2:
                                dwait(VE, "dt%d" % (ti % 2), tile_st[ti - 2])
                            VE.tensor_copy(ttb[trr][:], pb[bank][:, 0:128])
                            v_ev = mark(VE.engine_nop(), "ve")
                            tile_rel[ti] = v_ev
                            if ib == KC - 1:
                                slab_rel[obk] = v_ev
                            wait(SY, "ve", v_ev)
                            tile_st[ti] = mark(SY.dma_start(
                                dst_d[ib * 128:(ib + 1) * 128,
                                      obk * 128:(obk + 1) * 128],
                                ttb[trr][:]), "dt%d" % (ti % 2), 16)
                            ti += 1

                if "P0w" in phset:
                    transpose_weight(ipw, ipwT_d, 2 * D)
                    transpose_weight(dtw, dtwT_d, D)
                    transpose_weight(ow, owT_d, D)
                if "P0bar" in phset:
                    barrier()

            quiesce(SY)
            t_cat = dout("cat", [128, 16])
            if t_cat is not None:
                wait(SY, "ve")
                mark(SY.dma_start(t_cat, cat[:]), "st", 16)
            t_cerow = dout("cerow", [1, 16])
            if t_cerow is not None:
                mark(SY.dma_start(t_cerow, cerow[:]), "st", 16)
            t_A = dout("A", [128, KC * NST])
            if t_A is not None:
                mark(SY.dma_start(t_A, A_sb[:].rearrange("p c n -> p (c n)")), "st", 16)
            t_ipb = dout("ipb", [128, 16])
            if t_ipb is not None:
                mark(SY.dma_start(t_ipb, ipb_sb[:]), "st", 16)
            t_ceb = dout("ceb", [128, 8])
            if t_ceb is not None:
                mark(SY.dma_start(t_ceb, ceb_sb[:]), "st", 16)
            t_bewb = dout("bewb", [128, 8])
            if t_bewb is not None:
                mark(SY.dma_start(t_bewb, bewb_sb[:]), "st", 16)
            t_ipwT = dout("ipwT", [D, 2 * D], BF16)
            if t_ipwT is not None:
                mark(SY.dma_start(t_ipwT, ipwT_d), "st", 16)
            t_cwsb = dout("cwsb", [128, KC * 4])
            if t_cwsb is not None:
                mark(SY.dma_start(t_cwsb, cw_sb[:].rearrange("p c k -> p (c k)")), "st", 16)
            SY.wait_ge(sems["st"][0], sems["st"][1])

            nW1, kW1 = D * FF, _sel_k(D * FF)
            nW2, kW2 = FF * D, _sel_k(FF * D)
            nb1, kb1 = FF, _sel_k(FF)
            nb2, kb2 = D, _sel_k(D)

            if "noBabs" not in phset:
              VE.scalar_tensor_tensor(b1a_sb[:], b1_sb[:], -1.0, b1_sb[:],
                                      ALU.mult, ALU.max)
              VE.scalar_tensor_tensor(b2a_sb[:], b2_sb[:], -1.0, b2_sb[:],
                                      ALU.mult, ALU.max)
              a_babs = mark(VE.engine_nop(), "ve")

            if "P1" in phset:
              with ExitStack() as p1ctx:
                e1 = p1ctx.enter_context
                NCOL = 8
                lo_sb = e1(nc.sbuf_tensor("lo_sb", [128, NCOL], F32))
                hi_sb = e1(nc.sbuf_tensor("hi_sb", [128, NCOL], F32))
                mid_sb = e1(nc.sbuf_tensor("mid_sb", [128, NCOL], F32))
                K_sb = e1(nc.sbuf_tensor("K_sb", [128, NCOL], F32))
                cnt_sb = e1(nc.sbuf_tensor("cnt_sb", [128, 4 * NCOL], F32))
                tot_sb = e1(nc.sbuf_tensor("tot_sb", [128, NCOL], F32))
                cond_sb = e1(nc.sbuf_tensor("cond_sb", [128, NCOL], F32))
                sc1 = e1(nc.sbuf_tensor("sc1", [128, NCOL], F32))
                sc2 = e1(nc.sbuf_tensor("sc2", [128, NCOL], F32))
                sc3 = e1(nc.sbuf_tensor("sc3", [128, NCOL], F32))
                mveA = e1(nc.sbuf_tensor("mveA", [128, 8192], F32))
                p1f_sb = None
                if "p1f" in dbg:
                    p1f_sb = e1(nc.sbuf_tensor("p1f_sb", [128, 528], F32))
                w1s_sb = e1(nc.sbuf_tensor("w1s_sb", [128, 1024], F32))
                w2s_sb = e1(nc.sbuf_tensor("w2s_sb", [128, 1024], F32))

                d_s1 = mark(SY.dma_start(
                    w1s_sb[:], w1[0:32, :].rearrange("a f -> (a f)")
                    .rearrange("(p q) -> p q", p=128)), "d1", 16)
                d_s2 = mark(SY.dma_start(w2s_sb[:], w2[0:128, :]), "d1", 16)
                dwait(VE, "d1", d_s2)
                VE.scalar_tensor_tensor(w1s_sb[:], w1s_sb[:], -1.0, w1s_sb[:],
                                        ALU.mult, ALU.max)
                VE.scalar_tensor_tensor(w2s_sb[:], w2s_sb[:], -1.0, w2s_sb[:],
                                        ALU.mult, ALU.max)
                a_sabs = mark(VE.engine_nop(), "ve")

                def upd_chain(c0, c1):
                    VE.tensor_copy(tot_sb[:, c0:c1], pb[7][:, c0:c1])
                    bubble()
                    upd_chain2(c0, c1)

                def upd_chain2(c0, c1):
                    VE.tensor_tensor(cond_sb[:, c0:c1], tot_sb[:, c0:c1],
                                     K_sb[:, c0:c1], ALU.is_ge)
                    VE.tensor_sub(sc1[:, c0:c1], mid_sb[:, c0:c1], hi_sb[:, c0:c1])
                    VE.tensor_sub(sc2[:, c0:c1], mid_sb[:, c0:c1], lo_sb[:, c0:c1])
                    VE.tensor_mul(sc3[:, c0:c1], sc1[:, c0:c1], cond_sb[:, c0:c1])
                    VE.tensor_mul(sc1[:, c0:c1], sc2[:, c0:c1], cond_sb[:, c0:c1])
                    VE.tensor_add(hi_sb[:, c0:c1], hi_sb[:, c0:c1], sc3[:, c0:c1])
                    VE.tensor_sub(lo_sb[:, c0:c1], mid_sb[:, c0:c1], sc1[:, c0:c1])
                    bubble()
                    bubble()
                    VE.tensor_add(sc2[:, c0:c1], lo_sb[:, c0:c1], hi_sb[:, c0:c1])
                    bubble()
                    bubble()
                    VE.tensor_scalar(mid_sb[:, c0:c1], sc2[:, c0:c1], 0.5,
                                     None, ALU.mult)
                    bubble()
                    bubble()

                def count_iter(jobs, c0, c1, v_mid):
                    merge = {}
                    for data, col, sub in jobs:
                        n = data.shape[-1]
                        VE.tensor_scalar(mveA[:, 0:n], data,
                                         mid_sb[:, col:col + 1], None,
                                         ALU.is_le, ALU.add,
                                         accum_out=cnt_sb[:, col * 4 + sub:
                                                          col * 4 + sub + 1])
                        merge.setdefault(col, []).append(col * 4 + sub)
                    bubble()
                    for col in range(c0, c1):
                        subs = merge.get(col, [])
                        if len(subs) == 1:
                            VE.tensor_copy(sc3[:, col:col + 1],
                                           cnt_sb[:, subs[0]:subs[0] + 1])
                        else:
                            VE.tensor_add(sc3[:, col:col + 1],
                                          cnt_sb[:, subs[0]:subs[0] + 1],
                                          cnt_sb[:, subs[1]:subs[1] + 1])
                            for s3 in subs[2:]:
                                bubble()
                                VE.tensor_add(sc3[:, col:col + 1],
                                              sc3[:, col:col + 1],
                                              cnt_sb[:, s3:s3 + 1])
                    v_mrg = mark(VE.engine_nop(), "ve")
                    wait(PE, "ve", v_mrg)
                    PE.matmul(pb[7][:, c0:c1], ones_sb[:], sc3[:, c0:c1],
                              start=True, stop=True)
                    t_tot = pe_mark()
                    wait(VE, "pe", t_tot)
                    upd_chain(c0, c1)
                    vm = mark(VE.engine_nop(), "ve")
                    if p1tr is not None:
                        wait(SY, "ve", vm)
                        mark(SY.dma_start(p1tr[tr_i[0]:tr_i[0] + 1, 0:8],
                                          mid_sb[0:1, :]), "st", 16)
                        mark(SY.dma_start(p1tr[tr_i[0]:tr_i[0] + 1, 8:16],
                                          tot_sb[0:1, :]), "st", 16)
                        mark(SY.dma_start(p1tr[tr_i[0]:tr_i[0] + 1, 16:24],
                                          hi_sb[0:1, :]), "st", 16)
                        wait(SY, "st")
                        tr_i[0] += 1
                    return vm

                ns = 131072
                Ks = [max(1, round(kW1 / nW1 * ns)),
                      max(1, round((nW1 - kW1 + 1) / nW1 * ns)),
                      max(1, round(kW2 / nW2 * ns)),
                      max(1, round((nW2 - kW2 + 1) / nW2 * ns)),
                      kb1, nb1 - kb1 + 1, kb2, nb2 - kb2 + 1]
                for col in range(8):
                    VE.memset(lo_sb[:, col:col + 1], -1.0)
                    VE.memset(hi_sb[:, col:col + 1], 65536.0)
                    VE.memset(K_sb[:, col:col + 1], float(Ks[col]))
                    VE.memset(mid_sb[:, col:col + 1], 0.5 * (65536.0 - 1.0))
                v_mid = mark(VE.engine_nop(), "ve")
                for it in range(SAMPLE_ITERS):
                    jobs = [(w1s_sb[:], 0, 0), (w1s_sb[:], 1, 0),
                            (w2s_sb[:], 2, 0), (w2s_sb[:], 3, 0),
                            (b1a_sb[:], 4, 0), (b1a_sb[:], 5, 0),
                            (b2a_sb[:], 6, 0), (b2a_sb[:], 7, 0)]
                    v_mid = count_iter(jobs, 0, 8, v_mid)
                VE.tensor_copy(thr_sb[:, 4:8], hi_sb[:, 4:8])
                VE.tensor_scalar(lo_sb[:, 0:4], hi_sb[:, 0:4], 0.9, None, ALU.mult)
                VE.tensor_scalar(hi_sb[:, 0:4], hi_sb[:, 0:4], 1.1, None, ALU.mult)
                VE.memset(K_sb[:, 0:1], float(kW1))
                VE.memset(K_sb[:, 1:2], float(nW1 - kW1 + 1))
                VE.memset(K_sb[:, 2:3], float(kW2))
                VE.memset(K_sb[:, 3:4], float(nW2 - kW2 + 1))
                bubble()
                VE.tensor_add(sc2[:, 0:4], lo_sb[:, 0:4], hi_sb[:, 0:4])
                bubble()
                VE.tensor_scalar(mid_sb[:, 0:4], sc2[:, 0:4], 0.5, None, ALU.mult)
                v_mid = mark(VE.engine_nop(), "ve")

                for wi, (wsrc, c0) in enumerate(((w1, 0), (w2, 2))):
                    with ExitStack() as selctx:
                        wa = selctx.enter_context(
                            nc.sbuf_tensor(f"wa{wi}", [128, 32768], F32))
                        mpo = selctx.enter_context(
                            nc.sbuf_tensor(f"mpo{wi}", [128, 2, 2048], F32))
                        src_flat = wsrc.rearrange("a f -> (a f)").rearrange(
                            "(p q) -> p q", p=128)
                        for sl in range(4):
                            d_l = mark(SY.dma_start(
                                wa[:, sl * 8192:(sl + 1) * 8192],
                                src_flat[:, sl * 8192:(sl + 1) * 8192]),
                                "d1", 16)
                            dwait(SY, "d1", d_l)
                            dwait(VE, "d1", d_l)
                            VE.scalar_tensor_tensor(
                                wa[:, sl * 8192:(sl + 1) * 8192],
                                wa[:, sl * 8192:(sl + 1) * 8192], -1.0,
                                wa[:, sl * 8192:(sl + 1) * 8192],
                                ALU.mult, ALU.max)
                            bubble()
                        v_abs = mark(VE.engine_nop(), "ve")
                        wait(PO, "ve", v_abs)
                        bot, top = c0, c0 + 1
                        pmk = {}
                        for it in range(FULL_ITERS):
                            for sb_ in range(4):
                                VE.tensor_scalar(
                                    mveA[:, 0:8192],
                                    wa[:, sb_ * 8192:(sb_ + 1) * 8192],
                                    mid_sb[:, bot:bot + 1], None,
                                    ALU.is_le, ALU.add,
                                    accum_out=cnt_sb[:, bot * 4 + sb_:
                                                     bot * 4 + sb_ + 1])
                            bubble()
                            VE.tensor_add(sc3[:, bot:bot + 1],
                                          cnt_sb[:, bot * 4:bot * 4 + 1],
                                          cnt_sb[:, bot * 4 + 1:bot * 4 + 2])
                            bubble()
                            bubble()
                            VE.tensor_add(sc3[:, bot:bot + 1],
                                          sc3[:, bot:bot + 1],
                                          cnt_sb[:, bot * 4 + 2:bot * 4 + 3])
                            bubble()
                            bubble()
                            VE.tensor_add(sc3[:, bot:bot + 1],
                                          sc3[:, bot:bot + 1],
                                          cnt_sb[:, bot * 4 + 3:bot * 4 + 4])
                            v_mrg = mark(VE.engine_nop(), "ve")
                            wait(PO, "ve", v_mid)
                            for ch8 in range(16):
                                r = ch8 % 2
                                if ("pe", r) in pmk:
                                    wait(PO, "pe", pmk[("pe", r)])
                                PO.tensor_scalar(
                                    mpo[:, r, :],
                                    wa[:, ch8 * 2048:(ch8 + 1) * 2048],
                                    mid_sb[:, top:top + 1], None, ALU.is_le)
                                p_mk = mark(PO.engine_nop(), "po")
                                wait(PE, "po", p_mk)
                                for q in range(4):
                                    PE.matmul(pb[6][:, 0:512], ones_sb[:],
                                              mpo[:, r, q * 512:(q + 1) * 512],
                                              start=(ch8 == 0 and q == 0),
                                              stop=(ch8 == 15 and q == 3))
                                pmk[("pe", r)] = pe_mark()
                            wait(PE, "ve", v_mrg)
                            PE.matmul(pb[7][:, bot:bot + 1], ones_sb[:],
                                      sc3[:, bot:bot + 1],
                                      start=True, stop=True)
                            t_tot = pe_mark()
                            wait(VE, "pe", t_tot)
                            VE.tensor_copy(tot_sb[:, bot:bot + 1],
                                           pb[7][:, bot:bot + 1])
                            VE.reduce_sum(tot_sb[:, top:top + 1],
                                          pb[6][:, 0:512], axis=AX.X)
                            bubble()
                            bubble()
                            if it == 0 and wi == 0 and "p1f" in dbg:
                                VE.tensor_copy(p1f_sb[:, 0:8], tot_sb[:])
                                VE.tensor_copy(p1f_sb[:, 8:16], mid_sb[:])
                                VE.tensor_copy(p1f_sb[:, 16:528],
                                               pb[6][:, 0:512])
                            upd_chain2(c0, c0 + 2)
                            v_mid = mark(VE.engine_nop(), "ve")
                        VE.tensor_copy(thr_sb[:, c0:c0 + 2], hi_sb[:, c0:c0 + 2])
                        v_thr = mark(VE.engine_nop(), "ve")
                        wait(SY, "ve", v_thr)
                        barrier()

                t_p1f = dout("p1f", [128, 528])
                if t_p1f is not None:
                    mark(SY.dma_start(t_p1f, p1f_sb[:]), "st", 16)
                    SY.wait_ge(sems["st"][0], sems["st"][1])

            t_thr = dout("thr", [128, 8])
            if t_thr is not None:
                mark(SY.dma_start(t_thr, thr_sb[:]), "st", 16)
                SY.wait_ge(sems["st"][0], sems["st"][1])

            if "P2" in phset:
              with ExitStack() as p2ctx:
                e2 = p2ctx.enter_context
                SW = 4096
                slabs = [e2(nc.sbuf_tensor(f"pslab{i}", [128, SW], F32))
                         for i in range(2)]
                absb = [e2(nc.sbuf_tensor(f"pabs{i}", [128, SW], F32))
                        for i in range(2)]
                i1b = [e2(nc.sbuf_tensor(f"pi1{i}", [128, SW], F32))
                       for i in range(2)]
                obf = [e2(nc.sbuf_tensor(f"pobf{i}", [128, SW], BF16))
                       for i in range(2)]
                pbf = [e2(nc.sbuf_tensor(f"ppbf{i}", [128, SW], BF16))
                       for i in range(2)]

                def build_p(wsrc, dst_clean, dst_pert, nelem, bot_col, top_col):
                    nsl = nelem // (128 * SW)
                    srcf = wsrc.rearrange("a f -> (a f)").rearrange(
                        "(p q) -> p q", p=128)
                    dstcf = dst_clean.rearrange("a f -> (a f)").rearrange(
                        "(p q) -> p q", p=128)
                    dstpf = dst_pert.rearrange("a f -> (a f)").rearrange(
                        "(p q) -> p q", p=128)
                    rel_po = {}
                    rel_sy = {}
                    for sl in range(nsl):
                        r = sl % 2
                        if sl >= 2:
                            wait(SY, "po", rel_po[sl - 2])
                            dwait(SY, "d2a" if r == 0 else "d2b",
                                  rel_sy[sl - 2])
                        d_l = mark(SY.dma_start(
                            slabs[r][:], srcf[:, sl * SW:(sl + 1) * SW]),
                            "d2", 16)
                        dwait(SY, "d2", d_l)
                        dwait(AC, "d2", d_l)
                        AC.activation(obf[r][:], slabs[r][:], AF.Identity)
                        a_p = ac_mark()
                        dwait(VE, "d2", d_l)
                        VE.scalar_tensor_tensor(absb[r][:], slabs[r][:], -1.0,
                                                slabs[r][:], ALU.mult, ALU.max)
                        bubble()
                        VE.tensor_scalar(i1b[r][:], absb[r][:],
                                         thr_sb[:, bot_col:bot_col + 1], None,
                                         ALU.is_le)
                        bubble()
                        VE.scalar_tensor_tensor(absb[r][:], absb[r][:],
                                                thr_sb[:, top_col:top_col + 1],
                                                i1b[r][:], ALU.is_ge, ALU.subtract)
                        bubble()
                        VE.tensor_scalar(i1b[r][:], absb[r][:], -0.8, 1.0,
                                         ALU.mult, ALU.add)
                        v_m = mark(VE.engine_nop(), "ve")
                        wait(PO, "ve", v_m)
                        PO.tensor_mul(pbf[r][:], slabs[r][:], i1b[r][:])
                        p_m = mark(PO.engine_nop(), "po")
                        rel_po[sl] = p_m
                        wait(SY, "ac", a_p)
                        wait(SY, "po", p_m)
                        sm = "d2a" if r == 0 else "d2b"
                        mark(SY.dma_start(dstcf[:, sl * SW:(sl + 1) * SW],
                                          obf[r][:]), sm, 16)
                        rel_sy[sl] = mark(SY.dma_start(
                            dstpf[:, sl * SW:(sl + 1) * SW], pbf[r][:]),
                            sm, 16)

                build_p(w1, w1bf_d, w1pbf_d, D * FF, 0, 1)
                build_p(w2, w2bf_d, w2pbf_d, FF * D, 2, 3)

                for (asb, vsb, psb, w, bc_, tc_) in (
                        (b1a_sb, b1_sb, b1p_sb, 32, 4, 5),
                        (b2a_sb, b2_sb, b2p_sb, KC, 6, 7)):
                    VE.tensor_scalar(i1b[0][:, 0:w], asb[:, 0:w],
                                     thr_sb[:, bc_:bc_ + 1], None, ALU.is_le)
                    bubble()
                    VE.scalar_tensor_tensor(absb[0][:, 0:w], asb[:, 0:w],
                                            thr_sb[:, tc_:tc_ + 1],
                                            i1b[0][:, 0:w], ALU.is_ge,
                                            ALU.subtract)
                    bubble()
                    VE.tensor_scalar(absb[0][:, 0:w], absb[0][:, 0:w],
                                     -0.8, 1.0, ALU.mult, ALU.add)
                    bubble()
                    VE.tensor_mul(psb[:, 0:w], vsb[:, 0:w], absb[0][:, 0:w])
                v_bp = mark(VE.engine_nop(), "ve")
                barrier()

            t_b1p = dout("b1p", [128, 32])
            if t_b1p is not None:
                mark(SY.dma_start(t_b1p, b1p_sb[:]), "st", 16)
                SY.wait_ge(sems["st"][0], sems["st"][1])

            if "P3" in phset:
              with ExitStack() as m:
                em = m.enter_context
                xstage = [em(nc.sbuf_tensor("xstage0", [128, D], F32))] * 2
                x_fm = em(nc.sbuf_tensor("x_fm", [128, KC, TC], F32))
                x_bf = em(nc.sbuf_tensor("x_bf", [128, KC, TC], BF16))
                x_ssm = em(nc.sbuf_tensor("x_ssm", [128, KC, TC + 3], F32))
                gate_sb = em(nc.sbuf_tensor("gate_sb", [128, KC, TC], F32))
                xconv = em(nc.sbuf_tensor("xconv", [128, KC, TC], F32))
                xconv_bf = em(nc.sbuf_tensor("xconv_bf", [128, KC, TC], BF16))
                dt_sb = em(nc.sbuf_tensor("dt_sb", [128, KC, TC], F32))
                y_all = em(nc.sbuf_tensor("y_all", [128, KC, TC], F32))
                rbuf = em(nc.sbuf_tensor("rbuf", [128, KC, TC], F32))
                vabuf = em(nc.sbuf_tensor("vabuf", [128, KC, TC], F32))
                zz_sb = rbuf
                w_all = x_ssm[:, :, 3:3 + TC]
                z_sb = x_ssm[:, :, 3:3 + TC]
                sq_sb = vabuf
                ygbf = x_bf
                vg = [em(nc.sbuf_tensor(f"vg{i}", [128, TC], F32))
                      for i in range(2)]
                convt = em(nc.sbuf_tensor("convt", [128, TC], F32))
                tg = [em(nc.sbuf_tensor(f"tg{i}", [128, TC], F32))
                      for i in range(2)]
                dA_b = [em(nc.sbuf_tensor("dA0", [128, KC, TC], F32))] * 2
                u_b = [em(nc.sbuf_tensor("u0", [128, KC, TC], F32))] * 2
                h_b = [em(nc.sbuf_tensor(f"h{i}", [128, KC, TC], F32))
                       for i in range(2)]
                t_b = [em(nc.sbuf_tensor("t0", [128, KC, TC], F32))] * 2
                Bb = [em(nc.sbuf_tensor(f"Bb{i}", [128, TC], F32))
                      for i in range(2)]
                Cb = [em(nc.sbuf_tensor(f"Cb{i}", [128, TC], F32))
                      for i in range(2)]
                hcarry = em(nc.sbuf_tensor("hcarry", [128, KC, NST], F32))
                BCs = em(nc.sbuf_tensor("BCs", [16, 512], F32))
                stat = em(nc.sbuf_tensor("stat", [1, 512], F32))
                t1b = [em(nc.sbuf_tensor(f"t1b{i}", [128, TC], F32))
                       for i in range(2)]
                statb = em(nc.sbuf_tensor("statb", [128, 512], F32))
                t2b = [em(nc.sbuf_tensor(f"t2b{i}", [128, TC], F32))
                       for i in range(2)]
                ipwT_r = em(nc.sbuf_tensor("ipwT_r", [128, KC, 2 * D], BF16))
                dtwT_r = em(nc.sbuf_tensor("dtwT_r", [128, KC, D], BF16))
                owT_r = em(nc.sbuf_tensor("owT_r", [128, KC, D], BF16))

                mark(SY.dma_start(ipwT_r[:],
                                  ipwT_d.rearrange("(c p) o -> p c o", p=128)),
                     "dw0", 16)
                mark(SY.dma_start(dtwT_r[:],
                                  dtwT_d.rearrange("(c p) o -> p c o", p=128)),
                     "dw0", 16)
                d_wres = mark(SY.dma_start(
                    owT_r[:], owT_d.rearrange("(c p) o -> p c o", p=128)),
                    "dw0", 16)
                dwait(PE, "dw0", d_wres)
                VE.memset(hcarry[:], 0.0)
                VE.memset(x_ssm[:, :, 0:3], 0.0)
                v_init = mark(VE.engine_nop(), "ve")
                wait(AC, "ve", v_init)
                wait(PO, "ve", v_init)
                wait(PE, "ve", v_init)

                v_carry = {}
                state = {"v_x2": None, "p_tail": None}

                def mamba_chunk(ch):
                    t0 = ch * TC
                    if state["v_x2"] is not None:
                        wait(PE, "ve", state["v_x2"])
                    v_es = {}
                    for tt in range(2):
                        wait(SY, "pe")
                        d_x = mark(SY.dma_start(
                            xstage[tt][:],
                            x_in[t0 + tt * 128:t0 + (tt + 1) * 128, :]),
                            "dx", 16)
                        dwait(PE, "dx", d_x)
                        for kc2 in range(KC):
                            j = tt * 8 + kc2
                            bank = 6 + (j % 2)
                            if j >= 2:
                                wait(PE, "ve", v_es[j - 2])
                            tr(pb[bank][:, 0:128],
                                         xstage[tt][:, kc2 * 128:(kc2 + 1) * 128])
                            t_tr = pe_mark()
                            wait(VE, "pe", t_tr)
                            VE.tensor_copy(x_fm[:, kc2, tt * 128:(tt + 1) * 128],
                                           pb[bank][:, 0:128])
                            v_es[j] = mark(VE.engine_nop(), "ve")
                    v_xfm = mark(VE.engine_nop(), "ve")
                    wait(SY, "ve", v_xfm)
                    wait(AC, "ve", v_xfm)
                    if state.get("t_op") is not None:
                        wait(AC, "pe", state["t_op"])
                    for kc2 in range(KC):
                        AC.activation(x_bf[:, kc2, :], x_fm[:, kc2, :], AF.Identity)
                    a_xbf = ac_mark()

                    wait(PE, "ac", a_xbf)
                    for ot in range(16):
                        bank, colo = ot % 8, (ot // 8) * TC
                        for kc2 in range(KC):
                            PE.matmul(pb[bank][:, colo:colo + TC],
                                      ipwT_r[:, kc2, ot * 128:(ot + 1) * 128],
                                      x_bf[:, kc2, :],
                                      start=(kc2 == 0), stop=(kc2 == KC - 1))
                    t_ip = pe_mark()
                    wait(AC, "pe", t_ip)
                    if state["p_tail"] is not None:
                        wait(AC, "po", state["p_tail"])
                    for ot in range(8):
                        bank, colo = ot % 8, (ot // 8) * TC
                        AC.activation(x_ssm[:, ot, 3:3 + TC],
                                      pb[bank][:, colo:colo + TC],
                                      AF.Identity, bias=ipb_sb[:, ot:ot + 1])
                    p_g = {}
                    for ot in range(8, 16):
                        bank, colo = ot % 8, (ot // 8) * TC
                        r = ot % 2
                        if ot >= 10:
                            wait(AC, "po", p_g[ot - 2])
                        AC.activation(vg[r][:], pb[bank][:, colo:colo + TC],
                                      AF.Identity, bias=ipb_sb[:, ot:ot + 1])
                        AC.activation(tg[r][:], vg[r][:], AF.Tanh,
                                      scale=half_sb[:])
                        a_g = ac_mark()
                        wait(PO, "ac", a_g)
                        PO.tensor_scalar(tg[r][:], tg[r][:], 1.0, 0.5,
                                         ALU.add, ALU.mult)
                        PO.tensor_mul(gate_sb[:, ot - 8, :], tg[r][:], vg[r][:])
                        p_g[ot] = mark(PO.engine_nop(), "po")
                    wait(AC, "po", p_g[15])
                    a_ssm = ac_mark()

                    wait(PO, "ac", a_ssm)
                    p_c, a_c, p_x = {}, {}, {}

                    def rec_conv_tail(k):
                        r2 = k % 2
                        wait(PO, "ac", a_c[k])
                        PO.tensor_scalar(tg[r2][:], tg[r2][:], 1.0, 0.5,
                                         ALU.add, ALU.mult)
                        PO.tensor_mul(xconv[:, k, :], tg[r2][:], vg[r2][:])
                        p_x[k] = mark(PO.engine_nop(), "po")
                        wait(AC, "po", p_x[k])
                        AC.activation(xconv_bf[:, k, :], xconv[:, k, :],
                                      AF.Identity)

                    for kc2 in range(KC):
                        r = kc2 % 2
                        if kc2 >= 2:
                            wait(PO, "ac", a_c[kc2 - 2])
                        PO.tensor_scalar(vg[r][:], x_ssm[:, kc2, 0:TC],
                                         cw_sb[:, kc2, 0:1], None, ALU.mult)
                        for kk in range(1, 4):
                            PO.tensor_scalar(convt[:], x_ssm[:, kc2, kk:kk + TC],
                                             cw_sb[:, kc2, kk:kk + 1], None,
                                             ALU.mult)
                            PO.tensor_add(vg[r][:], vg[r][:], convt[:])
                        p_c[kc2] = mark(PO.engine_nop(), "po")
                        wait(AC, "po", p_c[kc2])
                        AC.activation(vg[r][:], vg[r][:], AF.Identity,
                                      bias=cb_sb[:, kc2:kc2 + 1])
                        AC.activation(tg[r][:], vg[r][:], AF.Tanh,
                                      scale=half_sb[:])
                        a_c[kc2] = ac_mark()
                        if kc2 >= 1:
                            rec_conv_tail(kc2 - 1)
                    rec_conv_tail(KC - 1)
                    PO.tensor_copy(x_ssm[:, :, 0:3], x_ssm[:, :, TC:TC + 3])
                    state["p_tail"] = mark(PO.engine_nop(), "po")
                    a_xc = ac_mark()

                    wait(PE, "ac", a_xc)
                    for kc2 in range(KC):
                        PE.matmul(pb[0][0:16, 0:TC], bcwT_sb[:, kc2, 0:16],
                                  xconv_bf[:, kc2, :],
                                  start=(kc2 == 0), stop=(kc2 == KC - 1))
                    for kc2 in range(KC):
                        PE.matmul(pb[0][0:16, TC:2 * TC], bcwT_sb[:, kc2, 16:32],
                                  xconv_bf[:, kc2, :],
                                  start=(kc2 == 0), stop=(kc2 == KC - 1))
                    t_bc = pe_mark()
                    wait(AC, "pe", t_bc)
                    AC.activation(BCs[:, 0:TC], pb[0][0:16, 0:TC], AF.Identity,
                                  bias=bcbB_sb[:])
                    AC.activation(BCs[:, TC:2 * TC], pb[0][0:16, TC:2 * TC],
                                  AF.Identity, bias=bcbC_sb[:])
                    a_bc = ac_mark()
                    wait(SY, "ac", a_bc)
                    mark(SY.dma_start(bc_d[0:16, t0:t0 + TC], BCs[:, 0:TC]),
                         "dbs", 16)
                    d_bcs = mark(SY.dma_start(bc_d[16:32, t0:t0 + TC],
                                              BCs[:, TC:2 * TC]), "dbs", 16)

                    for ot in range(8):
                        bank, colo = 1 + ot % 4, (ot // 4) * TC
                        for kc2 in range(KC):
                            PE.matmul(pb[bank][:, colo:colo + TC],
                                      dtwT_r[:, kc2, ot * 128:(ot + 1) * 128],
                                      xconv_bf[:, kc2, :],
                                      start=(kc2 == 0), stop=(kc2 == KC - 1))
                    t_dt = pe_mark()
                    wait(AC, "pe", t_dt)
                    for ot in range(8):
                        bank, colo = 1 + ot % 4, (ot // 4) * TC
                        AC.activation(vabuf[:, ot, :],
                                      pb[bank][:, colo:colo + TC],
                                      AF.Identity, bias=dtb_sb[:, ot:ot + 1])
                    a_dtv = ac_mark()
                    AC.activation(rbuf[:].rearrange("p c t -> p (c t)"),
                                  vabuf[:].rearrange("p c t -> p (c t)"),
                                  AF.Relu)
                    a_dt1 = ac_mark()
                    wait(VE, "ac", a_dt1)
                    VE.scalar_tensor_tensor(vabuf[:], vabuf[:], -1.0, vabuf[:],
                                            ALU.mult, ALU.max)
                    v_va = mark(VE.engine_nop(), "ve")
                    wait(AC, "ve", v_va)
                    AC.activation(vabuf[:].rearrange("p c t -> p (c t)"),
                                  vabuf[:].rearrange("p c t -> p (c t)"),
                                  AF.Exp, scale=neg1_sb[:])
                    AC.activation(vabuf[:].rearrange("p c t -> p (c t)"),
                                  vabuf[:].rearrange("p c t -> p (c t)"),
                                  AF.Ln, bias=one_sb[:])
                    a_dt2 = ac_mark()
                    wait(VE, "ac", a_dt2)
                    VE.tensor_add(dt_sb[:], rbuf[:], vabuf[:])
                    v_dt = mark(VE.engine_nop(), "ve")

                    wait(PO, "ve", v_dt)
                    PO.tensor_mul(w_all, dt_sb[:], xconv[:])
                    PO.tensor_tensor(
                        y_all[:], xconv[:],
                        dvec_sb[:].unsqueeze(2).broadcast_to((128, KC, TC)),
                        ALU.mult)
                    p_w = mark(PO.engine_nop(), "po")

                    dAp = [dA_b[0], rbuf]
                    up = [u_b[0], vabuf]
                    tp = [t_b[0], xconv]
                    dwait(SY, "dbs", d_bcs)
                    p_u, v_scan, p_ym, a_dA, v_ym = {}, {}, {}, {}, {}

                    def rec_yadd(n):
                        wait(PO, "ve", v_ym[n])
                        PO.tensor_add(y_all[:], y_all[:], tp[n % 2][:])
                        p_ym[n] = mark(PO.engine_nop(), "po")

                    for n in range(NST):
                        r = n % 2
                        sm = "dbc0" if r == 0 else "dbc1"
                        if n >= 2:
                            wait(SY, "po", p_u[n - 2])
                            wait(SY, "ve", v_ym[n - 2])
                        d_b = mark(SY.dma_start(
                            Bb[r][:], bc_d[n:n + 1, t0:t0 + TC]
                            .partition_broadcast(128).squeeze(1)), sm, 16)
                        d_c = mark(SY.dma_start(
                            Cb[r][:], bc_d[16 + n:17 + n, t0:t0 + TC]
                            .partition_broadcast(128).squeeze(1)), sm, 16)
                        if n <= 1:
                            wait(AC, "ve", v_dt)
                        if n >= 2:
                            wait(AC, "ve", v_scan[n - 2])
                        for c in range(KC):
                            AC.activation(dAp[r][:, c, :], dt_sb[:, c, :],
                                          AF.Exp, scale=A_sb[:, c, n:n + 1])
                        a_dA[n] = ac_mark()
                        if n == 1:
                            wait(PO, "ve", v_dt)
                        dwait(PO, sm, d_c)
                        if n >= 2:
                            wait(PO, "ve", v_scan[n - 2])
                        PO.tensor_tensor(
                            up[r][:], w_all,
                            Bb[r][:].unsqueeze(1).broadcast_to((128, KC, TC)),
                            ALU.mult)
                        p_u[n] = mark(PO.engine_nop(), "po")
                        if n >= 1:
                            rec_yadd(n - 1)
                        wait(VE, "ac", a_dA[n])
                        wait(VE, "po", p_u[n])
                        for c in range(KC):
                            VE.tensor_tensor_scan(h_b[r][:, c, :],
                                                  dAp[r][:, c, :],
                                                  up[r][:, c, :],
                                                  hcarry[:, c, n:n + 1],
                                                  ALU.mult, ALU.add)
                        v_scan[n] = mark(VE.engine_nop(), "ve")
                        if n == 1:
                            wait(VE, "po", p_w)
                        if n >= 2:
                            wait(VE, "po", p_ym[n - 2])
                        dwait(VE, sm, d_c)
                        bubble()
                        VE.tensor_tensor(
                            tp[r][:], h_b[r][:],
                            Cb[r][:].unsqueeze(1).broadcast_to((128, KC, TC)),
                            ALU.mult)
                        v_ym[n] = mark(VE.engine_nop(), "ve")
                        VE.tensor_copy(hcarry[:, :, n:n + 1],
                                       h_b[r][:, :, TC - 1:TC])
                        v_carry[n] = mark(VE.engine_nop(), "ve")
                    rec_yadd(NST - 1)

                    wait(VE, "po", p_ym[NST - 1])
                    bubble()
                    VE.tensor_mul(ygbf[:], y_all[:], gate_sb[:])
                    v_yg = mark(VE.engine_nop(), "ve")

                    wait(PE, "ve", v_yg)
                    wait(PE, "ac", a_dt2)
                    for ot in range(8):
                        bank, colo = 1 + ot % 4, (ot // 4) * TC
                        for kc2 in range(KC):
                            PE.matmul(pb[bank][:, colo:colo + TC],
                                      owT_r[:, kc2, ot * 128:(ot + 1) * 128],
                                      ygbf[:, kc2, :],
                                      start=(kc2 == 0), stop=(kc2 == KC - 1))
                    t_op = pe_mark()
                    state["t_op"] = t_op
                    wait(VE, "pe", t_op)
                    for ot in range(8):
                        bank, colo = 1 + ot % 4, (ot // 4) * TC
                        VE.scalar_tensor_tensor(z_sb[:, ot, :],
                                                pb[bank][:, colo:colo + TC],
                                                ob_sb[:, ot:ot + 1],
                                                x_fm[:, ot, :],
                                                ALU.add, ALU.add)
                    v_z = mark(VE.engine_nop(), "ve")

                    wait(AC, "ve", v_z)
                    AC.activation(sq_sb[:], z_sb[:], AF.Square)
                    a_sq = ac_mark()
                    wait(PE, "ac", a_sq)
                    for kc2 in range(KC):
                        PE.matmul(pb[5][0:2, 0:TC], ones_sb[:, 0:2],
                                  z_sb[:, kc2, :],
                                  start=(kc2 == 0), stop=(kc2 == KC - 1))
                    for kc2 in range(KC):
                        PE.matmul(pb[5][0:2, TC:2 * TC], ones_sb[:, 0:2],
                                  sq_sb[:, kc2, :],
                                  start=(kc2 == 0), stop=(kc2 == KC - 1))
                    t_s1 = pe_mark()

                    def stats(t_sums):
                        wait(VE, "pe", t_sums)
                        VE.tensor_scalar(stat[0:1, 0:TC], pb[5][0:1, 0:TC],
                                         INV_D, None, ALU.mult)
                        VE.tensor_scalar(stat[0:1, TC:2 * TC],
                                         pb[5][0:1, TC:2 * TC],
                                         INV_D, None, ALU.mult)
                        VE.tensor_mul(t1b[0][0:1, 0:TC], stat[0:1, 0:TC],
                                      stat[0:1, 0:TC])
                        bubble()
                        VE.tensor_sub(stat[0:1, TC:2 * TC],
                                      stat[0:1, TC:2 * TC], t1b[0][0:1, 0:TC])
                        v_var = mark(VE.engine_nop(), "ve")
                        wait(AC, "ve", v_var)
                        AC.activation(stat[0:1, TC:2 * TC],
                                      stat[0:1, TC:2 * TC], AF.Ln,
                                      bias=eps_sb[0:1, :])
                        AC.activation(stat[0:1, TC:2 * TC],
                                      stat[0:1, TC:2 * TC], AF.Exp,
                                      scale=negh_sb[0:1, :])
                        a_st = ac_mark()
                        wait(SY, "ac", a_st)
                        wait(SY, "ve")
                        d_w = mark(SY.dma_start(stat_d[0:1, 0:512], stat[:]),
                                   "dst", 16)
                        dwait(SY, "dst", d_w)
                        d_r = mark(SY.dma_start(
                            statb[:], stat_d[0:1, 0:512]
                            .partition_broadcast(128).squeeze(1)), "dst", 16)
                        dwait(VE, "dst", d_r)
                        return d_r

                    stats(t_s1)
                    for kc2 in range(0, KC, 2):
                        for q in range(2):
                            VE.tensor_sub(t1b[q][:], z_sb[:, kc2 + q, :],
                                          statb[:, 0:TC])
                        for q in range(2):
                            VE.tensor_mul(t2b[q][:], t1b[q][:],
                                          statb[:, TC:2 * TC])
                        v_t2 = mark(VE.engine_nop(), "ve")
                        wait(PO, "ve", v_t2)
                        for q in range(2):
                            PO.tensor_scalar(t2b[q][:], t2b[q][:],
                                             lng_sb[:, kc2 + q:kc2 + q + 1],
                                             None, ALU.mult)
                            PO.tensor_add(t2b[q][:], t2b[q][:],
                                          x_fm[:, kc2 + q, :])
                            PO.tensor_scalar(zz_sb[:, kc2 + q, :], t2b[q][:],
                                             lnb_sb[:, kc2 + q:kc2 + q + 1],
                                             None, ALU.add)
                        p_zz = mark(PO.engine_nop(), "po")
                        wait(VE, "po", p_zz)
                    p_zz = sems["po"][1]

                    wait(AC, "po", p_zz)
                    AC.activation(sq_sb[:], zz_sb[:], AF.Square)
                    a_sq2 = ac_mark()
                    wait(PE, "ac", a_sq2)
                    for kc2 in range(KC):
                        PE.matmul(pb[5][0:2, 0:TC], ones_sb[:, 0:2],
                                  zz_sb[:, kc2, :],
                                  start=(kc2 == 0), stop=(kc2 == KC - 1))
                    for kc2 in range(KC):
                        PE.matmul(pb[5][0:2, TC:2 * TC], ones_sb[:, 0:2],
                                  sq_sb[:, kc2, :],
                                  start=(kc2 == 0), stop=(kc2 == KC - 1))
                    t_s2 = pe_mark()
                    stats(t_s2)
                    for kc2 in range(0, KC, 2):
                        wait(VE, "dx2")
                        for q in range(2):
                            VE.tensor_sub(t1b[q][:], zz_sb[:, kc2 + q, :],
                                          statb[:, 0:TC])
                        for q in range(2):
                            VE.tensor_mul(t2b[q][:], t1b[q][:],
                                          statb[:, TC:2 * TC])
                        v_x2c = mark(VE.engine_nop(), "ve")
                        wait(SY, "ve", v_x2c)
                        for q in range(2):
                            mark(SY.dma_start(
                                x2_d[(kc2 + q) * 128:(kc2 + q + 1) * 128,
                                     t0:t0 + TC], t2b[q][:]), "dx2", 16)
                        wait(SY, "dx2")
                    state["v_x2"] = mark(VE.engine_nop(), "ve")

                for ch in range(NCH):
                    mamba_chunk(ch)
                barrier()

            t_x2 = dout("x2", [D, L])
            if t_x2 is not None:
                mark(SY.dma_start(t_x2, x2_d), "st", 16)
                SY.wait_ge(sems["st"][0], sems["st"][1])

            if "P4" in phset:
              with ExitStack() as f:
                ef = f.enter_context
                y_acc = ef(nc.sbuf_tensor("y_acc", [128, KC, L], F32))
                sqf = ef(nc.sbuf_tensor("sqf", [128, 2, 512], F32))
                statf = ef(nc.sbuf_tensor("statf", [1, 2 * L], F32))
                murd_tok = ef(nc.sbuf_tensor("murd_tok", [128, 16], F32))
                rstd_tok = ef(nc.sbuf_tensor("rstd_tok", [128, 16], F32))
                lg_sb = ef(nc.sbuf_tensor("lg_sb", [128, 8], F32))
                lg2_sb = ef(nc.sbuf_tensor("lg2_sb", [128, 8], F32))
                mx_sb = ef(nc.sbuf_tensor("mx_sb", [128, 1], F32))
                ex_sb = ef(nc.sbuf_tensor("ex_sb", [128, 8], F32))
                sm_sb = ef(nc.sbuf_tensor("sm_sb", [128, 1], F32))
                rs_sb = ef(nc.sbuf_tensor("rs_sb", [128, 1], F32))
                p_colt = ef(nc.sbuf_tensor("p_colt", [128, 16], F32))

                wait(SY, "dx2")
                d_x2 = mark(SY.dma_start(
                    y_acc[:], x2_d.rearrange("(c p) t -> p c t", p=128)),
                    "d4a", 16)
                dwait(AC, "d4a", d_x2)
                dwait(VE, "d4a", d_x2)
                dwait(PE, "d4a", d_x2)
                a_x2ld = ac_mark()
                wait(VE, "ac", a_x2ld)
                wait(PE, "ac", a_x2ld)

                for tb in range(4):
                    ts0 = tb * 512
                    sqmk = {}
                    for kc2 in range(KC):
                        r = kc2 % 2
                        if kc2 >= 2:
                            wait(AC, "pe", sqmk[kc2 - 2])
                        AC.activation(sqf[:, r, :], y_acc[:, kc2, ts0:ts0 + 512],
                                      AF.Square)
                        a_q = ac_mark()
                        wait(PE, "ac", a_q)
                        PE.matmul(pb[0][0:2, 0:512], ones_sb[:, 0:2],
                                  y_acc[:, kc2, ts0:ts0 + 512],
                                  start=(kc2 == 0), stop=(kc2 == KC - 1))
                        PE.matmul(pb[1][0:2, 0:512], ones_sb[:, 0:2],
                                  sqf[:, r, :],
                                  start=(kc2 == 0), stop=(kc2 == KC - 1))
                        sqmk[kc2] = pe_mark()
                    t_fs = sqmk[KC - 1]
                    wait(VE, "pe", t_fs)
                    VE.tensor_scalar(statf[0:1, ts0:ts0 + 512],
                                     pb[0][0:1, 0:512], INV_D, None, ALU.mult)
                    VE.tensor_scalar(statf[0:1, L + ts0:L + ts0 + 512],
                                     pb[1][0:1, 0:512], INV_D, None, ALU.mult)
                    v_fs = mark(VE.engine_nop(), "ve")
                    wait(PE, "ve", v_fs)
                stf = ef(nc.sbuf_tensor("stf", [1, L], F32))
                VE.tensor_mul(stf[0:1, :], statf[0:1, 0:L], statf[0:1, 0:L])
                bubble()
                VE.tensor_sub(statf[0:1, L:2 * L], statf[0:1, L:2 * L],
                              stf[0:1, :])
                v_vf = mark(VE.engine_nop(), "ve")
                wait(AC, "ve", v_vf)
                AC.activation(statf[0:1, L:2 * L], statf[0:1, L:2 * L], AF.Ln,
                              bias=eps_sb[0:1, :])
                AC.activation(statf[0:1, L:2 * L], statf[0:1, L:2 * L], AF.Exp,
                              scale=negh_sb[0:1, :])
                a_rf = ac_mark()
                wait(VE, "ac", a_rf)
                VE.tensor_mul(statf[0:1, 0:L], statf[0:1, 0:L],
                              statf[0:1, L:2 * L])
                bubble()
                VE.tensor_scalar(statf[0:1, 0:L], statf[0:1, 0:L], -1.0,
                                 None, ALU.mult)
                v_murd = mark(VE.engine_nop(), "ve")
                wait(PE, "ve", v_murd)
                for tt in range(16):
                    bank = 2 + (tt % 2)
                    tr(pb[bank][0:128, 0:1],
                                 statf[0:1, tt * 128:(tt + 1) * 128])
                    tr(pb[bank][0:128, 1:2],
                                 statf[0:1, L + tt * 128:L + (tt + 1) * 128])
                    t_tt = pe_mark()
                    wait(VE, "pe", t_tt)
                    VE.tensor_copy(murd_tok[:, tt:tt + 1], pb[bank][:, 0:1])
                    VE.tensor_copy(rstd_tok[:, tt:tt + 1], pb[bank][:, 1:2])
                    v_tt = mark(VE.engine_nop(), "ve")
                    wait(PE, "ve", v_tt)
                if "lgt" in dbg:
                    lgt_sb = ef(nc.sbuf_tensor("lgt_sb", [128, 128], F32))
                xrt = [ef(nc.sbuf_tensor(f"xrt{i}", [128, KC, 128], BF16))
                       for i in range(2)]
                for tt in range(16):
                    rr = tt % 2
                    if tt >= 2:
                        wait(AC, "pe")
                    for kc2 in range(KC):
                        AC.activation(xrt[rr][:, kc2, :],
                                      y_acc[:, kc2, tt * 128:(tt + 1) * 128],
                                      AF.Identity)
                    a_xr = ac_mark()
                    wait(PE, "ac", a_xr)
                    for kc2 in range(KC):
                        PE.matmul(pb[4][:, 0:8],
                                  xrt[rr][:, kc2, :],
                                  wrg_sb[:, kc2, :],
                                  start=(kc2 == 0), stop=(kc2 == KC - 1))
                    t_lg = pe_mark()
                    wait(VE, "pe", t_lg)
                    VE.tensor_copy(lg_sb[:], pb[4][:, 0:8])
                    bubble()
                    VE.tensor_scalar(lg_sb[:], lg_sb[:], rstd_tok[:, tt:tt + 1],
                                     None, ALU.mult)
                    bubble()
                    bubble()
                    VE.scalar_tensor_tensor(lg2_sb[:], ceb_sb[:],
                                            murd_tok[:, tt:tt + 1], lg_sb[:],
                                            ALU.mult, ALU.add)
                    bubble()
                    VE.tensor_add(lg2_sb[:], lg2_sb[:], bewb_sb[:])
                    bubble()
                    VE.reduce_max(mx_sb[:], lg2_sb[:], axis=AX.X, negate=True)
                    v_lg = mark(VE.engine_nop(), "ve")
                    wait(AC, "ve", v_lg)
                    AC.activation(ex_sb[:], lg2_sb[:], AF.Exp, bias=mx_sb[:])
                    a_ex = ac_mark()
                    wait(VE, "ac", a_ex)
                    VE.reduce_sum(sm_sb[:], ex_sb[:], axis=AX.X)
                    bubble()
                    VE.reciprocal(rs_sb[:], sm_sb[:])
                    bubble()
                    VE.tensor_mul(p_colt[:, tt:tt + 1], ex_sb[:, 0:1], rs_sb[:])
                    if "lgt" in dbg:
                        VE.tensor_copy(lgt_sb[:, tt * 8:tt * 8 + 8], lg2_sb[:])
                    v_p = mark(VE.engine_nop(), "ve")
                    wait(PE, "ve", v_p)
                prow = ef(nc.sbuf_tensor("prow", [1, L], F32))
                p0b = ef(nc.sbuf_tensor("p0b", [128, L], F32))
                p1b = ef(nc.sbuf_tensor("p1b", [128, L], F32))
                for tt in range(16):
                    bank = 2 + (tt % 2)
                    tr(pb[bank][0:1, 0:128], p_colt[:, tt:tt + 1])
                    t_pt = pe_mark()
                    wait(VE, "pe", t_pt)
                    VE.tensor_copy(prow[0:1, tt * 128:(tt + 1) * 128],
                                   pb[bank][0:1, 0:128])
                    v_pr = mark(VE.engine_nop(), "ve")
                    wait(PE, "ve", v_pr)
                v_prow = mark(VE.engine_nop(), "ve")
                wait(SY, "ve", v_prow)
                d_pw = mark(SY.dma_start(prow_d, prow[:]), "dst", 16)
                dwait(SY, "dst", d_pw)
                d_pr = mark(SY.dma_start(
                    p0b[:], prow_d.partition_broadcast(128).squeeze(1)),
                    "dst", 16)
                dwait(VE, "dst", d_pr)
                VE.tensor_scalar(p1b[:], p0b[:], -1.0, 1.0, ALU.mult, ALU.add)
                v_p1 = mark(VE.engine_nop(), "ve")

                fex = ExitStack()
                efx = fex.enter_context
                act_sb = efx(nc.sbuf_tensor("act_sb", [128, 32, FT], BF16))
                x2bft = efx(nc.sbuf_tensor("x2bft", [128, KC, FT], BF16))
                w1sl = [efx(nc.sbuf_tensor(f"w1sl{i}", [128, KC, 512], BF16))
                        for i in range(2)]
                w2sl = [efx(nc.sbuf_tensor(f"w2sl{i}", [128, 4, D], BF16))
                        for i in range(2)]
                yt_b = [efx(nc.sbuf_tensor(f"ytb{i}", [128, FT], F32))
                        for i in range(2)]
                for tb in range(NFT):
                    ts0 = tb * FT
                    wait(AC, "pe")
                    for kc2 in range(KC):
                        AC.activation(x2bft[:, kc2, :],
                                      y_acc[:, kc2, ts0:ts0 + FT], AF.Identity)
                    a_xt = ac_mark()
                    wait(PE, "ac", a_xt)
                    for ex in range(2):
                        w1src = w1bf_d if ex == 0 else w1pbf_d
                        w2src = w2bf_d if ex == 0 else w2pbf_d
                        b1src = b1_sb if ex == 0 else b1p_sb
                        b2src = b2_sb if ex == 0 else b2p_sb
                        psrc = p0b if ex == 0 else p1b
                        last_gelu = {0: 0, 1: 0}
                        fcpre = {}
                        for fc in range(8):
                            r = fc % 2
                            sm = "d4a" if r == 0 else "d4b"
                            if fc >= 2:
                                wait(SY, "pe", fcpre[fc - 2])
                            d_w1 = mark(SY.dma_start(
                                w1sl[r][:],
                                w1src[:, fc * 512:(fc + 1) * 512].rearrange(
                                    "(c p) f -> p c f", p=128)), sm, 16)
                            dwait(PE, sm, d_w1)
                            for ftl in range(4):
                                bank = ftl % 2
                                wait(PE, "ac", last_gelu[bank])
                                for kc2 in range(KC):
                                    PE.matmul(
                                        pb[bank][:, 0:FT],
                                        w1sl[r][:, kc2, ftl * 128:(ftl + 1) * 128],
                                        x2bft[:, kc2, :],
                                        start=(kc2 == 0), stop=(kc2 == KC - 1))
                                t_pre = pe_mark()
                                fcpre[fc] = t_pre
                                wait(AC, "pe", t_pre)
                                fcol = fc * 4 + ftl
                                AC.activation(act_sb[:, fcol, :],
                                              pb[bank][:, 0:FT],
                                              AF.Gelu_apprx_tanh,
                                              bias=b1src[:, fcol:fcol + 1])
                                last_gelu[bank] = ac_mark()
                        wait(PE, "ac", max(last_gelu.values()))
                        fcy = {}
                        for fc in range(8):
                            r = fc % 2
                            sm = "d4a" if r == 0 else "d4b"
                            if fc >= 2:
                                wait(SY, "pe", fcy[fc - 2])
                            d_w2 = mark(SY.dma_start(
                                w2sl[r][:],
                                w2src[fc * 512:(fc + 1) * 512, :].rearrange(
                                    "(q p) h -> p q h", p=128)), sm, 16)
                            dwait(PE, sm, d_w2)
                            for ht in range(8):
                                for fi in range(4):
                                    PE.matmul(
                                        pb[ht][:, 0:FT],
                                        w2sl[r][:, fi, ht * 128:(ht + 1) * 128],
                                        act_sb[:, fc * 4 + fi, :],
                                        start=(fc == 0 and fi == 0),
                                        stop=(fc == 7 and fi == 3))
                            fcy[fc] = pe_mark()
                        t_y = fcy[7]
                        wait(VE, "pe", t_y)
                        for ht in range(0, 8, 2):
                            for q in range(2):
                                VE.scalar_tensor_tensor(
                                    yt_b[q][:], pb[ht + q][:, 0:FT],
                                    b2src[:, ht + q:ht + q + 1],
                                    psrc[:, ts0:ts0 + FT],
                                    ALU.add, ALU.mult)
                            for q in range(2):
                                VE.tensor_add(
                                    y_acc[:, ht + q, ts0:ts0 + FT],
                                    y_acc[:, ht + q, ts0:ts0 + FT],
                                    yt_b[q][:])
                        v_ev = mark(VE.engine_nop(), "ve")
                        wait(PE, "ve", v_ev)

                fex.close()
                t_lgt = dout("lgt", [128, 128])
                if t_lgt is not None:
                    mark(SY.dma_start(t_lgt, lgt_sb[:]), "dst", 16)
                t_pr = dout("prow", [1, L])
                if t_pr is not None:
                    mark(SY.dma_start(t_pr, prow[:]), "dst", 16)
                t_ya = dout("yacc", [D, L])
                if t_ya is not None:
                    quiesce(VE)
                    v_yd = mark(VE.engine_nop(), "ve")
                    wait(SY, "ve", v_yd)
                    mark(SY.dma_start(
                        t_ya.rearrange("(c p) t -> p c t", p=128), y_acc[:]),
                        "dst", 16)
                    dwait(SY, "dst", sems["dst"][1])
                sq5 = sqf
                stat5 = statf
                ost = [ef(nc.sbuf_tensor(f"ost{i}", [128, D], F32))
                       for i in range(2)]
                stat5b = ef(nc.sbuf_tensor("stat5b", [128, 1024], F32))
                t15 = [ef(nc.sbuf_tensor(f"t15{i}", [128, 512], F32))
                       for i in range(2)]
                x3 = ef(nc.sbuf_tensor("x3", [128, KC, 512], F32))
                st_marks = {}
                for tb in range(4):
                    ts0 = tb * 512
                    sqmk = {}
                    for kc2 in range(KC):
                        r = kc2 % 2
                        if kc2 >= 2:
                            wait(AC, "pe", sqmk[kc2 - 2])
                        AC.activation(sq5[:, r, :], y_acc[:, kc2, ts0:ts0 + 512],
                                      AF.Square)
                        a_q = ac_mark()
                        wait(PE, "ac", a_q)
                        PE.matmul(pb[0][0:2, 0:512], ones_sb[:, 0:2],
                                  y_acc[:, kc2, ts0:ts0 + 512],
                                  start=(kc2 == 0), stop=(kc2 == KC - 1))
                        PE.matmul(pb[1][0:2, 0:512], ones_sb[:, 0:2],
                                  sq5[:, r, :],
                                  start=(kc2 == 0), stop=(kc2 == KC - 1))
                        sqmk[kc2] = pe_mark()
                    t_s5 = sqmk[KC - 1]
                    wait(VE, "pe", t_s5)
                    VE.tensor_scalar(stat5[0:1, 0:512], pb[0][0:1, 0:512],
                                     INV_D, None, ALU.mult)
                    VE.tensor_scalar(stat5[0:1, 512:1024], pb[1][0:1, 0:512],
                                     INV_D, None, ALU.mult)
                    VE.tensor_mul(sq5[0:1, 0, :], stat5[0:1, 0:512],
                                  stat5[0:1, 0:512])
                    bubble()
                    VE.tensor_sub(stat5[0:1, 512:1024], stat5[0:1, 512:1024],
                                  sq5[0:1, 0, :])
                    v_v5 = mark(VE.engine_nop(), "ve")
                    wait(AC, "ve", v_v5)
                    AC.activation(stat5[0:1, 512:1024], stat5[0:1, 512:1024],
                                  AF.Ln, bias=eps_sb[0:1, :])
                    AC.activation(stat5[0:1, 512:1024], stat5[0:1, 512:1024],
                                  AF.Exp, scale=negh_sb[0:1, :])
                    a_r5 = ac_mark()
                    wait(SY, "ac", a_r5)
                    wait(SY, "ve")
                    d_w5 = mark(SY.dma_start(stat_d[0:1, 0:1024],
                                             stat5[0:1, 0:1024]), "dst", 16)
                    dwait(SY, "dst", d_w5)
                    d_r5 = mark(SY.dma_start(
                        stat5b[:], stat_d[0:1, 0:1024]
                        .partition_broadcast(128).squeeze(1)), "dst", 16)
                    dwait(VE, "dst", d_r5)
                    for kc2 in range(0, KC, 2):
                        for q in range(2):
                            VE.tensor_sub(t15[q][:],
                                          y_acc[:, kc2 + q, ts0:ts0 + 512],
                                          stat5b[:, 0:512])
                        for q in range(2):
                            VE.tensor_mul(x3[:, kc2 + q, :], t15[q][:],
                                          stat5b[:, 512:1024])
                    v_x3 = mark(VE.engine_nop(), "ve")
                    wait(PE, "ve", v_x3)
                    for tt in range(4):
                        g = tt % 2
                        if (tb, tt) in st_marks:
                            pass
                        prev = st_marks.get(g)
                        for kc2 in range(KC):
                            bank = 4 + (kc2 % 2)
                            tr(pb[bank][:, 0:128],
                                         x3[:, kc2, tt * 128:(tt + 1) * 128])
                            t_t5 = pe_mark()
                            wait(VE, "pe", t_t5)
                            if prev is not None and kc2 == 0:
                                wait(VE, "st")
                            VE.tensor_copy(ost[g][:, kc2 * 128:(kc2 + 1) * 128],
                                           pb[bank][:, 0:128])
                            v_t5 = mark(VE.engine_nop(), "ve")
                            wait(PE, "ve", v_t5)
                        v_o = mark(VE.engine_nop(), "ve")
                        wait(SY, "ve", v_o)
                        st_marks[g] = mark(SY.dma_start(
                            out_d[ts0 + tt * 128:ts0 + (tt + 1) * 128, :],
                            ost[g][:]), "st", 16)
                SY.wait_ge(sems["st"][0], sems["st"][1])

            quiesce(SY)
            PO.wait_ge(sems["dpo"][0], sems["dpo"][1])
            SY.wait_ge(sems["st"][0], sems["st"][1])

        _body(None)

        def _replay(stream):
            def f(_eng):
                for fn, a, k, h in stream:
                    inst = fn(*a, **k)
                    for sem, n in h.incs:
                        inst.then_inc(sem, n)
            return f

        block.sync(_replay(streams["sy"]))
        block.vector(_replay(streams["ve"]))
        block.scalar(_replay(streams["ac"]))
        block.tensor(_replay(streams["pe"]))
        block.gpsimd(_replay(streams["po"]))

    return nc, dbg_outs


def _shard_inputs(inputs):
    x = np.ascontiguousarray(np.asarray(inputs["L0_attn_x"], dtype=np.float32))
    conv_w = np.ascontiguousarray(
        np.asarray(inputs["L0_attn_conv_w"], dtype=np.float32)[:, 0, :])

    def f32(k):
        return np.ascontiguousarray(np.asarray(inputs[k], dtype=np.float32))

    common = {
        "ipw": f32("L0_attn_in_proj_w"), "ipb": f32("L0_attn_in_proj_b"),
        "cw": conv_w, "cb": f32("L0_attn_conv_b"),
        "bcw": f32("L0_attn_bc_w"), "bcb": f32("L0_attn_bc_b"),
        "dtw": f32("L0_attn_dt_w"), "dtb": f32("L0_attn_dt_b"),
        "alog": f32("L0_attn_A_log"), "dvec": f32("L0_attn_D"),
        "ow": f32("L0_attn_out_w"), "ob": f32("L0_attn_out_b"),
        "lng": f32("L0_attn_ln_g"), "lnb": f32("L0_attn_ln_b"),
        "w1": f32("L0_ffn_W1"), "b1": f32("L0_ffn_b1"),
        "w2": f32("L0_ffn_W2"), "b2": f32("L0_ffn_b2"),
        "g2": f32("L0_ffn_ln_gamma"), "be2": f32("L0_ffn_ln_beta"),
        "wr": f32("L0_ffn_Wr"), "br": f32("L0_ffn_br"),
    }
    in_maps = []
    for c in range(8):
        mm = dict(common)
        mm["x"] = np.ascontiguousarray(x[c])
        in_maps.append(mm)
    return in_maps


def kernel(**inputs):
    in_maps = _shard_inputs(inputs)
    nc, _ = build()
    res = run_bass_kernel_spmd(nc, in_maps, core_ids=list(range(8)))
    out = np.stack([np.asarray(res.results[c]["out"]) for c in range(8)], axis=0)
    return np.ascontiguousarray(out.astype(np.float32))


# version 05
# speedup vs baseline: 2.8900x; 2.8900x over previous
import numpy as np
from contextlib import ExitStack

import concourse.bass as bass
import concourse.mybir as mybir
from concourse.bass_utils import run_bass_kernel_spmd

F32 = mybir.dt.float32
BF16 = mybir.dt.bfloat16
AF = mybir.ActivationFunctionType
ALU = mybir.AluOpType
AX = mybir.AxisListType

D = 1024
KC = 8
L = 2048
TC = 256
NCH = L // TC
NST = 16
FF = 4096
FT = 512
NFT = L // FT
EPS = 1e-5
INV_D = 1.0 / D

SAMPLE_ITERS = 50
FULL_ITERS = 22


class _IH:

    __slots__ = ("incs", "pe_val")

    def __init__(self):
        self.incs = []
        self.pe_val = None

    def then_inc(self, sem, n):
        self.incs.append((sem, n))
        return self


class _Rec:

    def __init__(self, real, stream):
        self._real = real
        self._stream = stream

    def __getattr__(self, op):
        fn = getattr(self._real, op)

        def call(*a, **k):
            h = _IH()
            self._stream.append((fn, a, k, h))
            return h

        return call


def _sel_k(n):
    return max(1, int(n * 0.05))


def build(debug=(), phases=("P0", "ce2", "ce4", "P0w", "P0bar", "P1", "P2", "P3", "P4", "P5")):
    nc = bass.Bass("TRN2", target_bir_lowering=False, detect_race_conditions=False)
    dbg = set(debug)
    phset = set(phases)

    def din(name, shape, dt=F32):
        return nc.dram_tensor(name, shape, dt, kind="ExternalInput").ap()

    x_in = din("x", [L, D])
    ipw = din("ipw", [2 * D, D])
    ipb = din("ipb", [2 * D])
    cw = din("cw", [D, 4])
    cb = din("cb", [D])
    bcw = din("bcw", [2 * NST, D])
    bcb = din("bcb", [2 * NST])
    dtw = din("dtw", [D, D])
    dtb = din("dtb", [D])
    alog = din("alog", [D, NST])
    dvec = din("dvec", [D])
    ow = din("ow", [D, D])
    ob_ = din("ob", [D])
    lng = din("lng", [D])
    lnb = din("lnb", [D])
    w1 = din("w1", [D, FF])
    b1 = din("b1", [FF])
    w2 = din("w2", [FF, D])
    b2 = din("b2", [D])
    g2 = din("g2", [D])
    be2 = din("be2", [D])
    wr = din("wr", [D, 8])
    br = din("br", [8])

    out_d = nc.dram_tensor("out", [L, D], F32, kind="ExternalOutput").ap()

    ipwT_d = nc.dram_tensor("ipwT_d", [D, 2 * D], BF16, kind="Internal").ap()
    dtwT_d = nc.dram_tensor("dtwT_d", [D, D], BF16, kind="Internal").ap()
    owT_d = nc.dram_tensor("owT_d", [D, D], BF16, kind="Internal").ap()
    w1bf_d = nc.dram_tensor("w1bf_d", [D, FF], BF16, kind="Internal").ap()
    w1pbf_d = nc.dram_tensor("w1pbf_d", [D, FF], BF16, kind="Internal").ap()
    w2bf_d = nc.dram_tensor("w2bf_d", [FF, D], BF16, kind="Internal").ap()
    w2pbf_d = nc.dram_tensor("w2pbf_d", [FF, D], BF16, kind="Internal").ap()
    bc_d = nc.dram_tensor("bc_d", [2 * NST, L], F32, kind="Internal").ap()
    x2_d = nc.dram_tensor("x2_d", [D, L], F32, kind="Internal").ap()
    stat_d = nc.dram_tensor("stat_d", [1, 1024], F32, kind="Internal").ap()
    prow_d = nc.dram_tensor("prow_d", [1, L], F32, kind="Internal").ap()

    dbg_outs = {}
    tr_d = None

    def dout(name, shape, dt=F32):
        if name in dbg:
            dbg_outs[name] = nc.dram_tensor("dbg_" + name, shape, dt,
                                            kind="ExternalOutput").ap()
            return dbg_outs[name]
        return None

    with ExitStack() as ctx:
        e = ctx.enter_context

        sems = {}
        for s in ("pe", "ve", "po", "ac", "sy", "dpo", "st",
                  "d0", "dsl", "dt0", "dt1", "d1", "d2", "d2a", "d2b",
                  "dw0", "dw1", "dbs", "dbc0", "dbc1", "dx", "dx2", "dst",
                  "d4a", "d4b"):
            sems[s] = [e(nc.semaphore("s_" + s)), 0]

        def mark(inst, s, n=1):
            sems[s][1] += n
            inst.then_inc(sems[s][0], n)
            return sems[s][1]

        def wait(eng, s, v=None):
            if s[0] == "d" or s == "st":
                assert v is None or v == sems[s][1], (s, v, sems[s][1])
                v = None
            eng.wait_ge(sems[s][0], sems[s][1] if v is None else v)

        def dwait(eng, s, v):
            assert v == sems[s][1], (s, v, sems[s][1])
            eng.wait_ge(sems[s][0], v)

        def quiesce(eng):
            for s in ("d0", "dsl", "dt0", "dt1", "d1", "d2", "d2a", "d2b",
                      "dw0", "dw1", "dbs", "dbc0", "dbc1", "dx", "dx2",
                      "dst", "d4a", "d4b", "st"):
                if sems[s][1]:
                    eng.wait_ge(sems[s][0], sems[s][1])

        streams = {k: [] for k in ("pe", "ve", "po", "ac", "sy")}
        PE = _Rec(nc.tensor, streams["pe"])
        VE = _Rec(nc.vector, streams["ve"])
        PO = _Rec(nc.gpsimd, streams["po"])
        AC = _Rec(nc.scalar, streams["ac"])
        SY = _Rec(nc.sync, streams["sy"])

        def tr(out_ap, in_ap):
            p = in_ap.shape[0]
            PE.transpose(out_ap, in_ap, ident[0:p, 0:p])
        ENG = {"pe": PE, "ve": VE, "po": PO, "ac": AC, "sy": SY}

        def barrier():
            quiesce(ENG["sy"])
            if sems["dpo"][1]:
                ENG["po"].wait_ge(sems["dpo"][0], sems["dpo"][1])
            for nm, eng in ENG.items():
                if nm == "ac":
                    ac_mark()
                elif nm == "pe":
                    if streams["pe"] and streams["pe"][-1][0].__name__ in (
                            "matmul", "transpose"):
                        pe_mark()
                    else:
                        mark(eng.nop(), nm)
                elif nm in ("ve", "po"):
                    mark(eng.engine_nop(), nm)
                else:
                    mark(eng.nop(), nm)
            vals = {nm: sems[nm][1] for nm in ENG}
            for nm, eng in ENG.items():
                for nm2 in ENG:
                    if nm2 != nm:
                        wait(eng, nm2, vals[nm2])

        A_sb = e(nc.sbuf_tensor("A_sb", [128, KC, NST], F32))
        cw_sb = e(nc.sbuf_tensor("cw_sb", [128, KC, 4], F32))
        cb_sb = e(nc.sbuf_tensor("cb_sb", [128, KC], F32))
        dvec_sb = e(nc.sbuf_tensor("dvec_sb", [128, KC], F32))
        lng_sb = e(nc.sbuf_tensor("lng_sb", [128, KC], F32))
        lnb_sb = e(nc.sbuf_tensor("lnb_sb", [128, KC], F32))
        g2_sb = e(nc.sbuf_tensor("g2_sb", [128, KC], F32))
        be2_sb = e(nc.sbuf_tensor("be2_sb", [128, KC], F32))
        ipb_sb = e(nc.sbuf_tensor("ipb_sb", [128, 16], F32))
        dtb_sb = e(nc.sbuf_tensor("dtb_sb", [128, KC], F32))
        ob_sb = e(nc.sbuf_tensor("ob_sb", [128, KC], F32))
        bcbB_sb = e(nc.sbuf_tensor("bcbB_sb", [16, 1], F32))
        bcbC_sb = e(nc.sbuf_tensor("bcbC_sb", [16, 1], F32))
        bcb_sb = e(nc.sbuf_tensor("bcb_sb", [32, 1], F32))
        b1_sb = e(nc.sbuf_tensor("b1_sb", [128, 32], F32))
        b1p_sb = e(nc.sbuf_tensor("b1p_sb", [128, 32], F32))
        b2_sb = e(nc.sbuf_tensor("b2_sb", [128, KC], F32))
        b2p_sb = e(nc.sbuf_tensor("b2p_sb", [128, KC], F32))
        bcwT_sb = e(nc.sbuf_tensor("bcwT_sb", [128, KC, 32], BF16))
        wrg_sb = e(nc.sbuf_tensor("wrg_sb", [128, KC, 8], BF16))
        ceb_sb = e(nc.sbuf_tensor("ceb_sb", [128, 8], F32))
        bewb_sb = e(nc.sbuf_tensor("bewb_sb", [128, 8], F32))
        ones_sb = e(nc.sbuf_tensor("ones_sb", [128, 128], F32))
        onesr_sb = e(nc.sbuf_tensor("onesr_sb", [1, 128], F32))
        ident = e(nc.sbuf_tensor("ident", [128, 128], F32))
        half_sb = e(nc.sbuf_tensor("half_sb", [128, 1], F32))
        neg1_sb = e(nc.sbuf_tensor("neg1_sb", [128, 1], F32))
        negh_sb = e(nc.sbuf_tensor("negh_sb", [128, 1], F32))
        one_sb = e(nc.sbuf_tensor("one_sb", [128, 1], F32))
        eps_sb = e(nc.sbuf_tensor("eps_sb", [128, 1], F32))
        thr_sb = e(nc.sbuf_tensor("thr_sb", [128, 8], F32))
        dumA = e(nc.sbuf_tensor("dumA", [128, 4], F32))
        dumB = e(nc.sbuf_tensor("dumB", [128, 4], F32))
        acd1 = e(nc.sbuf_tensor("acd1", [128, 1], F32))
        acd2 = e(nc.sbuf_tensor("acd2", [128, 1], F32))
        vstage = e(nc.sbuf_tensor("vstage", [32, 128], F32))
        b1a_sb = e(nc.sbuf_tensor("b1a_sb", [128, 32], F32))
        b2a_sb = e(nc.sbuf_tensor("b2a_sb", [128, KC], F32))

        pb = [e(nc.psum_tensor(f"pb{i}", [128, 512], F32)) for i in range(8)]

        def bubble():
            VE.tensor_copy(dumA[:], dumB[:])

        def ac_mark(s="ac"):
            return mark(AC.activation(acd1[:], acd2[:], AF.Identity), s)

        def pe_mark(s="pe"):
            fn, a, k, h = streams["pe"][-1]
            assert fn.__name__ in ("matmul", "transpose"), fn.__name__
            if getattr(h, "pe_val", None) is not None:
                return h.pe_val
            sems[s][1] += 1
            h.incs.append((sems[s][0], 1))
            h.pe_val = sems[s][1]
            return sems[s][1]

        p1tr = nc.dram_tensor("dbg_p1trace", [SAMPLE_ITERS + 2 * FULL_ITERS, 24],
                              F32, kind="ExternalOutput").ap()             if "p1trace" in dbg else None
        if p1tr is not None:
            dbg_outs["p1trace"] = p1tr
        tr_i = [0]
        block = e(nc.Block())

        def _body(sync):
            VE.memset(ones_sb[:], 1.0)
            VE.memset(onesr_sb[:], 1.0)
            VE.memset(half_sb[:], 0.5)
            VE.memset(neg1_sb[:], -1.0)
            VE.memset(negh_sb[:], -0.5)
            VE.memset(one_sb[:], 1.0)
            VE.memset(eps_sb[:], EPS)
            VE.memset(dumB[:], 0.0)
            VE.memset(dumA[:], 0.0)
            VE.memset(acd1[:], 0.0)
            VE.memset(acd2[:], 0.0)
            v_const = mark(VE.engine_nop(), "ve")
            PO.memset(ident[:], 0.0)
            PO.affine_select(out=ident[:], in_=ident[:],
                             compare_op=ALU.not_equal, fill=1.0,
                             base=0, pattern=[[-1, 128]], channel_multiplier=1)
            p_ident = mark(PO.engine_nop(), "po")
            wait(PE, "po", p_ident)
            wait(PE, "ve", v_const)

            smalls = [] if "noSmalls" in phset else [
                      (ipb, ipb_sb, 16), (cb, cb_sb, 8), (dtb, dtb_sb, 8),
                      (ob_, ob_sb, 8), (dvec, dvec_sb, 8), (lng, lng_sb, 8),
                      (lnb, lnb_sb, 8), (g2, g2_sb, 8), (be2, be2_sb, 8),
                      (b2, b2_sb, 8), (b1, b1_sb, 32)]
            for si, (src, dst, w) in enumerate(smalls):
                for hw0 in range(0, w, 16):
                    hw = min(16, w - hw0)
                    d_ld = mark(SY.dma_start(
                        vstage[0:hw, :],
                        src[hw0 * 128:(hw0 + hw) * 128]
                        .rearrange("(a p) -> a p", p=128)), "d0", 16)
                    dwait(PE, "d0", d_ld)
                    tr(pb[0][:, 0:hw], vstage[0:hw, :])
                    t_v = pe_mark()
                    wait(VE, "pe", t_v)
                    VE.tensor_copy(dst[:, hw0:hw0 + hw], pb[0][:, 0:hw])
                    v_done = mark(VE.engine_nop(), "ve")
                    wait(PE, "ve", v_done)
                    wait(SY, "ve", v_done)

            if "noBcb" not in phset:
              mark(SY.dma_start(bcb_sb[:], bcb.unsqueeze(1)), "d0", 16)
              wait(PO, "d0")
              mark(PO.dma_start(bcbB_sb[:], bcb_sb[0:16, :]), "dpo", 16)
              mark(PO.dma_start(bcbC_sb[:], bcb_sb[16:32, :]), "dpo", 16)
              mark(SY.dma_start(cw_sb[:],
                                cw.rearrange("(c p) k -> p c k", p=128)), "d0", 16)
            if "noA" not in phset:
              d_alog = mark(SY.dma_start(
                  A_sb[:], alog.rearrange("(c p) n -> p c n", p=128)), "d0", 16)
              dwait(AC, "d0", d_alog)
              AC.activation(A_sb[:], A_sb[:], AF.Exp)
              AC.activation(A_sb[:], A_sb[:], AF.Identity, scale=neg1_sb[:])
              a_A = ac_mark()

            if "noCe" not in phset:
              with ExitStack() as p0ctx:
                e0 = p0ctx.enter_context
                wr_sb = e0(nc.sbuf_tensor("wr_sb", [128, KC, 8], F32))
                br_sb = e0(nc.sbuf_tensor("br_sb", [1, 8], F32))
                cerow = e0(nc.sbuf_tensor("cerow", [1, 16], F32))
                bcw_stage = e0(nc.sbuf_tensor("bcw_stage", [16, D], F32))
                bcw_stage2 = e0(nc.sbuf_tensor("bcw_stage2", [16, D], F32))

                d_wr = mark(SY.dma_start(
                    wr_sb[:], wr.rearrange("(c p) e -> p c e", p=128)), "d0", 16)
                d_br = mark(SY.dma_start(br_sb[:], br.unsqueeze(0)),
                            "d0", 16)
                dwait(VE, "d0", d_br)
                for c in range(KC):
                    VE.tensor_scalar(wrg_sb[:, c, :], wr_sb[:, c, :],
                                     g2_sb[:, c:c + 1], None, ALU.mult)
                v_wrg = mark(VE.engine_nop(), "ve")
                if "ce2" in phset or "ceAll" in phset:
                    gtmp = e0(nc.sbuf_tensor("gtmp", [128, KC, 8], F32))
                    btmp = e0(nc.sbuf_tensor("btmp", [128, KC, 8], F32))
                    cat = e0(nc.sbuf_tensor("cat", [128, 16], F32))
                    VE.tensor_tensor(gtmp[:], wr_sb[:],
                                     g2_sb[:].unsqueeze(2)
                                     .broadcast_to((128, KC, 8)), ALU.mult)
                    VE.tensor_tensor(btmp[:], wr_sb[:],
                                     be2_sb[:].unsqueeze(2)
                                     .broadcast_to((128, KC, 8)), ALU.mult)
                    bubble()
                    VE.reduce_sum(cat[:, 0:8], gtmp[:].transpose([0, 2, 1]),
                                  axis=AX.X)
                    VE.reduce_sum(cat[:, 8:16], btmp[:].transpose([0, 2, 1]),
                                  axis=AX.X)
                    v_rep = mark(VE.engine_nop(), "ve")
                    wait(PE, "ve", v_rep)
                    PE.matmul(pb[1][0:2, 0:16], ones_sb[:, 0:2], cat[:],
                              start=True, stop=True)
                    t_cb = pe_mark()
                    wait(VE, "pe", t_cb)
                    pass
                    VE.tensor_copy(cerow[0:1, 0:8], pb[1][0:1, 0:8])
                    VE.tensor_add(cerow[0:1, 8:16], pb[1][0:1, 8:16], br_sb[:])
                    v_ce = mark(VE.engine_nop(), "ve")
                    wait(SY, "ve", v_ce)
                    d_cw = mark(SY.dma_start(stat_d[0:1, 0:16], cerow[:]),
                                "dst", 16)
                    dwait(SY, "dst", d_cw)
                    mark(SY.dma_start(ceb_sb[:], stat_d[0:1, 0:8]
                                      .partition_broadcast(128).squeeze(1)),
                         "dst", 16)
                    d_ceb = mark(SY.dma_start(
                        bewb_sb[:], stat_d[0:1, 8:16]
                        .partition_broadcast(128).squeeze(1)), "dst", 16)
                    dwait(VE, "dst", d_ceb)
                    v_ceb = mark(VE.engine_nop(), "ve")
                if "ce4" in phset or "ceAll" in phset:
                    mark(SY.dma_start(bcw_stage[:], bcw[0:16, :]), "d0", 16)
                    d_bcw = mark(SY.dma_start(bcw_stage2[:], bcw[16:32, :]),
                                 "d0", 16)
                    dwait(PE, "d0", d_bcw)
                    wait(PE, "ve")
                    evs = {}
                    for c in range(KC):
                        bank = 2 + (c % 2)
                        if c >= 2:
                            wait(PE, "ve", evs[c - 2])
                        tr(pb[bank][:, 0:16],
                           bcw_stage[:, c * 128:(c + 1) * 128])
                        tr(pb[bank][:, 16:32],
                           bcw_stage2[:, c * 128:(c + 1) * 128])
                        t_t = pe_mark()
                        wait(VE, "pe", t_t)
                        VE.tensor_copy(bcwT_sb[:, c, :], pb[bank][:, 0:32])
                        evs[c] = mark(VE.engine_nop(), "ve")
                tsl = [e0(nc.sbuf_tensor(f"tsl{i}", [128, D], F32))
                       for i in range(2)]
                ttb = [e0(nc.sbuf_tensor(f"ttb{i}", [128, 128], BF16))
                       for i in range(2)]

                def transpose_weight(src_, dst_d, O):
                    n_o = O // 128
                    slab_rel = {}
                    tile_rel = {}
                    tile_st = {}
                    ti = 0
                    for obk in range(n_o):
                        r = obk % 2
                        if obk >= 2:
                            wait(SY, "ve", slab_rel[obk - 2])
                        d_sl = mark(SY.dma_start(
                            tsl[r][:], src_[obk * 128:(obk + 1) * 128, :]),
                            "dsl", 16)
                        dwait(PE, "dsl", d_sl)
                        for ib in range(KC):
                            trr = ti % 2
                            bank = 4 + trr
                            if ti >= 2:
                                wait(PE, "ve", tile_rel[ti - 2])
                            tr(pb[bank][:, 0:128],
                               tsl[r][:, ib * 128:(ib + 1) * 128])
                            t_tr = pe_mark()
                            wait(VE, "pe", t_tr)
                            if ti >= 2:
                                dwait(VE, "dt%d" % (ti % 2), tile_st[ti - 2])
                            VE.tensor_copy(ttb[trr][:], pb[bank][:, 0:128])
                            v_ev = mark(VE.engine_nop(), "ve")
                            tile_rel[ti] = v_ev
                            if ib == KC - 1:
                                slab_rel[obk] = v_ev
                            wait(SY, "ve", v_ev)
                            tile_st[ti] = mark(SY.dma_start(
                                dst_d[ib * 128:(ib + 1) * 128,
                                      obk * 128:(obk + 1) * 128],
                                ttb[trr][:]), "dt%d" % (ti % 2), 16)
                            ti += 1

                if "P0w" in phset:
                    transpose_weight(ipw, ipwT_d, 2 * D)
                    transpose_weight(dtw, dtwT_d, D)
                    transpose_weight(ow, owT_d, D)
                if "P0bar" in phset:
                    barrier()

            quiesce(SY)
            t_cat = dout("cat", [128, 16])
            if t_cat is not None:
                wait(SY, "ve")
                mark(SY.dma_start(t_cat, cat[:]), "st", 16)
            t_cerow = dout("cerow", [1, 16])
            if t_cerow is not None:
                mark(SY.dma_start(t_cerow, cerow[:]), "st", 16)
            t_A = dout("A", [128, KC * NST])
            if t_A is not None:
                mark(SY.dma_start(t_A, A_sb[:].rearrange("p c n -> p (c n)")), "st", 16)
            t_ipb = dout("ipb", [128, 16])
            if t_ipb is not None:
                mark(SY.dma_start(t_ipb, ipb_sb[:]), "st", 16)
            t_ceb = dout("ceb", [128, 8])
            if t_ceb is not None:
                mark(SY.dma_start(t_ceb, ceb_sb[:]), "st", 16)
            t_bewb = dout("bewb", [128, 8])
            if t_bewb is not None:
                mark(SY.dma_start(t_bewb, bewb_sb[:]), "st", 16)
            t_ipwT = dout("ipwT", [D, 2 * D], BF16)
            if t_ipwT is not None:
                mark(SY.dma_start(t_ipwT, ipwT_d), "st", 16)
            t_cwsb = dout("cwsb", [128, KC * 4])
            if t_cwsb is not None:
                mark(SY.dma_start(t_cwsb, cw_sb[:].rearrange("p c k -> p (c k)")), "st", 16)
            SY.wait_ge(sems["st"][0], sems["st"][1])

            nW1, kW1 = D * FF, _sel_k(D * FF)
            nW2, kW2 = FF * D, _sel_k(FF * D)
            nb1, kb1 = FF, _sel_k(FF)
            nb2, kb2 = D, _sel_k(D)

            if "noBabs" not in phset:
              VE.scalar_tensor_tensor(b1a_sb[:], b1_sb[:], -1.0, b1_sb[:],
                                      ALU.mult, ALU.max)
              VE.scalar_tensor_tensor(b2a_sb[:], b2_sb[:], -1.0, b2_sb[:],
                                      ALU.mult, ALU.max)
              a_babs = mark(VE.engine_nop(), "ve")

            if "P1" in phset:
              with ExitStack() as p1ctx:
                e1 = p1ctx.enter_context
                NCOL = 8
                lo_sb = e1(nc.sbuf_tensor("lo_sb", [128, NCOL], F32))
                hi_sb = e1(nc.sbuf_tensor("hi_sb", [128, NCOL], F32))
                mid_sb = e1(nc.sbuf_tensor("mid_sb", [128, NCOL], F32))
                K_sb = e1(nc.sbuf_tensor("K_sb", [128, NCOL], F32))
                cnt_sb = e1(nc.sbuf_tensor("cnt_sb", [128, 4 * NCOL], F32))
                tot_sb = e1(nc.sbuf_tensor("tot_sb", [128, NCOL], F32))
                cond_sb = e1(nc.sbuf_tensor("cond_sb", [128, NCOL], F32))
                sc1 = e1(nc.sbuf_tensor("sc1", [128, NCOL], F32))
                sc2 = e1(nc.sbuf_tensor("sc2", [128, NCOL], F32))
                sc3 = e1(nc.sbuf_tensor("sc3", [128, NCOL], F32))
                mveA = e1(nc.sbuf_tensor("mveA", [128, 8192], F32))
                p1f_sb = None
                if "p1f" in dbg:
                    p1f_sb = e1(nc.sbuf_tensor("p1f_sb", [128, 528], F32))
                w1s_sb = e1(nc.sbuf_tensor("w1s_sb", [128, 1024], F32))
                w2s_sb = e1(nc.sbuf_tensor("w2s_sb", [128, 1024], F32))

                d_s1 = mark(SY.dma_start(
                    w1s_sb[:], w1[0:32, :].rearrange("a f -> (a f)")
                    .rearrange("(p q) -> p q", p=128)), "d1", 16)
                d_s2 = mark(SY.dma_start(w2s_sb[:], w2[0:128, :]), "d1", 16)
                dwait(VE, "d1", d_s2)
                VE.scalar_tensor_tensor(w1s_sb[:], w1s_sb[:], -1.0, w1s_sb[:],
                                        ALU.mult, ALU.max)
                VE.scalar_tensor_tensor(w2s_sb[:], w2s_sb[:], -1.0, w2s_sb[:],
                                        ALU.mult, ALU.max)
                a_sabs = mark(VE.engine_nop(), "ve")

                def upd_chain(c0, c1):
                    VE.tensor_copy(tot_sb[:, c0:c1], pb[7][:, c0:c1])
                    bubble()
                    upd_chain2(c0, c1)

                def upd_chain2(c0, c1):
                    VE.tensor_tensor(cond_sb[:, c0:c1], tot_sb[:, c0:c1],
                                     K_sb[:, c0:c1], ALU.is_ge)
                    VE.tensor_sub(sc1[:, c0:c1], mid_sb[:, c0:c1], hi_sb[:, c0:c1])
                    VE.tensor_sub(sc2[:, c0:c1], mid_sb[:, c0:c1], lo_sb[:, c0:c1])
                    VE.tensor_mul(sc3[:, c0:c1], sc1[:, c0:c1], cond_sb[:, c0:c1])
                    VE.tensor_mul(sc1[:, c0:c1], sc2[:, c0:c1], cond_sb[:, c0:c1])
                    VE.tensor_add(hi_sb[:, c0:c1], hi_sb[:, c0:c1], sc3[:, c0:c1])
                    VE.tensor_sub(lo_sb[:, c0:c1], mid_sb[:, c0:c1], sc1[:, c0:c1])
                    bubble()
                    bubble()
                    VE.tensor_add(sc2[:, c0:c1], lo_sb[:, c0:c1], hi_sb[:, c0:c1])
                    bubble()
                    bubble()
                    VE.tensor_scalar(mid_sb[:, c0:c1], sc2[:, c0:c1], 0.5,
                                     None, ALU.mult)
                    bubble()
                    bubble()

                def count_iter(jobs, c0, c1, v_mid):
                    merge = {}
                    for data, col, sub in jobs:
                        n = data.shape[-1]
                        VE.tensor_scalar(mveA[:, 0:n], data,
                                         mid_sb[:, col:col + 1], None,
                                         ALU.is_le, ALU.add,
                                         accum_out=cnt_sb[:, col * 4 + sub:
                                                          col * 4 + sub + 1])
                        merge.setdefault(col, []).append(col * 4 + sub)
                    bubble()
                    for col in range(c0, c1):
                        subs = merge.get(col, [])
                        if len(subs) == 1:
                            VE.tensor_copy(sc3[:, col:col + 1],
                                           cnt_sb[:, subs[0]:subs[0] + 1])
                        else:
                            VE.tensor_add(sc3[:, col:col + 1],
                                          cnt_sb[:, subs[0]:subs[0] + 1],
                                          cnt_sb[:, subs[1]:subs[1] + 1])
                            for s3 in subs[2:]:
                                bubble()
                                VE.tensor_add(sc3[:, col:col + 1],
                                              sc3[:, col:col + 1],
                                              cnt_sb[:, s3:s3 + 1])
                    v_mrg = mark(VE.engine_nop(), "ve")
                    wait(PE, "ve", v_mrg)
                    PE.matmul(pb[7][:, c0:c1], ones_sb[:], sc3[:, c0:c1],
                              start=True, stop=True)
                    t_tot = pe_mark()
                    wait(VE, "pe", t_tot)
                    upd_chain(c0, c1)
                    vm = mark(VE.engine_nop(), "ve")
                    if p1tr is not None:
                        wait(SY, "ve", vm)
                        mark(SY.dma_start(p1tr[tr_i[0]:tr_i[0] + 1, 0:8],
                                          mid_sb[0:1, :]), "st", 16)
                        mark(SY.dma_start(p1tr[tr_i[0]:tr_i[0] + 1, 8:16],
                                          tot_sb[0:1, :]), "st", 16)
                        mark(SY.dma_start(p1tr[tr_i[0]:tr_i[0] + 1, 16:24],
                                          hi_sb[0:1, :]), "st", 16)
                        wait(SY, "st")
                        tr_i[0] += 1
                    return vm

                ns = 131072
                Ks = [max(1, round(kW1 / nW1 * ns)),
                      max(1, round((nW1 - kW1 + 1) / nW1 * ns)),
                      max(1, round(kW2 / nW2 * ns)),
                      max(1, round((nW2 - kW2 + 1) / nW2 * ns)),
                      kb1, nb1 - kb1 + 1, kb2, nb2 - kb2 + 1]
                for col in range(8):
                    VE.memset(lo_sb[:, col:col + 1], -1.0)
                    VE.memset(hi_sb[:, col:col + 1], 65536.0)
                    VE.memset(K_sb[:, col:col + 1], float(Ks[col]))
                    VE.memset(mid_sb[:, col:col + 1], 0.5 * (65536.0 - 1.0))
                v_mid = mark(VE.engine_nop(), "ve")
                for it in range(SAMPLE_ITERS):
                    jobs = [(w1s_sb[:], 0, 0), (w1s_sb[:], 1, 0),
                            (w2s_sb[:], 2, 0), (w2s_sb[:], 3, 0),
                            (b1a_sb[:], 4, 0), (b1a_sb[:], 5, 0),
                            (b2a_sb[:], 6, 0), (b2a_sb[:], 7, 0)]
                    v_mid = count_iter(jobs, 0, 8, v_mid)
                VE.tensor_copy(thr_sb[:, 4:8], hi_sb[:, 4:8])
                VE.tensor_scalar(lo_sb[:, 0:4], hi_sb[:, 0:4], 0.9, None, ALU.mult)
                VE.tensor_scalar(hi_sb[:, 0:4], hi_sb[:, 0:4], 1.1, None, ALU.mult)
                VE.memset(K_sb[:, 0:1], float(kW1))
                VE.memset(K_sb[:, 1:2], float(nW1 - kW1 + 1))
                VE.memset(K_sb[:, 2:3], float(kW2))
                VE.memset(K_sb[:, 3:4], float(nW2 - kW2 + 1))
                bubble()
                VE.tensor_add(sc2[:, 0:4], lo_sb[:, 0:4], hi_sb[:, 0:4])
                bubble()
                VE.tensor_scalar(mid_sb[:, 0:4], sc2[:, 0:4], 0.5, None, ALU.mult)
                v_mid = mark(VE.engine_nop(), "ve")

                for wi, (wsrc, c0) in enumerate(((w1, 0), (w2, 2))):
                    with ExitStack() as selctx:
                        wa = selctx.enter_context(
                            nc.sbuf_tensor(f"wa{wi}", [128, 32768], F32))
                        src_flat = wsrc.rearrange("a f -> (a f)").rearrange(
                            "(p q) -> p q", p=128)
                        for sl in range(4):
                            d_l = mark(SY.dma_start(
                                wa[:, sl * 8192:(sl + 1) * 8192],
                                src_flat[:, sl * 8192:(sl + 1) * 8192]),
                                "d1", 16)
                            dwait(SY, "d1", d_l)
                            dwait(VE, "d1", d_l)
                            VE.scalar_tensor_tensor(
                                wa[:, sl * 8192:(sl + 1) * 8192],
                                wa[:, sl * 8192:(sl + 1) * 8192], -1.0,
                                wa[:, sl * 8192:(sl + 1) * 8192],
                                ALU.mult, ALU.max)
                            bubble()
                        for it in range(FULL_ITERS):
                            jobs = []
                            for col in (c0, c0 + 1):
                                for sb_ in range(4):
                                    jobs.append((wa[:, sb_ * 8192:
                                                    (sb_ + 1) * 8192], col, sb_))
                            v_mid = count_iter(jobs, c0, c0 + 2, v_mid)
                        VE.tensor_copy(thr_sb[:, c0:c0 + 2], hi_sb[:, c0:c0 + 2])
                        v_thr = mark(VE.engine_nop(), "ve")
                        wait(SY, "ve", v_thr)
                        barrier()

                t_p1f = dout("p1f", [128, 528])
                if t_p1f is not None:
                    mark(SY.dma_start(t_p1f, p1f_sb[:]), "st", 16)
                    SY.wait_ge(sems["st"][0], sems["st"][1])

            t_thr = dout("thr", [128, 8])
            if t_thr is not None:
                mark(SY.dma_start(t_thr, thr_sb[:]), "st", 16)
                SY.wait_ge(sems["st"][0], sems["st"][1])

            if "P2" in phset:
              with ExitStack() as p2ctx:
                e2 = p2ctx.enter_context
                SW = 4096
                slabs = [e2(nc.sbuf_tensor(f"pslab{i}", [128, SW], F32))
                         for i in range(2)]
                absb = [e2(nc.sbuf_tensor(f"pabs{i}", [128, SW], F32))
                        for i in range(2)]
                i1b = [e2(nc.sbuf_tensor(f"pi1{i}", [128, SW], F32))
                       for i in range(2)]
                obf = [e2(nc.sbuf_tensor(f"pobf{i}", [128, SW], BF16))
                       for i in range(2)]
                pbf = [e2(nc.sbuf_tensor(f"ppbf{i}", [128, SW], BF16))
                       for i in range(2)]

                def build_p(wsrc, dst_clean, dst_pert, nelem, bot_col, top_col):
                    nsl = nelem // (128 * SW)
                    srcf = wsrc.rearrange("a f -> (a f)").rearrange(
                        "(p q) -> p q", p=128)
                    dstcf = dst_clean.rearrange("a f -> (a f)").rearrange(
                        "(p q) -> p q", p=128)
                    dstpf = dst_pert.rearrange("a f -> (a f)").rearrange(
                        "(p q) -> p q", p=128)
                    rel_po = {}
                    rel_sy = {}
                    for sl in range(nsl):
                        r = sl % 2
                        if sl >= 2:
                            wait(SY, "po", rel_po[sl - 2])
                            dwait(SY, "d2a" if r == 0 else "d2b",
                                  rel_sy[sl - 2])
                        d_l = mark(SY.dma_start(
                            slabs[r][:], srcf[:, sl * SW:(sl + 1) * SW]),
                            "d2", 16)
                        dwait(SY, "d2", d_l)
                        dwait(AC, "d2", d_l)
                        AC.activation(obf[r][:], slabs[r][:], AF.Identity)
                        a_p = ac_mark()
                        dwait(VE, "d2", d_l)
                        VE.scalar_tensor_tensor(absb[r][:], slabs[r][:], -1.0,
                                                slabs[r][:], ALU.mult, ALU.max)
                        bubble()
                        VE.tensor_scalar(i1b[r][:], absb[r][:],
                                         thr_sb[:, bot_col:bot_col + 1], None,
                                         ALU.is_le)
                        bubble()
                        VE.scalar_tensor_tensor(absb[r][:], absb[r][:],
                                                thr_sb[:, top_col:top_col + 1],
                                                i1b[r][:], ALU.is_ge, ALU.subtract)
                        bubble()
                        VE.tensor_scalar(i1b[r][:], absb[r][:], -0.8, 1.0,
                                         ALU.mult, ALU.add)
                        v_m = mark(VE.engine_nop(), "ve")
                        wait(PO, "ve", v_m)
                        PO.tensor_mul(pbf[r][:], slabs[r][:], i1b[r][:])
                        p_m = mark(PO.engine_nop(), "po")
                        rel_po[sl] = p_m
                        wait(SY, "ac", a_p)
                        wait(SY, "po", p_m)
                        sm = "d2a" if r == 0 else "d2b"
                        mark(SY.dma_start(dstcf[:, sl * SW:(sl + 1) * SW],
                                          obf[r][:]), sm, 16)
                        rel_sy[sl] = mark(SY.dma_start(
                            dstpf[:, sl * SW:(sl + 1) * SW], pbf[r][:]),
                            sm, 16)

                build_p(w1, w1bf_d, w1pbf_d, D * FF, 0, 1)
                build_p(w2, w2bf_d, w2pbf_d, FF * D, 2, 3)

                for (asb, vsb, psb, w, bc_, tc_) in (
                        (b1a_sb, b1_sb, b1p_sb, 32, 4, 5),
                        (b2a_sb, b2_sb, b2p_sb, KC, 6, 7)):
                    VE.tensor_scalar(i1b[0][:, 0:w], asb[:, 0:w],
                                     thr_sb[:, bc_:bc_ + 1], None, ALU.is_le)
                    bubble()
                    VE.scalar_tensor_tensor(absb[0][:, 0:w], asb[:, 0:w],
                                            thr_sb[:, tc_:tc_ + 1],
                                            i1b[0][:, 0:w], ALU.is_ge,
                                            ALU.subtract)
                    bubble()
                    VE.tensor_scalar(absb[0][:, 0:w], absb[0][:, 0:w],
                                     -0.8, 1.0, ALU.mult, ALU.add)
                    bubble()
                    VE.tensor_mul(psb[:, 0:w], vsb[:, 0:w], absb[0][:, 0:w])
                v_bp = mark(VE.engine_nop(), "ve")
                barrier()

            t_b1p = dout("b1p", [128, 32])
            if t_b1p is not None:
                mark(SY.dma_start(t_b1p, b1p_sb[:]), "st", 16)
                SY.wait_ge(sems["st"][0], sems["st"][1])

            if "P3" in phset:
              with ExitStack() as m:
                em = m.enter_context
                xstage = [em(nc.sbuf_tensor("xstage0", [128, D], F32))] * 2
                x_fm = em(nc.sbuf_tensor("x_fm", [128, KC, TC], F32))
                x_bf = em(nc.sbuf_tensor("x_bf", [128, KC, TC], BF16))
                x_ssm = em(nc.sbuf_tensor("x_ssm", [128, KC, TC + 3], F32))
                gate_sb = em(nc.sbuf_tensor("gate_sb", [128, KC, TC], F32))
                xconv = em(nc.sbuf_tensor("xconv", [128, KC, TC], F32))
                xconv_bf = em(nc.sbuf_tensor("xconv_bf", [128, KC, TC], BF16))
                dt_sb = em(nc.sbuf_tensor("dt_sb", [128, KC, TC], F32))
                y_all = em(nc.sbuf_tensor("y_all", [128, KC, TC], F32))
                rbuf = em(nc.sbuf_tensor("rbuf", [128, KC, TC], F32))
                vabuf = em(nc.sbuf_tensor("vabuf", [128, KC, TC], F32))
                zz_sb = rbuf
                w_all = x_ssm[:, :, 3:3 + TC]
                z_sb = x_ssm[:, :, 3:3 + TC]
                sq_sb = vabuf
                ygbf = x_bf
                vg = [em(nc.sbuf_tensor(f"vg{i}", [128, TC], F32))
                      for i in range(2)]
                convt = em(nc.sbuf_tensor("convt", [128, TC], F32))
                tg = [em(nc.sbuf_tensor(f"tg{i}", [128, TC], F32))
                      for i in range(2)]
                dA_b = [em(nc.sbuf_tensor("dA0", [128, KC, TC], F32))] * 2
                u_b = [em(nc.sbuf_tensor("u0", [128, KC, TC], F32))] * 2
                h_b = [em(nc.sbuf_tensor(f"h{i}", [128, KC, TC], F32))
                       for i in range(2)]
                t_b = [em(nc.sbuf_tensor("t0", [128, KC, TC], F32))] * 2
                Bb = [em(nc.sbuf_tensor(f"Bb{i}", [128, TC], F32))
                      for i in range(2)]
                Cb = [em(nc.sbuf_tensor(f"Cb{i}", [128, TC], F32))
                      for i in range(2)]
                hcarry = em(nc.sbuf_tensor("hcarry", [128, KC, NST], F32))
                BCs = em(nc.sbuf_tensor("BCs", [16, 512], F32))
                stat = em(nc.sbuf_tensor("stat", [1, 512], F32))
                t1b = [em(nc.sbuf_tensor(f"t1b{i}", [128, TC], F32))
                       for i in range(2)]
                statb = em(nc.sbuf_tensor("statb", [128, 512], F32))
                t2b = [em(nc.sbuf_tensor(f"t2b{i}", [128, TC], F32))
                       for i in range(2)]
                ipwT_r = em(nc.sbuf_tensor("ipwT_r", [128, KC, 2 * D], BF16))
                dtwT_r = em(nc.sbuf_tensor("dtwT_r", [128, KC, D], BF16))
                owT_r = em(nc.sbuf_tensor("owT_r", [128, KC, D], BF16))

                mark(SY.dma_start(ipwT_r[:],
                                  ipwT_d.rearrange("(c p) o -> p c o", p=128)),
                     "dw0", 16)
                mark(SY.dma_start(dtwT_r[:],
                                  dtwT_d.rearrange("(c p) o -> p c o", p=128)),
                     "dw0", 16)
                d_wres = mark(SY.dma_start(
                    owT_r[:], owT_d.rearrange("(c p) o -> p c o", p=128)),
                    "dw0", 16)
                dwait(PE, "dw0", d_wres)
                VE.memset(hcarry[:], 0.0)
                VE.memset(x_ssm[:, :, 0:3], 0.0)
                v_init = mark(VE.engine_nop(), "ve")
                wait(AC, "ve", v_init)
                wait(PO, "ve", v_init)
                wait(PE, "ve", v_init)

                v_carry = {}
                state = {"v_x2": None, "p_tail": None}

                def mamba_chunk(ch):
                    t0 = ch * TC
                    if state["v_x2"] is not None:
                        wait(PE, "ve", state["v_x2"])
                    v_es = {}
                    for tt in range(2):
                        wait(SY, "pe")
                        d_x = mark(SY.dma_start(
                            xstage[tt][:],
                            x_in[t0 + tt * 128:t0 + (tt + 1) * 128, :]),
                            "dx", 16)
                        dwait(PE, "dx", d_x)
                        for kc2 in range(KC):
                            j = tt * 8 + kc2
                            bank = 6 + (j % 2)
                            if j >= 2:
                                wait(PE, "ve", v_es[j - 2])
                            tr(pb[bank][:, 0:128],
                                         xstage[tt][:, kc2 * 128:(kc2 + 1) * 128])
                            t_tr = pe_mark()
                            wait(VE, "pe", t_tr)
                            VE.tensor_copy(x_fm[:, kc2, tt * 128:(tt + 1) * 128],
                                           pb[bank][:, 0:128])
                            v_es[j] = mark(VE.engine_nop(), "ve")
                    v_xfm = mark(VE.engine_nop(), "ve")
                    wait(SY, "ve", v_xfm)
                    wait(AC, "ve", v_xfm)
                    if state.get("t_op") is not None:
                        wait(AC, "pe", state["t_op"])
                    for kc2 in range(KC):
                        AC.activation(x_bf[:, kc2, :], x_fm[:, kc2, :], AF.Identity)
                    a_xbf = ac_mark()

                    wait(PE, "ac", a_xbf)
                    for ot in range(16):
                        bank, colo = ot % 8, (ot // 8) * TC
                        for kc2 in range(KC):
                            PE.matmul(pb[bank][:, colo:colo + TC],
                                      ipwT_r[:, kc2, ot * 128:(ot + 1) * 128],
                                      x_bf[:, kc2, :],
                                      start=(kc2 == 0), stop=(kc2 == KC - 1))
                    t_ip = pe_mark()
                    wait(AC, "pe", t_ip)
                    if state["p_tail"] is not None:
                        wait(AC, "po", state["p_tail"])
                    for ot in range(8):
                        bank, colo = ot % 8, (ot // 8) * TC
                        AC.activation(x_ssm[:, ot, 3:3 + TC],
                                      pb[bank][:, colo:colo + TC],
                                      AF.Identity, bias=ipb_sb[:, ot:ot + 1])
                    p_g = {}
                    for ot in range(8, 16):
                        bank, colo = ot % 8, (ot // 8) * TC
                        r = ot % 2
                        if ot >= 10:
                            wait(AC, "po", p_g[ot - 2])
                        AC.activation(vg[r][:], pb[bank][:, colo:colo + TC],
                                      AF.Identity, bias=ipb_sb[:, ot:ot + 1])
                        AC.activation(tg[r][:], vg[r][:], AF.Tanh,
                                      scale=half_sb[:])
                        a_g = ac_mark()
                        wait(PO, "ac", a_g)
                        PO.tensor_scalar(tg[r][:], tg[r][:], 1.0, 0.5,
                                         ALU.add, ALU.mult)
                        PO.tensor_mul(gate_sb[:, ot - 8, :], tg[r][:], vg[r][:])
                        p_g[ot] = mark(PO.engine_nop(), "po")
                    wait(AC, "po", p_g[15])
                    a_ssm = ac_mark()

                    wait(PO, "ac", a_ssm)
                    p_c, a_c, p_x = {}, {}, {}

                    def rec_conv_tail(k):
                        r2 = k % 2
                        wait(PO, "ac", a_c[k])
                        PO.tensor_scalar(tg[r2][:], tg[r2][:], 1.0, 0.5,
                                         ALU.add, ALU.mult)
                        PO.tensor_mul(xconv[:, k, :], tg[r2][:], vg[r2][:])
                        p_x[k] = mark(PO.engine_nop(), "po")
                        wait(AC, "po", p_x[k])
                        AC.activation(xconv_bf[:, k, :], xconv[:, k, :],
                                      AF.Identity)

                    for kc2 in range(KC):
                        r = kc2 % 2
                        if kc2 >= 2:
                            wait(PO, "ac", a_c[kc2 - 2])
                        PO.tensor_scalar(vg[r][:], x_ssm[:, kc2, 0:TC],
                                         cw_sb[:, kc2, 0:1], None, ALU.mult)
                        for kk in range(1, 4):
                            PO.tensor_scalar(convt[:], x_ssm[:, kc2, kk:kk + TC],
                                             cw_sb[:, kc2, kk:kk + 1], None,
                                             ALU.mult)
                            PO.tensor_add(vg[r][:], vg[r][:], convt[:])
                        p_c[kc2] = mark(PO.engine_nop(), "po")
                        wait(AC, "po", p_c[kc2])
                        AC.activation(vg[r][:], vg[r][:], AF.Identity,
                                      bias=cb_sb[:, kc2:kc2 + 1])
                        AC.activation(tg[r][:], vg[r][:], AF.Tanh,
                                      scale=half_sb[:])
                        a_c[kc2] = ac_mark()
                        if kc2 >= 1:
                            rec_conv_tail(kc2 - 1)
                    rec_conv_tail(KC - 1)
                    PO.tensor_copy(x_ssm[:, :, 0:3], x_ssm[:, :, TC:TC + 3])
                    state["p_tail"] = mark(PO.engine_nop(), "po")
                    a_xc = ac_mark()

                    wait(PE, "ac", a_xc)
                    for kc2 in range(KC):
                        PE.matmul(pb[0][0:16, 0:TC], bcwT_sb[:, kc2, 0:16],
                                  xconv_bf[:, kc2, :],
                                  start=(kc2 == 0), stop=(kc2 == KC - 1))
                    for kc2 in range(KC):
                        PE.matmul(pb[0][0:16, TC:2 * TC], bcwT_sb[:, kc2, 16:32],
                                  xconv_bf[:, kc2, :],
                                  start=(kc2 == 0), stop=(kc2 == KC - 1))
                    t_bc = pe_mark()
                    wait(AC, "pe", t_bc)
                    AC.activation(BCs[:, 0:TC], pb[0][0:16, 0:TC], AF.Identity,
                                  bias=bcbB_sb[:])
                    AC.activation(BCs[:, TC:2 * TC], pb[0][0:16, TC:2 * TC],
                                  AF.Identity, bias=bcbC_sb[:])
                    a_bc = ac_mark()
                    wait(SY, "ac", a_bc)
                    mark(SY.dma_start(bc_d[0:16, t0:t0 + TC], BCs[:, 0:TC]),
                         "dbs", 16)
                    d_bcs = mark(SY.dma_start(bc_d[16:32, t0:t0 + TC],
                                              BCs[:, TC:2 * TC]), "dbs", 16)

                    for ot in range(8):
                        bank, colo = 1 + ot % 4, (ot // 4) * TC
                        for kc2 in range(KC):
                            PE.matmul(pb[bank][:, colo:colo + TC],
                                      dtwT_r[:, kc2, ot * 128:(ot + 1) * 128],
                                      xconv_bf[:, kc2, :],
                                      start=(kc2 == 0), stop=(kc2 == KC - 1))
                    t_dt = pe_mark()
                    wait(AC, "pe", t_dt)
                    for ot in range(8):
                        bank, colo = 1 + ot % 4, (ot // 4) * TC
                        AC.activation(vabuf[:, ot, :],
                                      pb[bank][:, colo:colo + TC],
                                      AF.Identity, bias=dtb_sb[:, ot:ot + 1])
                    a_dtv = ac_mark()
                    AC.activation(rbuf[:].rearrange("p c t -> p (c t)"),
                                  vabuf[:].rearrange("p c t -> p (c t)"),
                                  AF.Relu)
                    a_dt1 = ac_mark()
                    wait(VE, "ac", a_dt1)
                    VE.scalar_tensor_tensor(vabuf[:], vabuf[:], -1.0, vabuf[:],
                                            ALU.mult, ALU.max)
                    v_va = mark(VE.engine_nop(), "ve")
                    wait(AC, "ve", v_va)
                    AC.activation(vabuf[:].rearrange("p c t -> p (c t)"),
                                  vabuf[:].rearrange("p c t -> p (c t)"),
                                  AF.Exp, scale=neg1_sb[:])
                    AC.activation(vabuf[:].rearrange("p c t -> p (c t)"),
                                  vabuf[:].rearrange("p c t -> p (c t)"),
                                  AF.Ln, bias=one_sb[:])
                    a_dt2 = ac_mark()
                    wait(VE, "ac", a_dt2)
                    VE.tensor_add(dt_sb[:], rbuf[:], vabuf[:])
                    v_dt = mark(VE.engine_nop(), "ve")

                    wait(PO, "ve", v_dt)
                    PO.tensor_mul(w_all, dt_sb[:], xconv[:])
                    PO.tensor_tensor(
                        y_all[:], xconv[:],
                        dvec_sb[:].unsqueeze(2).broadcast_to((128, KC, TC)),
                        ALU.mult)
                    p_w = mark(PO.engine_nop(), "po")

                    dAp = [dA_b[0], rbuf]
                    up = [u_b[0], vabuf]
                    tp = [t_b[0], xconv]
                    dwait(SY, "dbs", d_bcs)
                    v_scan, a_dA, v_ym = {}, {}, {}
                    for n in range(NST):
                        r = n % 2
                        sm = "dbc0" if r == 0 else "dbc1"
                        if n >= 2:
                            wait(SY, "ve", v_scan[n - 2])
                            wait(SY, "ve", v_ym[n - 2])
                        d_b = mark(SY.dma_start(
                            Bb[r][:], bc_d[n:n + 1, t0:t0 + TC]
                            .partition_broadcast(128).squeeze(1)), sm, 16)
                        d_c = mark(SY.dma_start(
                            Cb[r][:], bc_d[16 + n:17 + n, t0:t0 + TC]
                            .partition_broadcast(128).squeeze(1)), sm, 16)
                        if n <= 1:
                            wait(AC, "ve", v_dt)
                        if n >= 2:
                            wait(AC, "ve", v_scan[n - 2])
                        for c in range(KC):
                            AC.activation(dAp[r][:, c, :], dt_sb[:, c, :],
                                          AF.Exp, scale=A_sb[:, c, n:n + 1])
                        a_dA[n] = ac_mark()
                        if n == 0:
                            wait(VE, "po", p_w)
                        dwait(VE, sm, d_c)
                        VE.tensor_tensor(
                            up[r][:], w_all,
                            Bb[r][:].unsqueeze(1).broadcast_to((128, KC, TC)),
                            ALU.mult)
                        bubble()
                        wait(VE, "ac", a_dA[n])
                        for c in range(KC):
                            VE.tensor_tensor_scan(h_b[r][:, c, :],
                                                  dAp[r][:, c, :],
                                                  up[r][:, c, :],
                                                  hcarry[:, c, n:n + 1],
                                                  ALU.mult, ALU.add)
                        v_scan[n] = mark(VE.engine_nop(), "ve")
                        bubble()
                        VE.tensor_tensor(
                            tp[r][:], h_b[r][:],
                            Cb[r][:].unsqueeze(1).broadcast_to((128, KC, TC)),
                            ALU.mult)
                        v_ym[n] = mark(VE.engine_nop(), "ve")
                        VE.tensor_copy(hcarry[:, :, n:n + 1],
                                       h_b[r][:, :, TC - 1:TC])
                        VE.tensor_add(y_all[:], y_all[:], tp[r][:])

                    bubble()
                    VE.tensor_mul(ygbf[:], y_all[:], gate_sb[:])
                    v_yg = mark(VE.engine_nop(), "ve")

                    wait(PE, "ve", v_yg)
                    wait(PE, "ac", a_dt2)
                    for ot in range(8):
                        bank, colo = 1 + ot % 4, (ot // 4) * TC
                        for kc2 in range(KC):
                            PE.matmul(pb[bank][:, colo:colo + TC],
                                      owT_r[:, kc2, ot * 128:(ot + 1) * 128],
                                      ygbf[:, kc2, :],
                                      start=(kc2 == 0), stop=(kc2 == KC - 1))
                    t_op = pe_mark()
                    state["t_op"] = t_op
                    wait(VE, "pe", t_op)
                    for ot in range(8):
                        bank, colo = 1 + ot % 4, (ot // 4) * TC
                        VE.scalar_tensor_tensor(z_sb[:, ot, :],
                                                pb[bank][:, colo:colo + TC],
                                                ob_sb[:, ot:ot + 1],
                                                x_fm[:, ot, :],
                                                ALU.add, ALU.add)
                    v_z = mark(VE.engine_nop(), "ve")

                    wait(AC, "ve", v_z)
                    AC.activation(sq_sb[:], z_sb[:], AF.Square)
                    a_sq = ac_mark()
                    wait(PE, "ac", a_sq)
                    for kc2 in range(KC):
                        PE.matmul(pb[5][0:2, 0:TC], ones_sb[:, 0:2],
                                  z_sb[:, kc2, :],
                                  start=(kc2 == 0), stop=(kc2 == KC - 1))
                    for kc2 in range(KC):
                        PE.matmul(pb[5][0:2, TC:2 * TC], ones_sb[:, 0:2],
                                  sq_sb[:, kc2, :],
                                  start=(kc2 == 0), stop=(kc2 == KC - 1))
                    t_s1 = pe_mark()

                    def stats(t_sums):
                        wait(VE, "pe", t_sums)
                        VE.tensor_scalar(stat[0:1, 0:TC], pb[5][0:1, 0:TC],
                                         INV_D, None, ALU.mult)
                        VE.tensor_scalar(stat[0:1, TC:2 * TC],
                                         pb[5][0:1, TC:2 * TC],
                                         INV_D, None, ALU.mult)
                        VE.tensor_mul(t1b[0][0:1, 0:TC], stat[0:1, 0:TC],
                                      stat[0:1, 0:TC])
                        bubble()
                        VE.tensor_sub(stat[0:1, TC:2 * TC],
                                      stat[0:1, TC:2 * TC], t1b[0][0:1, 0:TC])
                        v_var = mark(VE.engine_nop(), "ve")
                        wait(AC, "ve", v_var)
                        AC.activation(stat[0:1, TC:2 * TC],
                                      stat[0:1, TC:2 * TC], AF.Ln,
                                      bias=eps_sb[0:1, :])
                        AC.activation(stat[0:1, TC:2 * TC],
                                      stat[0:1, TC:2 * TC], AF.Exp,
                                      scale=negh_sb[0:1, :])
                        a_st = ac_mark()
                        wait(SY, "ac", a_st)
                        wait(SY, "ve")
                        d_w = mark(SY.dma_start(stat_d[0:1, 0:512], stat[:]),
                                   "dst", 16)
                        dwait(SY, "dst", d_w)
                        d_r = mark(SY.dma_start(
                            statb[:], stat_d[0:1, 0:512]
                            .partition_broadcast(128).squeeze(1)), "dst", 16)
                        dwait(VE, "dst", d_r)
                        return d_r

                    stats(t_s1)
                    for kc2 in range(0, KC, 2):
                        for q in range(2):
                            VE.tensor_sub(t1b[q][:], z_sb[:, kc2 + q, :],
                                          statb[:, 0:TC])
                        for q in range(2):
                            VE.tensor_mul(t2b[q][:], t1b[q][:],
                                          statb[:, TC:2 * TC])
                        for q in range(2):
                            VE.tensor_scalar(t2b[q][:], t2b[q][:],
                                             lng_sb[:, kc2 + q:kc2 + q + 1],
                                             None, ALU.mult)
                        for q in range(2):
                            VE.tensor_add(t2b[q][:], t2b[q][:],
                                          x_fm[:, kc2 + q, :])
                        for q in range(2):
                            VE.tensor_scalar(zz_sb[:, kc2 + q, :], t2b[q][:],
                                             lnb_sb[:, kc2 + q:kc2 + q + 1],
                                             None, ALU.add)
                    v_zz = mark(VE.engine_nop(), "ve")

                    wait(AC, "ve", v_zz)
                    AC.activation(sq_sb[:], zz_sb[:], AF.Square)
                    a_sq2 = ac_mark()
                    wait(PE, "ac", a_sq2)
                    for kc2 in range(KC):
                        PE.matmul(pb[5][0:2, 0:TC], ones_sb[:, 0:2],
                                  zz_sb[:, kc2, :],
                                  start=(kc2 == 0), stop=(kc2 == KC - 1))
                    for kc2 in range(KC):
                        PE.matmul(pb[5][0:2, TC:2 * TC], ones_sb[:, 0:2],
                                  sq_sb[:, kc2, :],
                                  start=(kc2 == 0), stop=(kc2 == KC - 1))
                    t_s2 = pe_mark()
                    stats(t_s2)
                    for kc2 in range(0, KC, 2):
                        wait(VE, "dx2")
                        for q in range(2):
                            VE.tensor_sub(t1b[q][:], zz_sb[:, kc2 + q, :],
                                          statb[:, 0:TC])
                        for q in range(2):
                            VE.tensor_mul(t2b[q][:], t1b[q][:],
                                          statb[:, TC:2 * TC])
                        v_x2c = mark(VE.engine_nop(), "ve")
                        wait(SY, "ve", v_x2c)
                        for q in range(2):
                            mark(SY.dma_start(
                                x2_d[(kc2 + q) * 128:(kc2 + q + 1) * 128,
                                     t0:t0 + TC], t2b[q][:]), "dx2", 16)
                        wait(SY, "dx2")
                    state["v_x2"] = mark(VE.engine_nop(), "ve")

                for ch in range(NCH):
                    mamba_chunk(ch)
                barrier()

            t_x2 = dout("x2", [D, L])
            if t_x2 is not None:
                mark(SY.dma_start(t_x2, x2_d), "st", 16)
                SY.wait_ge(sems["st"][0], sems["st"][1])

            if "P4" in phset:
              with ExitStack() as f:
                ef = f.enter_context
                y_acc = ef(nc.sbuf_tensor("y_acc", [128, KC, L], F32))
                sqf = ef(nc.sbuf_tensor("sqf", [128, 2, 512], F32))
                statf = ef(nc.sbuf_tensor("statf", [1, 2 * L], F32))
                murd_tok = ef(nc.sbuf_tensor("murd_tok", [128, 16], F32))
                rstd_tok = ef(nc.sbuf_tensor("rstd_tok", [128, 16], F32))
                lg_sb = ef(nc.sbuf_tensor("lg_sb", [128, 8], F32))
                lg2_sb = ef(nc.sbuf_tensor("lg2_sb", [128, 8], F32))
                mx_sb = ef(nc.sbuf_tensor("mx_sb", [128, 1], F32))
                ex_sb = ef(nc.sbuf_tensor("ex_sb", [128, 8], F32))
                sm_sb = ef(nc.sbuf_tensor("sm_sb", [128, 1], F32))
                rs_sb = ef(nc.sbuf_tensor("rs_sb", [128, 1], F32))
                p_colt = ef(nc.sbuf_tensor("p_colt", [128, 16], F32))

                wait(SY, "dx2")
                d_x2 = mark(SY.dma_start(
                    y_acc[:], x2_d.rearrange("(c p) t -> p c t", p=128)),
                    "d4a", 16)
                dwait(AC, "d4a", d_x2)
                dwait(VE, "d4a", d_x2)
                dwait(PE, "d4a", d_x2)
                a_x2ld = ac_mark()
                wait(VE, "ac", a_x2ld)
                wait(PE, "ac", a_x2ld)

                for tb in range(4):
                    ts0 = tb * 512
                    sqmk = {}
                    for kc2 in range(KC):
                        r = kc2 % 2
                        if kc2 >= 2:
                            wait(AC, "pe", sqmk[kc2 - 2])
                        AC.activation(sqf[:, r, :], y_acc[:, kc2, ts0:ts0 + 512],
                                      AF.Square)
                        a_q = ac_mark()
                        wait(PE, "ac", a_q)
                        PE.matmul(pb[0][0:2, 0:512], ones_sb[:, 0:2],
                                  y_acc[:, kc2, ts0:ts0 + 512],
                                  start=(kc2 == 0), stop=(kc2 == KC - 1))
                        PE.matmul(pb[1][0:2, 0:512], ones_sb[:, 0:2],
                                  sqf[:, r, :],
                                  start=(kc2 == 0), stop=(kc2 == KC - 1))
                        sqmk[kc2] = pe_mark()
                    t_fs = sqmk[KC - 1]
                    wait(VE, "pe", t_fs)
                    VE.tensor_scalar(statf[0:1, ts0:ts0 + 512],
                                     pb[0][0:1, 0:512], INV_D, None, ALU.mult)
                    VE.tensor_scalar(statf[0:1, L + ts0:L + ts0 + 512],
                                     pb[1][0:1, 0:512], INV_D, None, ALU.mult)
                    v_fs = mark(VE.engine_nop(), "ve")
                    wait(PE, "ve", v_fs)
                stf = ef(nc.sbuf_tensor("stf", [1, L], F32))
                VE.tensor_mul(stf[0:1, :], statf[0:1, 0:L], statf[0:1, 0:L])
                bubble()
                VE.tensor_sub(statf[0:1, L:2 * L], statf[0:1, L:2 * L],
                              stf[0:1, :])
                v_vf = mark(VE.engine_nop(), "ve")
                wait(AC, "ve", v_vf)
                AC.activation(statf[0:1, L:2 * L], statf[0:1, L:2 * L], AF.Ln,
                              bias=eps_sb[0:1, :])
                AC.activation(statf[0:1, L:2 * L], statf[0:1, L:2 * L], AF.Exp,
                              scale=negh_sb[0:1, :])
                a_rf = ac_mark()
                wait(VE, "ac", a_rf)
                VE.tensor_mul(statf[0:1, 0:L], statf[0:1, 0:L],
                              statf[0:1, L:2 * L])
                bubble()
                VE.tensor_scalar(statf[0:1, 0:L], statf[0:1, 0:L], -1.0,
                                 None, ALU.mult)
                v_murd = mark(VE.engine_nop(), "ve")
                wait(PE, "ve", v_murd)
                for tt in range(16):
                    bank = 2 + (tt % 2)
                    tr(pb[bank][0:128, 0:1],
                                 statf[0:1, tt * 128:(tt + 1) * 128])
                    tr(pb[bank][0:128, 1:2],
                                 statf[0:1, L + tt * 128:L + (tt + 1) * 128])
                    t_tt = pe_mark()
                    wait(VE, "pe", t_tt)
                    VE.tensor_copy(murd_tok[:, tt:tt + 1], pb[bank][:, 0:1])
                    VE.tensor_copy(rstd_tok[:, tt:tt + 1], pb[bank][:, 1:2])
                    v_tt = mark(VE.engine_nop(), "ve")
                    wait(PE, "ve", v_tt)
                if "lgt" in dbg:
                    lgt_sb = ef(nc.sbuf_tensor("lgt_sb", [128, 128], F32))
                xrt = [ef(nc.sbuf_tensor(f"xrt{i}", [128, KC, 128], BF16))
                       for i in range(2)]
                for tt in range(16):
                    rr = tt % 2
                    if tt >= 2:
                        wait(AC, "pe")
                    for kc2 in range(KC):
                        AC.activation(xrt[rr][:, kc2, :],
                                      y_acc[:, kc2, tt * 128:(tt + 1) * 128],
                                      AF.Identity)
                    a_xr = ac_mark()
                    wait(PE, "ac", a_xr)
                    for kc2 in range(KC):
                        PE.matmul(pb[4][:, 0:8],
                                  xrt[rr][:, kc2, :],
                                  wrg_sb[:, kc2, :],
                                  start=(kc2 == 0), stop=(kc2 == KC - 1))
                    t_lg = pe_mark()
                    wait(VE, "pe", t_lg)
                    VE.tensor_copy(lg_sb[:], pb[4][:, 0:8])
                    bubble()
                    VE.tensor_scalar(lg_sb[:], lg_sb[:], rstd_tok[:, tt:tt + 1],
                                     None, ALU.mult)
                    bubble()
                    bubble()
                    VE.scalar_tensor_tensor(lg2_sb[:], ceb_sb[:],
                                            murd_tok[:, tt:tt + 1], lg_sb[:],
                                            ALU.mult, ALU.add)
                    bubble()
                    VE.tensor_add(lg2_sb[:], lg2_sb[:], bewb_sb[:])
                    bubble()
                    VE.reduce_max(mx_sb[:], lg2_sb[:], axis=AX.X, negate=True)
                    v_lg = mark(VE.engine_nop(), "ve")
                    wait(AC, "ve", v_lg)
                    AC.activation(ex_sb[:], lg2_sb[:], AF.Exp, bias=mx_sb[:])
                    a_ex = ac_mark()
                    wait(VE, "ac", a_ex)
                    VE.reduce_sum(sm_sb[:], ex_sb[:], axis=AX.X)
                    bubble()
                    VE.reciprocal(rs_sb[:], sm_sb[:])
                    bubble()
                    VE.tensor_mul(p_colt[:, tt:tt + 1], ex_sb[:, 0:1], rs_sb[:])
                    if "lgt" in dbg:
                        VE.tensor_copy(lgt_sb[:, tt * 8:tt * 8 + 8], lg2_sb[:])
                    v_p = mark(VE.engine_nop(), "ve")
                    wait(PE, "ve", v_p)
                prow = ef(nc.sbuf_tensor("prow", [1, L], F32))
                p0b = ef(nc.sbuf_tensor("p0b", [128, L], F32))
                p1b = ef(nc.sbuf_tensor("p1b", [128, L], F32))
                for tt in range(16):
                    bank = 2 + (tt % 2)
                    tr(pb[bank][0:1, 0:128], p_colt[:, tt:tt + 1])
                    t_pt = pe_mark()
                    wait(VE, "pe", t_pt)
                    VE.tensor_copy(prow[0:1, tt * 128:(tt + 1) * 128],
                                   pb[bank][0:1, 0:128])
                    v_pr = mark(VE.engine_nop(), "ve")
                    wait(PE, "ve", v_pr)
                v_prow = mark(VE.engine_nop(), "ve")
                wait(SY, "ve", v_prow)
                d_pw = mark(SY.dma_start(prow_d, prow[:]), "dst", 16)
                dwait(SY, "dst", d_pw)
                d_pr = mark(SY.dma_start(
                    p0b[:], prow_d.partition_broadcast(128).squeeze(1)),
                    "dst", 16)
                dwait(VE, "dst", d_pr)
                VE.tensor_scalar(p1b[:], p0b[:], -1.0, 1.0, ALU.mult, ALU.add)
                v_p1 = mark(VE.engine_nop(), "ve")

                fex = ExitStack()
                efx = fex.enter_context
                act_sb = efx(nc.sbuf_tensor("act_sb", [128, 32, FT], BF16))
                x2bft = efx(nc.sbuf_tensor("x2bft", [128, KC, FT], BF16))
                w1sl = [efx(nc.sbuf_tensor(f"w1sl{i}", [128, KC, 512], BF16))
                        for i in range(2)]
                w2sl = [efx(nc.sbuf_tensor(f"w2sl{i}", [128, 4, D], BF16))
                        for i in range(2)]
                yt_b = [efx(nc.sbuf_tensor(f"ytb{i}", [128, FT], F32))
                        for i in range(2)]
                for tb in range(NFT):
                    ts0 = tb * FT
                    wait(AC, "pe")
                    for kc2 in range(KC):
                        AC.activation(x2bft[:, kc2, :],
                                      y_acc[:, kc2, ts0:ts0 + FT], AF.Identity)
                    a_xt = ac_mark()
                    wait(PE, "ac", a_xt)
                    for ex in range(2):
                        w1src = w1bf_d if ex == 0 else w1pbf_d
                        w2src = w2bf_d if ex == 0 else w2pbf_d
                        b1src = b1_sb if ex == 0 else b1p_sb
                        b2src = b2_sb if ex == 0 else b2p_sb
                        psrc = p0b if ex == 0 else p1b
                        last_gelu = {0: 0, 1: 0}
                        fcpre = {}
                        for fc in range(8):
                            r = fc % 2
                            sm = "d4a" if r == 0 else "d4b"
                            if fc >= 2:
                                wait(SY, "pe", fcpre[fc - 2])
                            d_w1 = mark(SY.dma_start(
                                w1sl[r][:],
                                w1src[:, fc * 512:(fc + 1) * 512].rearrange(
                                    "(c p) f -> p c f", p=128)), sm, 16)
                            dwait(PE, sm, d_w1)
                            for ftl in range(4):
                                bank = ftl % 2
                                wait(PE, "ac", last_gelu[bank])
                                for kc2 in range(KC):
                                    PE.matmul(
                                        pb[bank][:, 0:FT],
                                        w1sl[r][:, kc2, ftl * 128:(ftl + 1) * 128],
                                        x2bft[:, kc2, :],
                                        start=(kc2 == 0), stop=(kc2 == KC - 1))
                                t_pre = pe_mark()
                                fcpre[fc] = t_pre
                                wait(AC, "pe", t_pre)
                                fcol = fc * 4 + ftl
                                AC.activation(act_sb[:, fcol, :],
                                              pb[bank][:, 0:FT],
                                              AF.Gelu_apprx_tanh,
                                              bias=b1src[:, fcol:fcol + 1])
                                last_gelu[bank] = ac_mark()
                        wait(PE, "ac", max(last_gelu.values()))
                        fcy = {}
                        for fc in range(8):
                            r = fc % 2
                            sm = "d4a" if r == 0 else "d4b"
                            if fc >= 2:
                                wait(SY, "pe", fcy[fc - 2])
                            d_w2 = mark(SY.dma_start(
                                w2sl[r][:],
                                w2src[fc * 512:(fc + 1) * 512, :].rearrange(
                                    "(q p) h -> p q h", p=128)), sm, 16)
                            dwait(PE, sm, d_w2)
                            for ht in range(8):
                                for fi in range(4):
                                    PE.matmul(
                                        pb[ht][:, 0:FT],
                                        w2sl[r][:, fi, ht * 128:(ht + 1) * 128],
                                        act_sb[:, fc * 4 + fi, :],
                                        start=(fc == 0 and fi == 0),
                                        stop=(fc == 7 and fi == 3))
                            fcy[fc] = pe_mark()
                        t_y = fcy[7]
                        wait(VE, "pe", t_y)
                        for ht in range(0, 8, 2):
                            for q in range(2):
                                VE.scalar_tensor_tensor(
                                    yt_b[q][:], pb[ht + q][:, 0:FT],
                                    b2src[:, ht + q:ht + q + 1],
                                    psrc[:, ts0:ts0 + FT],
                                    ALU.add, ALU.mult)
                            for q in range(2):
                                VE.tensor_add(
                                    y_acc[:, ht + q, ts0:ts0 + FT],
                                    y_acc[:, ht + q, ts0:ts0 + FT],
                                    yt_b[q][:])
                        v_ev = mark(VE.engine_nop(), "ve")
                        wait(PE, "ve", v_ev)

                fex.close()
                t_lgt = dout("lgt", [128, 128])
                if t_lgt is not None:
                    mark(SY.dma_start(t_lgt, lgt_sb[:]), "dst", 16)
                t_pr = dout("prow", [1, L])
                if t_pr is not None:
                    mark(SY.dma_start(t_pr, prow[:]), "dst", 16)
                t_ya = dout("yacc", [D, L])
                if t_ya is not None:
                    quiesce(VE)
                    v_yd = mark(VE.engine_nop(), "ve")
                    wait(SY, "ve", v_yd)
                    mark(SY.dma_start(
                        t_ya.rearrange("(c p) t -> p c t", p=128), y_acc[:]),
                        "dst", 16)
                    dwait(SY, "dst", sems["dst"][1])
                sq5 = sqf
                stat5 = statf
                ost = [ef(nc.sbuf_tensor(f"ost{i}", [128, D], F32))
                       for i in range(2)]
                stat5b = ef(nc.sbuf_tensor("stat5b", [128, 1024], F32))
                t15 = [ef(nc.sbuf_tensor(f"t15{i}", [128, 512], F32))
                       for i in range(2)]
                x3 = ef(nc.sbuf_tensor("x3", [128, KC, 512], F32))
                st_marks = {}
                for tb in range(4):
                    ts0 = tb * 512
                    sqmk = {}
                    for kc2 in range(KC):
                        r = kc2 % 2
                        if kc2 >= 2:
                            wait(AC, "pe", sqmk[kc2 - 2])
                        AC.activation(sq5[:, r, :], y_acc[:, kc2, ts0:ts0 + 512],
                                      AF.Square)
                        a_q = ac_mark()
                        wait(PE, "ac", a_q)
                        PE.matmul(pb[0][0:2, 0:512], ones_sb[:, 0:2],
                                  y_acc[:, kc2, ts0:ts0 + 512],
                                  start=(kc2 == 0), stop=(kc2 == KC - 1))
                        PE.matmul(pb[1][0:2, 0:512], ones_sb[:, 0:2],
                                  sq5[:, r, :],
                                  start=(kc2 == 0), stop=(kc2 == KC - 1))
                        sqmk[kc2] = pe_mark()
                    t_s5 = sqmk[KC - 1]
                    wait(VE, "pe", t_s5)
                    VE.tensor_scalar(stat5[0:1, 0:512], pb[0][0:1, 0:512],
                                     INV_D, None, ALU.mult)
                    VE.tensor_scalar(stat5[0:1, 512:1024], pb[1][0:1, 0:512],
                                     INV_D, None, ALU.mult)
                    VE.tensor_mul(sq5[0:1, 0, :], stat5[0:1, 0:512],
                                  stat5[0:1, 0:512])
                    bubble()
                    VE.tensor_sub(stat5[0:1, 512:1024], stat5[0:1, 512:1024],
                                  sq5[0:1, 0, :])
                    v_v5 = mark(VE.engine_nop(), "ve")
                    wait(AC, "ve", v_v5)
                    AC.activation(stat5[0:1, 512:1024], stat5[0:1, 512:1024],
                                  AF.Ln, bias=eps_sb[0:1, :])
                    AC.activation(stat5[0:1, 512:1024], stat5[0:1, 512:1024],
                                  AF.Exp, scale=negh_sb[0:1, :])
                    a_r5 = ac_mark()
                    wait(SY, "ac", a_r5)
                    wait(SY, "ve")
                    d_w5 = mark(SY.dma_start(stat_d[0:1, 0:1024],
                                             stat5[0:1, 0:1024]), "dst", 16)
                    dwait(SY, "dst", d_w5)
                    d_r5 = mark(SY.dma_start(
                        stat5b[:], stat_d[0:1, 0:1024]
                        .partition_broadcast(128).squeeze(1)), "dst", 16)
                    dwait(VE, "dst", d_r5)
                    for kc2 in range(0, KC, 2):
                        for q in range(2):
                            VE.tensor_sub(t15[q][:],
                                          y_acc[:, kc2 + q, ts0:ts0 + 512],
                                          stat5b[:, 0:512])
                        for q in range(2):
                            VE.tensor_mul(x3[:, kc2 + q, :], t15[q][:],
                                          stat5b[:, 512:1024])
                    v_x3 = mark(VE.engine_nop(), "ve")
                    wait(PE, "ve", v_x3)
                    for tt in range(4):
                        g = tt % 2
                        if (tb, tt) in st_marks:
                            pass
                        prev = st_marks.get(g)
                        for kc2 in range(KC):
                            bank = 4 + (kc2 % 2)
                            tr(pb[bank][:, 0:128],
                                         x3[:, kc2, tt * 128:(tt + 1) * 128])
                            t_t5 = pe_mark()
                            wait(VE, "pe", t_t5)
                            if prev is not None and kc2 == 0:
                                wait(VE, "st")
                            VE.tensor_copy(ost[g][:, kc2 * 128:(kc2 + 1) * 128],
                                           pb[bank][:, 0:128])
                            v_t5 = mark(VE.engine_nop(), "ve")
                            wait(PE, "ve", v_t5)
                        v_o = mark(VE.engine_nop(), "ve")
                        wait(SY, "ve", v_o)
                        st_marks[g] = mark(SY.dma_start(
                            out_d[ts0 + tt * 128:ts0 + (tt + 1) * 128, :],
                            ost[g][:]), "st", 16)
                SY.wait_ge(sems["st"][0], sems["st"][1])

            quiesce(SY)
            PO.wait_ge(sems["dpo"][0], sems["dpo"][1])
            SY.wait_ge(sems["st"][0], sems["st"][1])

        _body(None)

        def _replay(stream):
            def f(_eng):
                for fn, a, k, h in stream:
                    inst = fn(*a, **k)
                    for sem, n in h.incs:
                        inst.then_inc(sem, n)
            return f

        block.sync(_replay(streams["sy"]))
        block.vector(_replay(streams["ve"]))
        block.scalar(_replay(streams["ac"]))
        block.tensor(_replay(streams["pe"]))
        block.gpsimd(_replay(streams["po"]))

    return nc, dbg_outs


def _shard_inputs(inputs):
    x = np.ascontiguousarray(np.asarray(inputs["L0_attn_x"], dtype=np.float32))
    conv_w = np.ascontiguousarray(
        np.asarray(inputs["L0_attn_conv_w"], dtype=np.float32)[:, 0, :])

    def f32(k):
        return np.ascontiguousarray(np.asarray(inputs[k], dtype=np.float32))

    common = {
        "ipw": f32("L0_attn_in_proj_w"), "ipb": f32("L0_attn_in_proj_b"),
        "cw": conv_w, "cb": f32("L0_attn_conv_b"),
        "bcw": f32("L0_attn_bc_w"), "bcb": f32("L0_attn_bc_b"),
        "dtw": f32("L0_attn_dt_w"), "dtb": f32("L0_attn_dt_b"),
        "alog": f32("L0_attn_A_log"), "dvec": f32("L0_attn_D"),
        "ow": f32("L0_attn_out_w"), "ob": f32("L0_attn_out_b"),
        "lng": f32("L0_attn_ln_g"), "lnb": f32("L0_attn_ln_b"),
        "w1": f32("L0_ffn_W1"), "b1": f32("L0_ffn_b1"),
        "w2": f32("L0_ffn_W2"), "b2": f32("L0_ffn_b2"),
        "g2": f32("L0_ffn_ln_gamma"), "be2": f32("L0_ffn_ln_beta"),
        "wr": f32("L0_ffn_Wr"), "br": f32("L0_ffn_br"),
    }
    in_maps = []
    for c in range(8):
        mm = dict(common)
        mm["x"] = np.ascontiguousarray(x[c])
        in_maps.append(mm)
    return in_maps


def kernel(**inputs):
    in_maps = _shard_inputs(inputs)
    nc, _ = build()
    res = run_bass_kernel_spmd(nc, in_maps, core_ids=list(range(8)))
    out = np.stack([np.asarray(res.results[c]["out"]) for c in range(8)], axis=0)
    return np.ascontiguousarray(out.astype(np.float32))


# version 06
# speedup vs baseline: 3.2063x; 1.1094x over previous
import numpy as np
from contextlib import ExitStack

import concourse.bass as bass
import concourse.mybir as mybir
from concourse.bass_utils import run_bass_kernel_spmd

F32 = mybir.dt.float32
BF16 = mybir.dt.bfloat16
AF = mybir.ActivationFunctionType
ALU = mybir.AluOpType
AX = mybir.AxisListType

D = 1024
KC = 8
L = 2048
TC = 256
NCH = L // TC
NST = 16
FF = 4096
FT = 512
NFT = L // FT
EPS = 1e-5
INV_D = 1.0 / D

SAMPLE_ITERS = 50
FULL_ITERS = 22


class _IH:

    __slots__ = ("incs", "pe_val")

    def __init__(self):
        self.incs = []
        self.pe_val = None

    def then_inc(self, sem, n):
        self.incs.append((sem, n))
        return self


class _Rec:

    def __init__(self, real, stream):
        self._real = real
        self._stream = stream

    def __getattr__(self, op):
        fn = getattr(self._real, op)

        def call(*a, **k):
            h = _IH()
            self._stream.append((fn, a, k, h))
            return h

        return call


def _sel_k(n):
    return max(1, int(n * 0.05))


def build(debug=(), phases=("P0", "ce2", "ce4", "P0w", "P0bar", "P1", "P2", "P3", "P4", "P5")):
    nc = bass.Bass("TRN2", target_bir_lowering=False, detect_race_conditions=False)
    dbg = set(debug)
    phset = set(phases)

    def din(name, shape, dt=F32):
        return nc.dram_tensor(name, shape, dt, kind="ExternalInput").ap()

    x_in = din("x", [L, D])
    ipw = din("ipw", [2 * D, D])
    ipb = din("ipb", [2 * D])
    cw = din("cw", [D, 4])
    cb = din("cb", [D])
    bcw = din("bcw", [2 * NST, D])
    bcb = din("bcb", [2 * NST])
    dtw = din("dtw", [D, D])
    dtb = din("dtb", [D])
    alog = din("alog", [D, NST])
    dvec = din("dvec", [D])
    ow = din("ow", [D, D])
    ob_ = din("ob", [D])
    lng = din("lng", [D])
    lnb = din("lnb", [D])
    w1 = din("w1", [D, FF])
    b1 = din("b1", [FF])
    w2 = din("w2", [FF, D])
    b2 = din("b2", [D])
    g2 = din("g2", [D])
    be2 = din("be2", [D])
    wr = din("wr", [D, 8])
    br = din("br", [8])

    out_d = nc.dram_tensor("out", [L, D], F32, kind="ExternalOutput").ap()

    ipwT_d = nc.dram_tensor("ipwT_d", [D, 2 * D], BF16, kind="Internal").ap()
    dtwT_d = nc.dram_tensor("dtwT_d", [D, D], BF16, kind="Internal").ap()
    owT_d = nc.dram_tensor("owT_d", [D, D], BF16, kind="Internal").ap()
    w1bf_d = nc.dram_tensor("w1bf_d", [D, FF], BF16, kind="Internal").ap()
    w1pbf_d = nc.dram_tensor("w1pbf_d", [D, FF], BF16, kind="Internal").ap()
    w2bf_d = nc.dram_tensor("w2bf_d", [FF, D], BF16, kind="Internal").ap()
    w2pbf_d = nc.dram_tensor("w2pbf_d", [FF, D], BF16, kind="Internal").ap()
    bc_d = nc.dram_tensor("bc_d", [2 * NST, L], F32, kind="Internal").ap()
    x2_d = nc.dram_tensor("x2_d", [D, L], F32, kind="Internal").ap()
    stat_d = nc.dram_tensor("stat_d", [1, 1024], F32, kind="Internal").ap()
    prow_d = nc.dram_tensor("prow_d", [1, L], F32, kind="Internal").ap()

    dbg_outs = {}
    tr_d = None

    def dout(name, shape, dt=F32):
        if name in dbg:
            dbg_outs[name] = nc.dram_tensor("dbg_" + name, shape, dt,
                                            kind="ExternalOutput").ap()
            return dbg_outs[name]
        return None

    with ExitStack() as ctx:
        e = ctx.enter_context

        sems = {}
        for s in ("pe", "ve", "po", "ac", "sy", "dpo", "st",
                  "d0", "dsl", "dt0", "dt1", "d1", "d2", "d2a", "d2b",
                  "dw0", "dw1", "dbs", "dbc0", "dbc1", "dx", "dx2", "dst",
                  "d4a", "d4b"):
            sems[s] = [e(nc.semaphore("s_" + s)), 0]

        def mark(inst, s, n=1):
            sems[s][1] += n
            inst.then_inc(sems[s][0], n)
            return sems[s][1]

        def wait(eng, s, v=None):
            if s[0] == "d" or s == "st":
                assert v is None or v == sems[s][1], (s, v, sems[s][1])
                v = None
            eng.wait_ge(sems[s][0], sems[s][1] if v is None else v)

        def dwait(eng, s, v):
            assert v == sems[s][1], (s, v, sems[s][1])
            eng.wait_ge(sems[s][0], v)

        def quiesce(eng):
            for s in ("d0", "dsl", "dt0", "dt1", "d1", "d2", "d2a", "d2b",
                      "dw0", "dw1", "dbs", "dbc0", "dbc1", "dx", "dx2",
                      "dst", "d4a", "d4b", "st"):
                if sems[s][1]:
                    eng.wait_ge(sems[s][0], sems[s][1])

        streams = {k: [] for k in ("pe", "ve", "po", "ac", "sy")}
        PE = _Rec(nc.tensor, streams["pe"])
        VE = _Rec(nc.vector, streams["ve"])
        PO = _Rec(nc.gpsimd, streams["po"])
        AC = _Rec(nc.scalar, streams["ac"])
        SY = _Rec(nc.sync, streams["sy"])

        def tr(out_ap, in_ap):
            p = in_ap.shape[0]
            PE.transpose(out_ap, in_ap, ident[0:p, 0:p])
        ENG = {"pe": PE, "ve": VE, "po": PO, "ac": AC, "sy": SY}

        def barrier():
            quiesce(ENG["sy"])
            if sems["dpo"][1]:
                ENG["po"].wait_ge(sems["dpo"][0], sems["dpo"][1])
            for nm, eng in ENG.items():
                if nm == "ac":
                    ac_mark()
                elif nm == "pe":
                    if streams["pe"] and streams["pe"][-1][0].__name__ in (
                            "matmul", "transpose"):
                        pe_mark()
                    else:
                        mark(eng.nop(), nm)
                elif nm in ("ve", "po"):
                    mark(eng.engine_nop(), nm)
                else:
                    mark(eng.nop(), nm)
            vals = {nm: sems[nm][1] for nm in ENG}
            for nm, eng in ENG.items():
                for nm2 in ENG:
                    if nm2 != nm:
                        wait(eng, nm2, vals[nm2])

        A_sb = e(nc.sbuf_tensor("A_sb", [128, KC, NST], F32))
        cw_sb = e(nc.sbuf_tensor("cw_sb", [128, KC, 4], F32))
        cb_sb = e(nc.sbuf_tensor("cb_sb", [128, KC], F32))
        dvec_sb = e(nc.sbuf_tensor("dvec_sb", [128, KC], F32))
        lng_sb = e(nc.sbuf_tensor("lng_sb", [128, KC], F32))
        lnb_sb = e(nc.sbuf_tensor("lnb_sb", [128, KC], F32))
        g2_sb = e(nc.sbuf_tensor("g2_sb", [128, KC], F32))
        be2_sb = e(nc.sbuf_tensor("be2_sb", [128, KC], F32))
        ipb_sb = e(nc.sbuf_tensor("ipb_sb", [128, 16], F32))
        dtb_sb = e(nc.sbuf_tensor("dtb_sb", [128, KC], F32))
        ob_sb = e(nc.sbuf_tensor("ob_sb", [128, KC], F32))
        bcbB_sb = e(nc.sbuf_tensor("bcbB_sb", [16, 1], F32))
        bcbC_sb = e(nc.sbuf_tensor("bcbC_sb", [16, 1], F32))
        bcb_sb = e(nc.sbuf_tensor("bcb_sb", [32, 1], F32))
        b1_sb = e(nc.sbuf_tensor("b1_sb", [128, 32], F32))
        b1p_sb = e(nc.sbuf_tensor("b1p_sb", [128, 32], F32))
        b2_sb = e(nc.sbuf_tensor("b2_sb", [128, KC], F32))
        b2p_sb = e(nc.sbuf_tensor("b2p_sb", [128, KC], F32))
        bcwT_sb = e(nc.sbuf_tensor("bcwT_sb", [128, KC, 32], BF16))
        wrg_sb = e(nc.sbuf_tensor("wrg_sb", [128, KC, 8], BF16))
        ceb_sb = e(nc.sbuf_tensor("ceb_sb", [128, 8], F32))
        bewb_sb = e(nc.sbuf_tensor("bewb_sb", [128, 8], F32))
        ones_sb = e(nc.sbuf_tensor("ones_sb", [128, 128], F32))
        onesr_sb = e(nc.sbuf_tensor("onesr_sb", [1, 128], F32))
        ident = e(nc.sbuf_tensor("ident", [128, 128], F32))
        half_sb = e(nc.sbuf_tensor("half_sb", [128, 1], F32))
        neg1_sb = e(nc.sbuf_tensor("neg1_sb", [128, 1], F32))
        negh_sb = e(nc.sbuf_tensor("negh_sb", [128, 1], F32))
        one_sb = e(nc.sbuf_tensor("one_sb", [128, 1], F32))
        eps_sb = e(nc.sbuf_tensor("eps_sb", [128, 1], F32))
        thr_sb = e(nc.sbuf_tensor("thr_sb", [128, 8], F32))
        dumA = e(nc.sbuf_tensor("dumA", [128, 4], F32))
        dumB = e(nc.sbuf_tensor("dumB", [128, 4], F32))
        acd1 = e(nc.sbuf_tensor("acd1", [128, 1], F32))
        acd2 = e(nc.sbuf_tensor("acd2", [128, 1], F32))
        vstage = e(nc.sbuf_tensor("vstage", [32, 128], F32))
        b1a_sb = e(nc.sbuf_tensor("b1a_sb", [128, 32], F32))
        b2a_sb = e(nc.sbuf_tensor("b2a_sb", [128, KC], F32))

        pb = [e(nc.psum_tensor(f"pb{i}", [128, 512], F32)) for i in range(8)]

        def bubble():
            VE.tensor_copy(dumA[:], dumB[:])

        def ac_mark(s="ac"):
            return mark(AC.activation(acd1[:], acd2[:], AF.Identity), s)

        def pe_mark(s="pe"):
            fn, a, k, h = streams["pe"][-1]
            assert fn.__name__ in ("matmul", "transpose"), fn.__name__
            if getattr(h, "pe_val", None) is not None:
                return h.pe_val
            sems[s][1] += 1
            h.incs.append((sems[s][0], 1))
            h.pe_val = sems[s][1]
            return sems[s][1]

        p1tr = nc.dram_tensor("dbg_p1trace", [SAMPLE_ITERS + 2 * FULL_ITERS, 24],
                              F32, kind="ExternalOutput").ap()             if "p1trace" in dbg else None
        if p1tr is not None:
            dbg_outs["p1trace"] = p1tr
        tr_i = [0]
        block = e(nc.Block())

        def _body(sync):
            VE.memset(ones_sb[:], 1.0)
            VE.memset(onesr_sb[:], 1.0)
            VE.memset(half_sb[:], 0.5)
            VE.memset(neg1_sb[:], -1.0)
            VE.memset(negh_sb[:], -0.5)
            VE.memset(one_sb[:], 1.0)
            VE.memset(eps_sb[:], EPS)
            VE.memset(dumB[:], 0.0)
            VE.memset(dumA[:], 0.0)
            VE.memset(acd1[:], 0.0)
            VE.memset(acd2[:], 0.0)
            v_const = mark(VE.engine_nop(), "ve")
            PO.memset(ident[:], 0.0)
            PO.affine_select(out=ident[:], in_=ident[:],
                             compare_op=ALU.not_equal, fill=1.0,
                             base=0, pattern=[[-1, 128]], channel_multiplier=1)
            p_ident = mark(PO.engine_nop(), "po")
            wait(PE, "po", p_ident)
            wait(PE, "ve", v_const)

            smalls = [] if "noSmalls" in phset else [
                      (ipb, ipb_sb, 16), (cb, cb_sb, 8), (dtb, dtb_sb, 8),
                      (ob_, ob_sb, 8), (dvec, dvec_sb, 8), (lng, lng_sb, 8),
                      (lnb, lnb_sb, 8), (g2, g2_sb, 8), (be2, be2_sb, 8),
                      (b2, b2_sb, 8), (b1, b1_sb, 32)]
            for si, (src, dst, w) in enumerate(smalls):
                for hw0 in range(0, w, 16):
                    hw = min(16, w - hw0)
                    d_ld = mark(SY.dma_start(
                        vstage[0:hw, :],
                        src[hw0 * 128:(hw0 + hw) * 128]
                        .rearrange("(a p) -> a p", p=128)), "d0", 16)
                    dwait(PE, "d0", d_ld)
                    tr(pb[0][:, 0:hw], vstage[0:hw, :])
                    t_v = pe_mark()
                    wait(VE, "pe", t_v)
                    VE.tensor_copy(dst[:, hw0:hw0 + hw], pb[0][:, 0:hw])
                    v_done = mark(VE.engine_nop(), "ve")
                    wait(PE, "ve", v_done)
                    wait(SY, "ve", v_done)

            if "noBcb" not in phset:
              mark(SY.dma_start(bcb_sb[:], bcb.unsqueeze(1)), "d0", 16)
              wait(PO, "d0")
              mark(PO.dma_start(bcbB_sb[:], bcb_sb[0:16, :]), "dpo", 16)
              mark(PO.dma_start(bcbC_sb[:], bcb_sb[16:32, :]), "dpo", 16)
              mark(SY.dma_start(cw_sb[:],
                                cw.rearrange("(c p) k -> p c k", p=128)), "d0", 16)
            if "noA" not in phset:
              d_alog = mark(SY.dma_start(
                  A_sb[:], alog.rearrange("(c p) n -> p c n", p=128)), "d0", 16)
              dwait(AC, "d0", d_alog)
              AC.activation(A_sb[:], A_sb[:], AF.Exp)
              AC.activation(A_sb[:], A_sb[:], AF.Identity, scale=neg1_sb[:])
              a_A = ac_mark()

            if "noCe" not in phset:
              with ExitStack() as p0ctx:
                e0 = p0ctx.enter_context
                wr_sb = e0(nc.sbuf_tensor("wr_sb", [128, KC, 8], F32))
                br_sb = e0(nc.sbuf_tensor("br_sb", [1, 8], F32))
                cerow = e0(nc.sbuf_tensor("cerow", [1, 16], F32))
                bcw_stage = e0(nc.sbuf_tensor("bcw_stage", [16, D], F32))
                bcw_stage2 = e0(nc.sbuf_tensor("bcw_stage2", [16, D], F32))

                d_wr = mark(SY.dma_start(
                    wr_sb[:], wr.rearrange("(c p) e -> p c e", p=128)), "d0", 16)
                d_br = mark(SY.dma_start(br_sb[:], br.unsqueeze(0)),
                            "d0", 16)
                dwait(VE, "d0", d_br)
                for c in range(KC):
                    VE.tensor_scalar(wrg_sb[:, c, :], wr_sb[:, c, :],
                                     g2_sb[:, c:c + 1], None, ALU.mult)
                v_wrg = mark(VE.engine_nop(), "ve")
                if "ce2" in phset or "ceAll" in phset:
                    gtmp = e0(nc.sbuf_tensor("gtmp", [128, KC, 8], F32))
                    btmp = e0(nc.sbuf_tensor("btmp", [128, KC, 8], F32))
                    cat = e0(nc.sbuf_tensor("cat", [128, 16], F32))
                    VE.tensor_tensor(gtmp[:], wr_sb[:],
                                     g2_sb[:].unsqueeze(2)
                                     .broadcast_to((128, KC, 8)), ALU.mult)
                    VE.tensor_tensor(btmp[:], wr_sb[:],
                                     be2_sb[:].unsqueeze(2)
                                     .broadcast_to((128, KC, 8)), ALU.mult)
                    bubble()
                    VE.reduce_sum(cat[:, 0:8], gtmp[:].transpose([0, 2, 1]),
                                  axis=AX.X)
                    VE.reduce_sum(cat[:, 8:16], btmp[:].transpose([0, 2, 1]),
                                  axis=AX.X)
                    v_rep = mark(VE.engine_nop(), "ve")
                    wait(PE, "ve", v_rep)
                    PE.matmul(pb[1][0:2, 0:16], ones_sb[:, 0:2], cat[:],
                              start=True, stop=True)
                    t_cb = pe_mark()
                    wait(VE, "pe", t_cb)
                    pass
                    VE.tensor_copy(cerow[0:1, 0:8], pb[1][0:1, 0:8])
                    VE.tensor_add(cerow[0:1, 8:16], pb[1][0:1, 8:16], br_sb[:])
                    v_ce = mark(VE.engine_nop(), "ve")
                    wait(SY, "ve", v_ce)
                    d_cw = mark(SY.dma_start(stat_d[0:1, 0:16], cerow[:]),
                                "dst", 16)
                    dwait(SY, "dst", d_cw)
                    mark(SY.dma_start(ceb_sb[:], stat_d[0:1, 0:8]
                                      .partition_broadcast(128).squeeze(1)),
                         "dst", 16)
                    d_ceb = mark(SY.dma_start(
                        bewb_sb[:], stat_d[0:1, 8:16]
                        .partition_broadcast(128).squeeze(1)), "dst", 16)
                    dwait(VE, "dst", d_ceb)
                    v_ceb = mark(VE.engine_nop(), "ve")
                if "ce4" in phset or "ceAll" in phset:
                    mark(SY.dma_start(bcw_stage[:], bcw[0:16, :]), "d0", 16)
                    d_bcw = mark(SY.dma_start(bcw_stage2[:], bcw[16:32, :]),
                                 "d0", 16)
                    dwait(PE, "d0", d_bcw)
                    wait(PE, "ve")
                    evs = {}
                    for c in range(KC):
                        bank = 2 + (c % 2)
                        if c >= 2:
                            wait(PE, "ve", evs[c - 2])
                        tr(pb[bank][:, 0:16],
                           bcw_stage[:, c * 128:(c + 1) * 128])
                        tr(pb[bank][:, 16:32],
                           bcw_stage2[:, c * 128:(c + 1) * 128])
                        t_t = pe_mark()
                        wait(VE, "pe", t_t)
                        VE.tensor_copy(bcwT_sb[:, c, :], pb[bank][:, 0:32])
                        evs[c] = mark(VE.engine_nop(), "ve")
                tsl = [e0(nc.sbuf_tensor(f"tsl{i}", [128, D], F32))
                       for i in range(2)]
                ttb = [e0(nc.sbuf_tensor(f"ttb{i}", [128, 128], BF16))
                       for i in range(2)]

                def transpose_weight(src_, dst_d, O):
                    n_o = O // 128
                    slab_rel = {}
                    tile_rel = {}
                    tile_st = {}
                    ti = 0
                    for obk in range(n_o):
                        r = obk % 2
                        if obk >= 2:
                            wait(SY, "ve", slab_rel[obk - 2])
                        d_sl = mark(SY.dma_start(
                            tsl[r][:], src_[obk * 128:(obk + 1) * 128, :]),
                            "dsl", 16)
                        dwait(PE, "dsl", d_sl)
                        for ib in range(KC):
                            trr = ti % 2
                            bank = 4 + trr
                            if ti >= 2:
                                wait(PE, "ve", tile_rel[ti - 2])
                            tr(pb[bank][:, 0:128],
                               tsl[r][:, ib * 128:(ib + 1) * 128])
                            t_tr = pe_mark()
                            wait(VE, "pe", t_tr)
                            if ti >= 2:
                                dwait(VE, "dt%d" % (ti % 2), tile_st[ti - 2])
                            VE.tensor_copy(ttb[trr][:], pb[bank][:, 0:128])
                            v_ev = mark(VE.engine_nop(), "ve")
                            tile_rel[ti] = v_ev
                            if ib == KC - 1:
                                slab_rel[obk] = v_ev
                            wait(SY, "ve", v_ev)
                            tile_st[ti] = mark(SY.dma_start(
                                dst_d[ib * 128:(ib + 1) * 128,
                                      obk * 128:(obk + 1) * 128],
                                ttb[trr][:]), "dt%d" % (ti % 2), 16)
                            ti += 1

                if "P0w" in phset:
                    transpose_weight(ipw, ipwT_d, 2 * D)
                    transpose_weight(dtw, dtwT_d, D)
                    transpose_weight(ow, owT_d, D)
                if "P0bar" in phset:
                    barrier()

            quiesce(SY)
            t_cat = dout("cat", [128, 16])
            if t_cat is not None:
                wait(SY, "ve")
                mark(SY.dma_start(t_cat, cat[:]), "st", 16)
            t_cerow = dout("cerow", [1, 16])
            if t_cerow is not None:
                mark(SY.dma_start(t_cerow, cerow[:]), "st", 16)
            t_A = dout("A", [128, KC * NST])
            if t_A is not None:
                mark(SY.dma_start(t_A, A_sb[:].rearrange("p c n -> p (c n)")), "st", 16)
            t_ipb = dout("ipb", [128, 16])
            if t_ipb is not None:
                mark(SY.dma_start(t_ipb, ipb_sb[:]), "st", 16)
            t_ceb = dout("ceb", [128, 8])
            if t_ceb is not None:
                mark(SY.dma_start(t_ceb, ceb_sb[:]), "st", 16)
            t_bewb = dout("bewb", [128, 8])
            if t_bewb is not None:
                mark(SY.dma_start(t_bewb, bewb_sb[:]), "st", 16)
            t_ipwT = dout("ipwT", [D, 2 * D], BF16)
            if t_ipwT is not None:
                mark(SY.dma_start(t_ipwT, ipwT_d), "st", 16)
            t_cwsb = dout("cwsb", [128, KC * 4])
            if t_cwsb is not None:
                mark(SY.dma_start(t_cwsb, cw_sb[:].rearrange("p c k -> p (c k)")), "st", 16)
            SY.wait_ge(sems["st"][0], sems["st"][1])

            nW1, kW1 = D * FF, _sel_k(D * FF)
            nW2, kW2 = FF * D, _sel_k(FF * D)
            nb1, kb1 = FF, _sel_k(FF)
            nb2, kb2 = D, _sel_k(D)

            if "noBabs" not in phset:
              VE.scalar_tensor_tensor(b1a_sb[:], b1_sb[:], -1.0, b1_sb[:],
                                      ALU.mult, ALU.max)
              VE.scalar_tensor_tensor(b2a_sb[:], b2_sb[:], -1.0, b2_sb[:],
                                      ALU.mult, ALU.max)
              a_babs = mark(VE.engine_nop(), "ve")

            if "P1" in phset:
              with ExitStack() as p1ctx:
                e1 = p1ctx.enter_context
                NCOL = 8
                lo_sb = e1(nc.sbuf_tensor("lo_sb", [128, NCOL], F32))
                hi_sb = e1(nc.sbuf_tensor("hi_sb", [128, NCOL], F32))
                mid_sb = e1(nc.sbuf_tensor("mid_sb", [128, NCOL], F32))
                K_sb = e1(nc.sbuf_tensor("K_sb", [128, NCOL], F32))
                cnt_sb = e1(nc.sbuf_tensor("cnt_sb", [128, 4 * NCOL], F32))
                tot_sb = e1(nc.sbuf_tensor("tot_sb", [128, NCOL], F32))
                cond_sb = e1(nc.sbuf_tensor("cond_sb", [128, NCOL], F32))
                sc1 = e1(nc.sbuf_tensor("sc1", [128, NCOL], F32))
                sc2 = e1(nc.sbuf_tensor("sc2", [128, NCOL], F32))
                sc3 = e1(nc.sbuf_tensor("sc3", [128, NCOL], F32))
                mveA = e1(nc.sbuf_tensor("mveA", [128, 8192], F32))
                p1f_sb = None
                if "p1f" in dbg:
                    p1f_sb = e1(nc.sbuf_tensor("p1f_sb", [128, 528], F32))
                w1s_sb = e1(nc.sbuf_tensor("w1s_sb", [128, 1024], F32))
                w2s_sb = e1(nc.sbuf_tensor("w2s_sb", [128, 1024], F32))

                d_s1 = mark(SY.dma_start(
                    w1s_sb[:], w1[0:32, :].rearrange("a f -> (a f)")
                    .rearrange("(p q) -> p q", p=128)), "d1", 16)
                d_s2 = mark(SY.dma_start(w2s_sb[:], w2[0:128, :]), "d1", 16)
                dwait(VE, "d1", d_s2)
                VE.scalar_tensor_tensor(w1s_sb[:], w1s_sb[:], -1.0, w1s_sb[:],
                                        ALU.mult, ALU.max)
                VE.scalar_tensor_tensor(w2s_sb[:], w2s_sb[:], -1.0, w2s_sb[:],
                                        ALU.mult, ALU.max)
                a_sabs = mark(VE.engine_nop(), "ve")

                def upd_chain(c0, c1):
                    VE.tensor_copy(tot_sb[:, c0:c1], pb[7][:, c0:c1])
                    bubble()
                    upd_chain2(c0, c1)

                def upd_chain2(c0, c1):
                    VE.tensor_tensor(cond_sb[:, c0:c1], tot_sb[:, c0:c1],
                                     K_sb[:, c0:c1], ALU.is_ge)
                    VE.tensor_sub(sc1[:, c0:c1], mid_sb[:, c0:c1], hi_sb[:, c0:c1])
                    VE.tensor_sub(sc2[:, c0:c1], mid_sb[:, c0:c1], lo_sb[:, c0:c1])
                    VE.tensor_mul(sc3[:, c0:c1], sc1[:, c0:c1], cond_sb[:, c0:c1])
                    VE.tensor_mul(sc1[:, c0:c1], sc2[:, c0:c1], cond_sb[:, c0:c1])
                    VE.tensor_add(hi_sb[:, c0:c1], hi_sb[:, c0:c1], sc3[:, c0:c1])
                    VE.tensor_sub(lo_sb[:, c0:c1], mid_sb[:, c0:c1], sc1[:, c0:c1])
                    bubble()
                    bubble()
                    VE.tensor_add(sc2[:, c0:c1], lo_sb[:, c0:c1], hi_sb[:, c0:c1])
                    bubble()
                    bubble()
                    VE.tensor_scalar(mid_sb[:, c0:c1], sc2[:, c0:c1], 0.5,
                                     None, ALU.mult)
                    bubble()
                    bubble()

                def count_iter(jobs, c0, c1, v_mid):
                    merge = {}
                    for data, col, sub in jobs:
                        n = data.shape[-1]
                        VE.tensor_scalar(mveA[:, 0:n], data,
                                         mid_sb[:, col:col + 1], None,
                                         ALU.is_le, ALU.add,
                                         accum_out=cnt_sb[:, col * 4 + sub:
                                                          col * 4 + sub + 1])
                        merge.setdefault(col, []).append(col * 4 + sub)
                    bubble()
                    for col in range(c0, c1):
                        subs = merge.get(col, [])
                        if len(subs) == 1:
                            VE.tensor_copy(sc3[:, col:col + 1],
                                           cnt_sb[:, subs[0]:subs[0] + 1])
                        else:
                            VE.tensor_add(sc3[:, col:col + 1],
                                          cnt_sb[:, subs[0]:subs[0] + 1],
                                          cnt_sb[:, subs[1]:subs[1] + 1])
                            for s3 in subs[2:]:
                                bubble()
                                VE.tensor_add(sc3[:, col:col + 1],
                                              sc3[:, col:col + 1],
                                              cnt_sb[:, s3:s3 + 1])
                    v_mrg = mark(VE.engine_nop(), "ve")
                    wait(PE, "ve", v_mrg)
                    PE.matmul(pb[7][:, c0:c1], ones_sb[:], sc3[:, c0:c1],
                              start=True, stop=True)
                    t_tot = pe_mark()
                    wait(VE, "pe", t_tot)
                    upd_chain(c0, c1)
                    vm = mark(VE.engine_nop(), "ve")
                    if p1tr is not None:
                        wait(SY, "ve", vm)
                        mark(SY.dma_start(p1tr[tr_i[0]:tr_i[0] + 1, 0:8],
                                          mid_sb[0:1, :]), "st", 16)
                        mark(SY.dma_start(p1tr[tr_i[0]:tr_i[0] + 1, 8:16],
                                          tot_sb[0:1, :]), "st", 16)
                        mark(SY.dma_start(p1tr[tr_i[0]:tr_i[0] + 1, 16:24],
                                          hi_sb[0:1, :]), "st", 16)
                        wait(SY, "st")
                        tr_i[0] += 1
                    return vm

                ns = 131072
                Ks = [max(1, round(kW1 / nW1 * ns)),
                      max(1, round((nW1 - kW1 + 1) / nW1 * ns)),
                      max(1, round(kW2 / nW2 * ns)),
                      max(1, round((nW2 - kW2 + 1) / nW2 * ns)),
                      kb1, nb1 - kb1 + 1, kb2, nb2 - kb2 + 1]
                for col in range(8):
                    VE.memset(lo_sb[:, col:col + 1], -1.0)
                    VE.memset(hi_sb[:, col:col + 1], 65536.0)
                    VE.memset(K_sb[:, col:col + 1], float(Ks[col]))
                    VE.memset(mid_sb[:, col:col + 1], 0.5 * (65536.0 - 1.0))
                v_mid = mark(VE.engine_nop(), "ve")
                for it in range(SAMPLE_ITERS):
                    jobs = [(w1s_sb[:], 0, 0), (w1s_sb[:], 1, 0),
                            (w2s_sb[:], 2, 0), (w2s_sb[:], 3, 0),
                            (b1a_sb[:], 4, 0), (b1a_sb[:], 5, 0),
                            (b2a_sb[:], 6, 0), (b2a_sb[:], 7, 0)]
                    v_mid = count_iter(jobs, 0, 8, v_mid)
                VE.tensor_copy(thr_sb[:, 4:8], hi_sb[:, 4:8])
                VE.tensor_scalar(lo_sb[:, 0:4], hi_sb[:, 0:4], 0.9, None, ALU.mult)
                VE.tensor_scalar(hi_sb[:, 0:4], hi_sb[:, 0:4], 1.1, None, ALU.mult)
                VE.memset(K_sb[:, 0:1], float(kW1))
                VE.memset(K_sb[:, 1:2], float(nW1 - kW1 + 1))
                VE.memset(K_sb[:, 2:3], float(kW2))
                VE.memset(K_sb[:, 3:4], float(nW2 - kW2 + 1))
                bubble()
                VE.tensor_add(sc2[:, 0:4], lo_sb[:, 0:4], hi_sb[:, 0:4])
                bubble()
                VE.tensor_scalar(mid_sb[:, 0:4], sc2[:, 0:4], 0.5, None, ALU.mult)
                v_mid = mark(VE.engine_nop(), "ve")

                for wi, (wsrc, c0) in enumerate(((w1, 0), (w2, 2))):
                    with ExitStack() as selctx:
                        wa = selctx.enter_context(
                            nc.sbuf_tensor(f"wa{wi}", [128, 32768], F32))
                        src_flat = wsrc.rearrange("a f -> (a f)").rearrange(
                            "(p q) -> p q", p=128)
                        for sl in range(4):
                            d_l = mark(SY.dma_start(
                                wa[:, sl * 8192:(sl + 1) * 8192],
                                src_flat[:, sl * 8192:(sl + 1) * 8192]),
                                "d1", 16)
                            dwait(SY, "d1", d_l)
                            dwait(VE, "d1", d_l)
                            VE.scalar_tensor_tensor(
                                wa[:, sl * 8192:(sl + 1) * 8192],
                                wa[:, sl * 8192:(sl + 1) * 8192], -1.0,
                                wa[:, sl * 8192:(sl + 1) * 8192],
                                ALU.mult, ALU.max)
                            bubble()
                        for it in range(FULL_ITERS):
                            jobs = []
                            for col in (c0, c0 + 1):
                                for sb_ in range(4):
                                    jobs.append((wa[:, sb_ * 8192:
                                                    (sb_ + 1) * 8192], col, sb_))
                            v_mid = count_iter(jobs, c0, c0 + 2, v_mid)
                        VE.tensor_copy(thr_sb[:, c0:c0 + 2], hi_sb[:, c0:c0 + 2])
                        v_thr = mark(VE.engine_nop(), "ve")
                        wait(SY, "ve", v_thr)
                        barrier()

                t_p1f = dout("p1f", [128, 528])
                if t_p1f is not None:
                    mark(SY.dma_start(t_p1f, p1f_sb[:]), "st", 16)
                    SY.wait_ge(sems["st"][0], sems["st"][1])

            t_thr = dout("thr", [128, 8])
            if t_thr is not None:
                mark(SY.dma_start(t_thr, thr_sb[:]), "st", 16)
                SY.wait_ge(sems["st"][0], sems["st"][1])

            if "P2" in phset:
              with ExitStack() as p2ctx:
                e2 = p2ctx.enter_context
                SW = 4096
                slabs = [e2(nc.sbuf_tensor(f"pslab{i}", [128, SW], F32))
                         for i in range(2)]
                absb = [e2(nc.sbuf_tensor(f"pabs{i}", [128, SW], F32))
                        for i in range(2)]
                i1b = [e2(nc.sbuf_tensor(f"pi1{i}", [128, SW], F32))
                       for i in range(2)]
                obf = [e2(nc.sbuf_tensor(f"pobf{i}", [128, SW], BF16))
                       for i in range(2)]
                pbf = [e2(nc.sbuf_tensor(f"ppbf{i}", [128, SW], BF16))
                       for i in range(2)]

                def build_p(wsrc, dst_clean, dst_pert, nelem, bot_col, top_col):
                    nsl = nelem // (128 * SW)
                    srcf = wsrc.rearrange("a f -> (a f)").rearrange(
                        "(p q) -> p q", p=128)
                    dstcf = dst_clean.rearrange("a f -> (a f)").rearrange(
                        "(p q) -> p q", p=128)
                    dstpf = dst_pert.rearrange("a f -> (a f)").rearrange(
                        "(p q) -> p q", p=128)
                    rel_po = {}
                    rel_sy = {}
                    for sl in range(nsl):
                        r = sl % 2
                        if sl >= 2:
                            wait(SY, "po", rel_po[sl - 2])
                            dwait(SY, "d2a" if r == 0 else "d2b",
                                  rel_sy[sl - 2])
                        d_l = mark(SY.dma_start(
                            slabs[r][:], srcf[:, sl * SW:(sl + 1) * SW]),
                            "d2", 16)
                        dwait(SY, "d2", d_l)
                        dwait(AC, "d2", d_l)
                        AC.activation(obf[r][:], slabs[r][:], AF.Identity)
                        a_p = ac_mark()
                        dwait(VE, "d2", d_l)
                        VE.scalar_tensor_tensor(absb[r][:], slabs[r][:], -1.0,
                                                slabs[r][:], ALU.mult, ALU.max)
                        bubble()
                        VE.tensor_scalar(i1b[r][:], absb[r][:],
                                         thr_sb[:, bot_col:bot_col + 1], None,
                                         ALU.is_le)
                        bubble()
                        VE.scalar_tensor_tensor(absb[r][:], absb[r][:],
                                                thr_sb[:, top_col:top_col + 1],
                                                i1b[r][:], ALU.is_ge, ALU.subtract)
                        bubble()
                        VE.tensor_scalar(i1b[r][:], absb[r][:], -0.8, 1.0,
                                         ALU.mult, ALU.add)
                        v_m = mark(VE.engine_nop(), "ve")
                        wait(PO, "ve", v_m)
                        PO.tensor_mul(pbf[r][:], slabs[r][:], i1b[r][:])
                        p_m = mark(PO.engine_nop(), "po")
                        rel_po[sl] = p_m
                        wait(SY, "ac", a_p)
                        wait(SY, "po", p_m)
                        sm = "d2a" if r == 0 else "d2b"
                        mark(SY.dma_start(dstcf[:, sl * SW:(sl + 1) * SW],
                                          obf[r][:]), sm, 16)
                        rel_sy[sl] = mark(SY.dma_start(
                            dstpf[:, sl * SW:(sl + 1) * SW], pbf[r][:]),
                            sm, 16)

                build_p(w1, w1bf_d, w1pbf_d, D * FF, 0, 1)
                build_p(w2, w2bf_d, w2pbf_d, FF * D, 2, 3)

                for (asb, vsb, psb, w, bc_, tc_) in (
                        (b1a_sb, b1_sb, b1p_sb, 32, 4, 5),
                        (b2a_sb, b2_sb, b2p_sb, KC, 6, 7)):
                    VE.tensor_scalar(i1b[0][:, 0:w], asb[:, 0:w],
                                     thr_sb[:, bc_:bc_ + 1], None, ALU.is_le)
                    bubble()
                    VE.scalar_tensor_tensor(absb[0][:, 0:w], asb[:, 0:w],
                                            thr_sb[:, tc_:tc_ + 1],
                                            i1b[0][:, 0:w], ALU.is_ge,
                                            ALU.subtract)
                    bubble()
                    VE.tensor_scalar(absb[0][:, 0:w], absb[0][:, 0:w],
                                     -0.8, 1.0, ALU.mult, ALU.add)
                    bubble()
                    VE.tensor_mul(psb[:, 0:w], vsb[:, 0:w], absb[0][:, 0:w])
                v_bp = mark(VE.engine_nop(), "ve")
                barrier()

            t_b1p = dout("b1p", [128, 32])
            if t_b1p is not None:
                mark(SY.dma_start(t_b1p, b1p_sb[:]), "st", 16)
                SY.wait_ge(sems["st"][0], sems["st"][1])

            if "P3" in phset:
              with ExitStack() as m:
                em = m.enter_context
                xstage = [em(nc.sbuf_tensor("xstage0", [128, D], F32))] * 2
                x_fm = em(nc.sbuf_tensor("x_fm", [128, KC, TC], F32))
                x_bf = em(nc.sbuf_tensor("x_bf", [128, KC, TC], BF16))
                x_ssm = em(nc.sbuf_tensor("x_ssm", [128, KC, TC + 3], F32))
                gate_sb = em(nc.sbuf_tensor("gate_sb", [128, KC, TC], F32))
                xconv = em(nc.sbuf_tensor("xconv", [128, KC, TC], F32))
                xconv_bf = em(nc.sbuf_tensor("xconv_bf", [128, KC, TC], BF16))
                dt_sb = em(nc.sbuf_tensor("dt_sb", [128, KC, TC], F32))
                y_all = em(nc.sbuf_tensor("y_all", [128, KC, TC], F32))
                rbuf = em(nc.sbuf_tensor("rbuf", [128, KC, TC], F32))
                vabuf = em(nc.sbuf_tensor("vabuf", [128, KC, TC], F32))
                zz_sb = rbuf
                w_all = x_ssm[:, :, 3:3 + TC]
                z_sb = x_ssm[:, :, 3:3 + TC]
                sq_sb = vabuf
                ygbf = x_bf
                vg = [em(nc.sbuf_tensor(f"vg{i}", [128, TC], F32))
                      for i in range(4)]
                tg = [em(nc.sbuf_tensor(f"tg{i}", [128, TC], F32))
                      for i in range(4)]
                dA_b = [em(nc.sbuf_tensor("dA0", [128, KC, TC], F32))] * 2
                u_b = [em(nc.sbuf_tensor("u0", [128, KC, TC], F32))] * 2
                h_b = [em(nc.sbuf_tensor(f"h{i}", [128, KC, TC], F32))
                       for i in range(2)]
                t_b = [em(nc.sbuf_tensor("t0", [128, KC, TC], F32))] * 2
                Bb = [em(nc.sbuf_tensor(f"Bb{i}", [128, TC], F32))
                      for i in range(2)]
                Cb = [em(nc.sbuf_tensor(f"Cb{i}", [128, TC], F32))
                      for i in range(2)]
                hcarry = em(nc.sbuf_tensor("hcarry", [128, KC, NST], F32))
                BCs = em(nc.sbuf_tensor("BCs", [16, 512], F32))
                stat = em(nc.sbuf_tensor("stat", [1, 512], F32))
                t1b = [em(nc.sbuf_tensor(f"t1b{i}", [128, TC], F32))
                       for i in range(2)]
                statb = em(nc.sbuf_tensor("statb", [128, 512], F32))
                t2b = [em(nc.sbuf_tensor(f"t2b{i}", [128, TC], F32))
                       for i in range(2)]
                ipwT_r = em(nc.sbuf_tensor("ipwT_r", [128, KC, 2 * D], BF16))
                dtwT_r = em(nc.sbuf_tensor("dtwT_r", [128, KC, D], BF16))
                owT_r = em(nc.sbuf_tensor("owT_r", [128, KC, D], BF16))

                mark(SY.dma_start(ipwT_r[:],
                                  ipwT_d.rearrange("(c p) o -> p c o", p=128)),
                     "dw0", 16)
                mark(SY.dma_start(dtwT_r[:],
                                  dtwT_d.rearrange("(c p) o -> p c o", p=128)),
                     "dw0", 16)
                d_wres = mark(SY.dma_start(
                    owT_r[:], owT_d.rearrange("(c p) o -> p c o", p=128)),
                    "dw0", 16)
                dwait(PE, "dw0", d_wres)
                VE.memset(hcarry[:], 0.0)
                VE.memset(x_ssm[:, :, 0:3], 0.0)
                v_init = mark(VE.engine_nop(), "ve")
                wait(AC, "ve", v_init)
                wait(PO, "ve", v_init)
                wait(PE, "ve", v_init)

                v_carry = {}
                state = {"v_x2": None, "p_tail": None}

                def mamba_chunk(ch):
                    t0 = ch * TC
                    if state["v_x2"] is not None:
                        wait(PE, "ve", state["v_x2"])
                    v_es = {}
                    for tt in range(2):
                        wait(SY, "pe")
                        d_x = mark(SY.dma_start(
                            xstage[tt][:],
                            x_in[t0 + tt * 128:t0 + (tt + 1) * 128, :]),
                            "dx", 16)
                        dwait(PE, "dx", d_x)
                        for kc2 in range(KC):
                            j = tt * 8 + kc2
                            bank = 6 + (j % 2)
                            if j >= 2:
                                wait(PE, "ve", v_es[j - 2])
                            tr(pb[bank][:, 0:128],
                                         xstage[tt][:, kc2 * 128:(kc2 + 1) * 128])
                            t_tr = pe_mark()
                            wait(VE, "pe", t_tr)
                            VE.tensor_copy(x_fm[:, kc2, tt * 128:(tt + 1) * 128],
                                           pb[bank][:, 0:128])
                            v_es[j] = mark(VE.engine_nop(), "ve")
                    v_xfm = mark(VE.engine_nop(), "ve")
                    wait(SY, "ve", v_xfm)
                    wait(AC, "ve", v_xfm)
                    if state.get("t_op") is not None:
                        wait(AC, "pe", state["t_op"])
                    for kc2 in range(KC):
                        AC.activation(x_bf[:, kc2, :], x_fm[:, kc2, :], AF.Identity)
                    a_xbf = ac_mark()

                    wait(PE, "ac", a_xbf)
                    for ot in range(16):
                        bank, colo = ot % 8, (ot // 8) * TC
                        for kc2 in range(KC):
                            PE.matmul(pb[bank][:, colo:colo + TC],
                                      ipwT_r[:, kc2, ot * 128:(ot + 1) * 128],
                                      x_bf[:, kc2, :],
                                      start=(kc2 == 0), stop=(kc2 == KC - 1))
                    t_ip = pe_mark()
                    wait(AC, "pe", t_ip)
                    if state["p_tail"] is not None:
                        wait(AC, "po", state["p_tail"])
                    for ot in range(8):
                        bank, colo = ot % 8, (ot // 8) * TC
                        AC.activation(x_ssm[:, ot, 3:3 + TC],
                                      pb[bank][:, colo:colo + TC],
                                      AF.Identity, bias=ipb_sb[:, ot:ot + 1])
                    p_g = {}
                    for ot in range(8, 16):
                        bank, colo = ot % 8, (ot // 8) * TC
                        r = ot % 2
                        if ot >= 10:
                            wait(AC, "po", p_g[ot - 2])
                        AC.activation(vg[r][:], pb[bank][:, colo:colo + TC],
                                      AF.Identity, bias=ipb_sb[:, ot:ot + 1])
                        AC.activation(tg[r][:], vg[r][:], AF.Tanh,
                                      scale=half_sb[:])
                        a_g = ac_mark()
                        wait(PO, "ac", a_g)
                        PO.tensor_scalar(tg[r][:], tg[r][:], 1.0, 0.5,
                                         ALU.add, ALU.mult)
                        PO.tensor_mul(gate_sb[:, ot - 8, :], tg[r][:], vg[r][:])
                        p_g[ot] = mark(PO.engine_nop(), "po")
                    wait(AC, "po", p_g[15])
                    a_ssm = ac_mark()

                    wait(VE, "ac", a_ssm)
                    wait(VE, "po", p_g[15])
                    v_c, a_c, v_x = {}, {}, {}

                    def rec_conv_tail(p):
                        la, lb = 2 * (p % 2), 2 * (p % 2) + 1
                        k0, k1 = 2 * p, 2 * p + 1
                        wait(VE, "ac", a_c[p])
                        VE.tensor_scalar(tg[la][:], tg[la][:], 1.0, 0.5,
                                         ALU.add, ALU.mult)
                        VE.tensor_scalar(tg[lb][:], tg[lb][:], 1.0, 0.5,
                                         ALU.add, ALU.mult)
                        VE.tensor_mul(xconv[:, k0, :], tg[la][:], vg[la][:])
                        VE.tensor_mul(xconv[:, k1, :], tg[lb][:], vg[lb][:])
                        v_x[p] = mark(VE.engine_nop(), "ve")
                        wait(AC, "ve", v_x[p])
                        AC.activation(xconv_bf[:, k0, :], xconv[:, k0, :],
                                      AF.Identity)
                        AC.activation(xconv_bf[:, k1, :], xconv[:, k1, :],
                                      AF.Identity)

                    for p in range(4):
                        la, lb = 2 * (p % 2), 2 * (p % 2) + 1
                        k0, k1 = 2 * p, 2 * p + 1
                        if p >= 2:
                            wait(VE, "ac", a_c[p - 2])
                        VE.tensor_scalar(vg[la][:], x_ssm[:, k0, 0:TC],
                                         cw_sb[:, k0, 0:1], None, ALU.mult)
                        VE.tensor_scalar(vg[lb][:], x_ssm[:, k1, 0:TC],
                                         cw_sb[:, k1, 0:1], None, ALU.mult)
                        for kk in range(1, 4):
                            VE.scalar_tensor_tensor(
                                vg[la][:], x_ssm[:, k0, kk:kk + TC],
                                cw_sb[:, k0, kk:kk + 1], vg[la][:],
                                ALU.mult, ALU.add)
                            VE.scalar_tensor_tensor(
                                vg[lb][:], x_ssm[:, k1, kk:kk + TC],
                                cw_sb[:, k1, kk:kk + 1], vg[lb][:],
                                ALU.mult, ALU.add)
                        v_c[p] = mark(VE.engine_nop(), "ve")
                        if p >= 1:
                            rec_conv_tail(p - 1)
                        wait(AC, "ve", v_c[p])
                        AC.activation(vg[la][:], vg[la][:], AF.Identity,
                                      bias=cb_sb[:, k0:k0 + 1])
                        AC.activation(tg[la][:], vg[la][:], AF.Tanh,
                                      scale=half_sb[:])
                        AC.activation(vg[lb][:], vg[lb][:], AF.Identity,
                                      bias=cb_sb[:, k1:k1 + 1])
                        AC.activation(tg[lb][:], vg[lb][:], AF.Tanh,
                                      scale=half_sb[:])
                        a_c[p] = ac_mark()
                    rec_conv_tail(3)
                    wait(PO, "ve", v_c[3])
                    PO.tensor_copy(x_ssm[:, :, 0:3], x_ssm[:, :, TC:TC + 3])
                    state["p_tail"] = mark(PO.engine_nop(), "po")
                    a_xc = ac_mark()

                    wait(PE, "ac", a_xc)
                    for kc2 in range(KC):
                        PE.matmul(pb[0][0:16, 0:TC], bcwT_sb[:, kc2, 0:16],
                                  xconv_bf[:, kc2, :],
                                  start=(kc2 == 0), stop=(kc2 == KC - 1))
                    for kc2 in range(KC):
                        PE.matmul(pb[0][0:16, TC:2 * TC], bcwT_sb[:, kc2, 16:32],
                                  xconv_bf[:, kc2, :],
                                  start=(kc2 == 0), stop=(kc2 == KC - 1))
                    t_bc = pe_mark()
                    wait(AC, "pe", t_bc)
                    AC.activation(BCs[:, 0:TC], pb[0][0:16, 0:TC], AF.Identity,
                                  bias=bcbB_sb[:])
                    AC.activation(BCs[:, TC:2 * TC], pb[0][0:16, TC:2 * TC],
                                  AF.Identity, bias=bcbC_sb[:])
                    a_bc = ac_mark()
                    wait(SY, "ac", a_bc)
                    mark(SY.dma_start(bc_d[0:16, t0:t0 + TC], BCs[:, 0:TC]),
                         "dbs", 16)
                    d_bcs = mark(SY.dma_start(bc_d[16:32, t0:t0 + TC],
                                              BCs[:, TC:2 * TC]), "dbs", 16)

                    for ot in range(8):
                        bank, colo = 1 + ot % 4, (ot // 4) * TC
                        for kc2 in range(KC):
                            PE.matmul(pb[bank][:, colo:colo + TC],
                                      dtwT_r[:, kc2, ot * 128:(ot + 1) * 128],
                                      xconv_bf[:, kc2, :],
                                      start=(kc2 == 0), stop=(kc2 == KC - 1))
                    t_dt = pe_mark()
                    wait(AC, "pe", t_dt)
                    for ot in range(8):
                        bank, colo = 1 + ot % 4, (ot // 4) * TC
                        AC.activation(vabuf[:, ot, :],
                                      pb[bank][:, colo:colo + TC],
                                      AF.Identity, bias=dtb_sb[:, ot:ot + 1])
                    a_dtv = ac_mark()
                    AC.activation(rbuf[:].rearrange("p c t -> p (c t)"),
                                  vabuf[:].rearrange("p c t -> p (c t)"),
                                  AF.Relu)
                    a_dt1 = ac_mark()
                    wait(VE, "ac", a_dt1)
                    VE.scalar_tensor_tensor(vabuf[:], vabuf[:], -1.0, vabuf[:],
                                            ALU.mult, ALU.max)
                    v_va = mark(VE.engine_nop(), "ve")
                    wait(AC, "ve", v_va)
                    AC.activation(vabuf[:].rearrange("p c t -> p (c t)"),
                                  vabuf[:].rearrange("p c t -> p (c t)"),
                                  AF.Exp, scale=neg1_sb[:])
                    AC.activation(vabuf[:].rearrange("p c t -> p (c t)"),
                                  vabuf[:].rearrange("p c t -> p (c t)"),
                                  AF.Ln, bias=one_sb[:])
                    a_dt2 = ac_mark()
                    wait(VE, "ac", a_dt2)
                    VE.tensor_add(dt_sb[:], rbuf[:], vabuf[:])
                    v_dt = mark(VE.engine_nop(), "ve")

                    wait(PO, "ve", v_dt)
                    PO.tensor_mul(w_all, dt_sb[:], xconv[:])
                    PO.tensor_tensor(
                        y_all[:], xconv[:],
                        dvec_sb[:].unsqueeze(2).broadcast_to((128, KC, TC)),
                        ALU.mult)
                    p_w = mark(PO.engine_nop(), "po")

                    dAp = [dA_b[0], rbuf]
                    up = [u_b[0], vabuf]
                    tp = [t_b[0], xconv]
                    dwait(SY, "dbs", d_bcs)
                    v_scan, a_dA, v_ym = {}, {}, {}
                    for n in range(NST):
                        r = n % 2
                        sm = "dbc0" if r == 0 else "dbc1"
                        if n >= 2:
                            wait(SY, "ve", v_scan[n - 2])
                            wait(SY, "ve", v_ym[n - 2])
                        d_b = mark(SY.dma_start(
                            Bb[r][:], bc_d[n:n + 1, t0:t0 + TC]
                            .partition_broadcast(128).squeeze(1)), sm, 16)
                        d_c = mark(SY.dma_start(
                            Cb[r][:], bc_d[16 + n:17 + n, t0:t0 + TC]
                            .partition_broadcast(128).squeeze(1)), sm, 16)
                        if n <= 1:
                            wait(AC, "ve", v_dt)
                        if n >= 2:
                            wait(AC, "ve", v_scan[n - 2])
                        for c in range(KC):
                            AC.activation(dAp[r][:, c, :], dt_sb[:, c, :],
                                          AF.Exp, scale=A_sb[:, c, n:n + 1])
                        a_dA[n] = ac_mark()
                        if n == 0:
                            wait(VE, "po", p_w)
                        dwait(VE, sm, d_c)
                        VE.tensor_tensor(
                            up[r][:], w_all,
                            Bb[r][:].unsqueeze(1).broadcast_to((128, KC, TC)),
                            ALU.mult)
                        bubble()
                        wait(VE, "ac", a_dA[n])
                        for c in range(KC):
                            VE.tensor_tensor_scan(h_b[r][:, c, :],
                                                  dAp[r][:, c, :],
                                                  up[r][:, c, :],
                                                  hcarry[:, c, n:n + 1],
                                                  ALU.mult, ALU.add)
                        v_scan[n] = mark(VE.engine_nop(), "ve")
                        bubble()
                        VE.tensor_tensor(
                            tp[r][:], h_b[r][:],
                            Cb[r][:].unsqueeze(1).broadcast_to((128, KC, TC)),
                            ALU.mult)
                        v_ym[n] = mark(VE.engine_nop(), "ve")
                        VE.tensor_copy(hcarry[:, :, n:n + 1],
                                       h_b[r][:, :, TC - 1:TC])
                        VE.tensor_add(y_all[:], y_all[:], tp[r][:])

                    bubble()
                    VE.tensor_mul(ygbf[:], y_all[:], gate_sb[:])
                    v_yg = mark(VE.engine_nop(), "ve")

                    wait(PE, "ve", v_yg)
                    wait(PE, "ac", a_dt2)
                    for ot in range(8):
                        bank, colo = 1 + ot % 4, (ot // 4) * TC
                        for kc2 in range(KC):
                            PE.matmul(pb[bank][:, colo:colo + TC],
                                      owT_r[:, kc2, ot * 128:(ot + 1) * 128],
                                      ygbf[:, kc2, :],
                                      start=(kc2 == 0), stop=(kc2 == KC - 1))
                    t_op = pe_mark()
                    state["t_op"] = t_op
                    wait(VE, "pe", t_op)
                    for ot in range(8):
                        bank, colo = 1 + ot % 4, (ot // 4) * TC
                        VE.scalar_tensor_tensor(z_sb[:, ot, :],
                                                pb[bank][:, colo:colo + TC],
                                                ob_sb[:, ot:ot + 1],
                                                x_fm[:, ot, :],
                                                ALU.add, ALU.add)
                    v_z = mark(VE.engine_nop(), "ve")

                    wait(AC, "ve", v_z)
                    AC.activation(sq_sb[:], z_sb[:], AF.Square)
                    a_sq = ac_mark()
                    wait(PE, "ac", a_sq)
                    for kc2 in range(KC):
                        PE.matmul(pb[5][0:2, 0:TC], ones_sb[:, 0:2],
                                  z_sb[:, kc2, :],
                                  start=(kc2 == 0), stop=(kc2 == KC - 1))
                    for kc2 in range(KC):
                        PE.matmul(pb[5][0:2, TC:2 * TC], ones_sb[:, 0:2],
                                  sq_sb[:, kc2, :],
                                  start=(kc2 == 0), stop=(kc2 == KC - 1))
                    t_s1 = pe_mark()

                    def stats(t_sums):
                        wait(VE, "pe", t_sums)
                        VE.tensor_scalar(stat[0:1, 0:TC], pb[5][0:1, 0:TC],
                                         INV_D, None, ALU.mult)
                        VE.tensor_scalar(stat[0:1, TC:2 * TC],
                                         pb[5][0:1, TC:2 * TC],
                                         INV_D, None, ALU.mult)
                        VE.tensor_mul(t1b[0][0:1, 0:TC], stat[0:1, 0:TC],
                                      stat[0:1, 0:TC])
                        bubble()
                        VE.tensor_sub(stat[0:1, TC:2 * TC],
                                      stat[0:1, TC:2 * TC], t1b[0][0:1, 0:TC])
                        v_var = mark(VE.engine_nop(), "ve")
                        wait(AC, "ve", v_var)
                        AC.activation(stat[0:1, TC:2 * TC],
                                      stat[0:1, TC:2 * TC], AF.Ln,
                                      bias=eps_sb[0:1, :])
                        AC.activation(stat[0:1, TC:2 * TC],
                                      stat[0:1, TC:2 * TC], AF.Exp,
                                      scale=negh_sb[0:1, :])
                        a_st = ac_mark()
                        wait(SY, "ac", a_st)
                        wait(SY, "ve")
                        d_w = mark(SY.dma_start(stat_d[0:1, 0:512], stat[:]),
                                   "dst", 16)
                        dwait(SY, "dst", d_w)
                        d_r = mark(SY.dma_start(
                            statb[:], stat_d[0:1, 0:512]
                            .partition_broadcast(128).squeeze(1)), "dst", 16)
                        dwait(VE, "dst", d_r)
                        return d_r

                    stats(t_s1)
                    for kc2 in range(0, KC, 2):
                        for q in range(2):
                            VE.tensor_sub(t1b[q][:], z_sb[:, kc2 + q, :],
                                          statb[:, 0:TC])
                        for q in range(2):
                            VE.tensor_mul(t2b[q][:], t1b[q][:],
                                          statb[:, TC:2 * TC])
                        for q in range(2):
                            VE.tensor_scalar(t2b[q][:], t2b[q][:],
                                             lng_sb[:, kc2 + q:kc2 + q + 1],
                                             None, ALU.mult)
                        for q in range(2):
                            VE.tensor_add(t2b[q][:], t2b[q][:],
                                          x_fm[:, kc2 + q, :])
                        for q in range(2):
                            VE.tensor_scalar(zz_sb[:, kc2 + q, :], t2b[q][:],
                                             lnb_sb[:, kc2 + q:kc2 + q + 1],
                                             None, ALU.add)
                    v_zz = mark(VE.engine_nop(), "ve")

                    wait(AC, "ve", v_zz)
                    AC.activation(sq_sb[:], zz_sb[:], AF.Square)
                    a_sq2 = ac_mark()
                    wait(PE, "ac", a_sq2)
                    for kc2 in range(KC):
                        PE.matmul(pb[5][0:2, 0:TC], ones_sb[:, 0:2],
                                  zz_sb[:, kc2, :],
                                  start=(kc2 == 0), stop=(kc2 == KC - 1))
                    for kc2 in range(KC):
                        PE.matmul(pb[5][0:2, TC:2 * TC], ones_sb[:, 0:2],
                                  sq_sb[:, kc2, :],
                                  start=(kc2 == 0), stop=(kc2 == KC - 1))
                    t_s2 = pe_mark()
                    stats(t_s2)
                    for kc2 in range(0, KC, 2):
                        wait(VE, "dx2")
                        for q in range(2):
                            VE.tensor_sub(t1b[q][:], zz_sb[:, kc2 + q, :],
                                          statb[:, 0:TC])
                        for q in range(2):
                            VE.tensor_mul(t2b[q][:], t1b[q][:],
                                          statb[:, TC:2 * TC])
                        v_x2c = mark(VE.engine_nop(), "ve")
                        wait(SY, "ve", v_x2c)
                        for q in range(2):
                            mark(SY.dma_start(
                                x2_d[(kc2 + q) * 128:(kc2 + q + 1) * 128,
                                     t0:t0 + TC], t2b[q][:]), "dx2", 16)
                        wait(SY, "dx2")
                    state["v_x2"] = mark(VE.engine_nop(), "ve")

                for ch in range(NCH):
                    mamba_chunk(ch)
                barrier()

            t_x2 = dout("x2", [D, L])
            if t_x2 is not None:
                mark(SY.dma_start(t_x2, x2_d), "st", 16)
                SY.wait_ge(sems["st"][0], sems["st"][1])

            if "P4" in phset:
              with ExitStack() as f:
                ef = f.enter_context
                y_acc = ef(nc.sbuf_tensor("y_acc", [128, KC, L], F32))
                sqf = ef(nc.sbuf_tensor("sqf", [128, 2, 512], F32))
                statf = ef(nc.sbuf_tensor("statf", [1, 2 * L], F32))
                murd_tok = ef(nc.sbuf_tensor("murd_tok", [128, 16], F32))
                rstd_tok = ef(nc.sbuf_tensor("rstd_tok", [128, 16], F32))
                lg_sb = ef(nc.sbuf_tensor("lg_sb", [128, 8], F32))
                lg2_sb = ef(nc.sbuf_tensor("lg2_sb", [128, 8], F32))
                mx_sb = ef(nc.sbuf_tensor("mx_sb", [128, 1], F32))
                ex_sb = ef(nc.sbuf_tensor("ex_sb", [128, 8], F32))
                sm_sb = ef(nc.sbuf_tensor("sm_sb", [128, 1], F32))
                rs_sb = ef(nc.sbuf_tensor("rs_sb", [128, 1], F32))
                p_colt = ef(nc.sbuf_tensor("p_colt", [128, 16], F32))

                wait(SY, "dx2")
                d_x2 = mark(SY.dma_start(
                    y_acc[:], x2_d.rearrange("(c p) t -> p c t", p=128)),
                    "d4a", 16)
                dwait(AC, "d4a", d_x2)
                dwait(VE, "d4a", d_x2)
                dwait(PE, "d4a", d_x2)
                a_x2ld = ac_mark()
                wait(VE, "ac", a_x2ld)
                wait(PE, "ac", a_x2ld)

                for tb in range(4):
                    ts0 = tb * 512
                    sqmk = {}
                    for kc2 in range(KC):
                        r = kc2 % 2
                        if kc2 >= 2:
                            wait(AC, "pe", sqmk[kc2 - 2])
                        AC.activation(sqf[:, r, :], y_acc[:, kc2, ts0:ts0 + 512],
                                      AF.Square)
                        a_q = ac_mark()
                        wait(PE, "ac", a_q)
                        PE.matmul(pb[0][0:2, 0:512], ones_sb[:, 0:2],
                                  y_acc[:, kc2, ts0:ts0 + 512],
                                  start=(kc2 == 0), stop=(kc2 == KC - 1))
                        PE.matmul(pb[1][0:2, 0:512], ones_sb[:, 0:2],
                                  sqf[:, r, :],
                                  start=(kc2 == 0), stop=(kc2 == KC - 1))
                        sqmk[kc2] = pe_mark()
                    t_fs = sqmk[KC - 1]
                    wait(VE, "pe", t_fs)
                    VE.tensor_scalar(statf[0:1, ts0:ts0 + 512],
                                     pb[0][0:1, 0:512], INV_D, None, ALU.mult)
                    VE.tensor_scalar(statf[0:1, L + ts0:L + ts0 + 512],
                                     pb[1][0:1, 0:512], INV_D, None, ALU.mult)
                    v_fs = mark(VE.engine_nop(), "ve")
                    wait(PE, "ve", v_fs)
                stf = ef(nc.sbuf_tensor("stf", [1, L], F32))
                VE.tensor_mul(stf[0:1, :], statf[0:1, 0:L], statf[0:1, 0:L])
                bubble()
                VE.tensor_sub(statf[0:1, L:2 * L], statf[0:1, L:2 * L],
                              stf[0:1, :])
                v_vf = mark(VE.engine_nop(), "ve")
                wait(AC, "ve", v_vf)
                AC.activation(statf[0:1, L:2 * L], statf[0:1, L:2 * L], AF.Ln,
                              bias=eps_sb[0:1, :])
                AC.activation(statf[0:1, L:2 * L], statf[0:1, L:2 * L], AF.Exp,
                              scale=negh_sb[0:1, :])
                a_rf = ac_mark()
                wait(VE, "ac", a_rf)
                VE.tensor_mul(statf[0:1, 0:L], statf[0:1, 0:L],
                              statf[0:1, L:2 * L])
                bubble()
                VE.tensor_scalar(statf[0:1, 0:L], statf[0:1, 0:L], -1.0,
                                 None, ALU.mult)
                v_murd = mark(VE.engine_nop(), "ve")
                wait(PE, "ve", v_murd)
                for tt in range(16):
                    bank = 2 + (tt % 2)
                    tr(pb[bank][0:128, 0:1],
                                 statf[0:1, tt * 128:(tt + 1) * 128])
                    tr(pb[bank][0:128, 1:2],
                                 statf[0:1, L + tt * 128:L + (tt + 1) * 128])
                    t_tt = pe_mark()
                    wait(VE, "pe", t_tt)
                    VE.tensor_copy(murd_tok[:, tt:tt + 1], pb[bank][:, 0:1])
                    VE.tensor_copy(rstd_tok[:, tt:tt + 1], pb[bank][:, 1:2])
                    v_tt = mark(VE.engine_nop(), "ve")
                    wait(PE, "ve", v_tt)
                if "lgt" in dbg:
                    lgt_sb = ef(nc.sbuf_tensor("lgt_sb", [128, 128], F32))
                xrt = [ef(nc.sbuf_tensor(f"xrt{i}", [128, KC, 128], BF16))
                       for i in range(2)]
                for tt in range(16):
                    rr = tt % 2
                    if tt >= 2:
                        wait(AC, "pe")
                    for kc2 in range(KC):
                        AC.activation(xrt[rr][:, kc2, :],
                                      y_acc[:, kc2, tt * 128:(tt + 1) * 128],
                                      AF.Identity)
                    a_xr = ac_mark()
                    wait(PE, "ac", a_xr)
                    for kc2 in range(KC):
                        PE.matmul(pb[4][:, 0:8],
                                  xrt[rr][:, kc2, :],
                                  wrg_sb[:, kc2, :],
                                  start=(kc2 == 0), stop=(kc2 == KC - 1))
                    t_lg = pe_mark()
                    wait(VE, "pe", t_lg)
                    VE.tensor_copy(lg_sb[:], pb[4][:, 0:8])
                    bubble()
                    VE.tensor_scalar(lg_sb[:], lg_sb[:], rstd_tok[:, tt:tt + 1],
                                     None, ALU.mult)
                    bubble()
                    bubble()
                    VE.scalar_tensor_tensor(lg2_sb[:], ceb_sb[:],
                                            murd_tok[:, tt:tt + 1], lg_sb[:],
                                            ALU.mult, ALU.add)
                    bubble()
                    VE.tensor_add(lg2_sb[:], lg2_sb[:], bewb_sb[:])
                    bubble()
                    VE.reduce_max(mx_sb[:], lg2_sb[:], axis=AX.X, negate=True)
                    v_lg = mark(VE.engine_nop(), "ve")
                    wait(AC, "ve", v_lg)
                    AC.activation(ex_sb[:], lg2_sb[:], AF.Exp, bias=mx_sb[:])
                    a_ex = ac_mark()
                    wait(VE, "ac", a_ex)
                    VE.reduce_sum(sm_sb[:], ex_sb[:], axis=AX.X)
                    bubble()
                    VE.reciprocal(rs_sb[:], sm_sb[:])
                    bubble()
                    VE.tensor_mul(p_colt[:, tt:tt + 1], ex_sb[:, 0:1], rs_sb[:])
                    if "lgt" in dbg:
                        VE.tensor_copy(lgt_sb[:, tt * 8:tt * 8 + 8], lg2_sb[:])
                    v_p = mark(VE.engine_nop(), "ve")
                    wait(PE, "ve", v_p)
                prow = ef(nc.sbuf_tensor("prow", [1, L], F32))
                p0b = ef(nc.sbuf_tensor("p0b", [128, L], F32))
                p1b = ef(nc.sbuf_tensor("p1b", [128, L], F32))
                for tt in range(16):
                    bank = 2 + (tt % 2)
                    tr(pb[bank][0:1, 0:128], p_colt[:, tt:tt + 1])
                    t_pt = pe_mark()
                    wait(VE, "pe", t_pt)
                    VE.tensor_copy(prow[0:1, tt * 128:(tt + 1) * 128],
                                   pb[bank][0:1, 0:128])
                    v_pr = mark(VE.engine_nop(), "ve")
                    wait(PE, "ve", v_pr)
                v_prow = mark(VE.engine_nop(), "ve")
                wait(SY, "ve", v_prow)
                d_pw = mark(SY.dma_start(prow_d, prow[:]), "dst", 16)
                dwait(SY, "dst", d_pw)
                d_pr = mark(SY.dma_start(
                    p0b[:], prow_d.partition_broadcast(128).squeeze(1)),
                    "dst", 16)
                dwait(VE, "dst", d_pr)
                VE.tensor_scalar(p1b[:], p0b[:], -1.0, 1.0, ALU.mult, ALU.add)
                v_p1 = mark(VE.engine_nop(), "ve")

                fex = ExitStack()
                efx = fex.enter_context
                act_sb = efx(nc.sbuf_tensor("act_sb", [128, 32, FT], BF16))
                x2bft = efx(nc.sbuf_tensor("x2bft", [128, KC, FT], BF16))
                w1sl = [efx(nc.sbuf_tensor(f"w1sl{i}", [128, KC, 512], BF16))
                        for i in range(2)]
                w2sl = [efx(nc.sbuf_tensor(f"w2sl{i}", [128, 4, D], BF16))
                        for i in range(2)]
                yt_b = [efx(nc.sbuf_tensor(f"ytb{i}", [128, FT], F32))
                        for i in range(2)]
                for tb in range(NFT):
                    ts0 = tb * FT
                    wait(AC, "pe")
                    for kc2 in range(KC):
                        AC.activation(x2bft[:, kc2, :],
                                      y_acc[:, kc2, ts0:ts0 + FT], AF.Identity)
                    a_xt = ac_mark()
                    wait(PE, "ac", a_xt)
                    for ex in range(2):
                        w1src = w1bf_d if ex == 0 else w1pbf_d
                        w2src = w2bf_d if ex == 0 else w2pbf_d
                        b1src = b1_sb if ex == 0 else b1p_sb
                        b2src = b2_sb if ex == 0 else b2p_sb
                        psrc = p0b if ex == 0 else p1b
                        last_gelu = {0: 0, 1: 0}
                        fcpre = {}
                        for fc in range(8):
                            r = fc % 2
                            sm = "d4a" if r == 0 else "d4b"
                            if fc >= 2:
                                wait(SY, "pe", fcpre[fc - 2])
                            d_w1 = mark(SY.dma_start(
                                w1sl[r][:],
                                w1src[:, fc * 512:(fc + 1) * 512].rearrange(
                                    "(c p) f -> p c f", p=128)), sm, 16)
                            dwait(PE, sm, d_w1)
                            for ftl in range(4):
                                bank = ftl % 2
                                wait(PE, "ac", last_gelu[bank])
                                for kc2 in range(KC):
                                    PE.matmul(
                                        pb[bank][:, 0:FT],
                                        w1sl[r][:, kc2, ftl * 128:(ftl + 1) * 128],
                                        x2bft[:, kc2, :],
                                        start=(kc2 == 0), stop=(kc2 == KC - 1))
                                t_pre = pe_mark()
                                fcpre[fc] = t_pre
                                wait(AC, "pe", t_pre)
                                fcol = fc * 4 + ftl
                                AC.activation(act_sb[:, fcol, :],
                                              pb[bank][:, 0:FT],
                                              AF.Gelu_apprx_tanh,
                                              bias=b1src[:, fcol:fcol + 1])
                                last_gelu[bank] = ac_mark()
                        wait(PE, "ac", max(last_gelu.values()))
                        fcy = {}
                        for fc in range(8):
                            r = fc % 2
                            sm = "d4a" if r == 0 else "d4b"
                            if fc >= 2:
                                wait(SY, "pe", fcy[fc - 2])
                            d_w2 = mark(SY.dma_start(
                                w2sl[r][:],
                                w2src[fc * 512:(fc + 1) * 512, :].rearrange(
                                    "(q p) h -> p q h", p=128)), sm, 16)
                            dwait(PE, sm, d_w2)
                            for ht in range(8):
                                for fi in range(4):
                                    PE.matmul(
                                        pb[ht][:, 0:FT],
                                        w2sl[r][:, fi, ht * 128:(ht + 1) * 128],
                                        act_sb[:, fc * 4 + fi, :],
                                        start=(fc == 0 and fi == 0),
                                        stop=(fc == 7 and fi == 3))
                            fcy[fc] = pe_mark()
                        t_y = fcy[7]
                        wait(VE, "pe", t_y)
                        for ht in range(0, 8, 2):
                            for q in range(2):
                                VE.scalar_tensor_tensor(
                                    yt_b[q][:], pb[ht + q][:, 0:FT],
                                    b2src[:, ht + q:ht + q + 1],
                                    psrc[:, ts0:ts0 + FT],
                                    ALU.add, ALU.mult)
                            for q in range(2):
                                VE.tensor_add(
                                    y_acc[:, ht + q, ts0:ts0 + FT],
                                    y_acc[:, ht + q, ts0:ts0 + FT],
                                    yt_b[q][:])
                        v_ev = mark(VE.engine_nop(), "ve")
                        wait(PE, "ve", v_ev)

                fex.close()
                t_lgt = dout("lgt", [128, 128])
                if t_lgt is not None:
                    mark(SY.dma_start(t_lgt, lgt_sb[:]), "dst", 16)
                t_pr = dout("prow", [1, L])
                if t_pr is not None:
                    mark(SY.dma_start(t_pr, prow[:]), "dst", 16)
                t_ya = dout("yacc", [D, L])
                if t_ya is not None:
                    quiesce(VE)
                    v_yd = mark(VE.engine_nop(), "ve")
                    wait(SY, "ve", v_yd)
                    mark(SY.dma_start(
                        t_ya.rearrange("(c p) t -> p c t", p=128), y_acc[:]),
                        "dst", 16)
                    dwait(SY, "dst", sems["dst"][1])
                sq5 = sqf
                stat5 = statf
                ost = [ef(nc.sbuf_tensor(f"ost{i}", [128, D], F32))
                       for i in range(2)]
                stat5b = ef(nc.sbuf_tensor("stat5b", [128, 1024], F32))
                t15 = [ef(nc.sbuf_tensor(f"t15{i}", [128, 512], F32))
                       for i in range(2)]
                x3 = ef(nc.sbuf_tensor("x3", [128, KC, 512], F32))
                st_marks = {}
                for tb in range(4):
                    ts0 = tb * 512
                    sqmk = {}
                    for kc2 in range(KC):
                        r = kc2 % 2
                        if kc2 >= 2:
                            wait(AC, "pe", sqmk[kc2 - 2])
                        AC.activation(sq5[:, r, :], y_acc[:, kc2, ts0:ts0 + 512],
                                      AF.Square)
                        a_q = ac_mark()
                        wait(PE, "ac", a_q)
                        PE.matmul(pb[0][0:2, 0:512], ones_sb[:, 0:2],
                                  y_acc[:, kc2, ts0:ts0 + 512],
                                  start=(kc2 == 0), stop=(kc2 == KC - 1))
                        PE.matmul(pb[1][0:2, 0:512], ones_sb[:, 0:2],
                                  sq5[:, r, :],
                                  start=(kc2 == 0), stop=(kc2 == KC - 1))
                        sqmk[kc2] = pe_mark()
                    t_s5 = sqmk[KC - 1]
                    wait(VE, "pe", t_s5)
                    VE.tensor_scalar(stat5[0:1, 0:512], pb[0][0:1, 0:512],
                                     INV_D, None, ALU.mult)
                    VE.tensor_scalar(stat5[0:1, 512:1024], pb[1][0:1, 0:512],
                                     INV_D, None, ALU.mult)
                    VE.tensor_mul(sq5[0:1, 0, :], stat5[0:1, 0:512],
                                  stat5[0:1, 0:512])
                    bubble()
                    VE.tensor_sub(stat5[0:1, 512:1024], stat5[0:1, 512:1024],
                                  sq5[0:1, 0, :])
                    v_v5 = mark(VE.engine_nop(), "ve")
                    wait(AC, "ve", v_v5)
                    AC.activation(stat5[0:1, 512:1024], stat5[0:1, 512:1024],
                                  AF.Ln, bias=eps_sb[0:1, :])
                    AC.activation(stat5[0:1, 512:1024], stat5[0:1, 512:1024],
                                  AF.Exp, scale=negh_sb[0:1, :])
                    a_r5 = ac_mark()
                    wait(SY, "ac", a_r5)
                    wait(SY, "ve")
                    d_w5 = mark(SY.dma_start(stat_d[0:1, 0:1024],
                                             stat5[0:1, 0:1024]), "dst", 16)
                    dwait(SY, "dst", d_w5)
                    d_r5 = mark(SY.dma_start(
                        stat5b[:], stat_d[0:1, 0:1024]
                        .partition_broadcast(128).squeeze(1)), "dst", 16)
                    dwait(VE, "dst", d_r5)
                    for kc2 in range(0, KC, 2):
                        for q in range(2):
                            VE.tensor_sub(t15[q][:],
                                          y_acc[:, kc2 + q, ts0:ts0 + 512],
                                          stat5b[:, 0:512])
                        for q in range(2):
                            VE.tensor_mul(x3[:, kc2 + q, :], t15[q][:],
                                          stat5b[:, 512:1024])
                    v_x3 = mark(VE.engine_nop(), "ve")
                    wait(PE, "ve", v_x3)
                    for tt in range(4):
                        g = tt % 2
                        if (tb, tt) in st_marks:
                            pass
                        prev = st_marks.get(g)
                        for kc2 in range(KC):
                            bank = 4 + (kc2 % 2)
                            tr(pb[bank][:, 0:128],
                                         x3[:, kc2, tt * 128:(tt + 1) * 128])
                            t_t5 = pe_mark()
                            wait(VE, "pe", t_t5)
                            if prev is not None and kc2 == 0:
                                wait(VE, "st")
                            VE.tensor_copy(ost[g][:, kc2 * 128:(kc2 + 1) * 128],
                                           pb[bank][:, 0:128])
                            v_t5 = mark(VE.engine_nop(), "ve")
                            wait(PE, "ve", v_t5)
                        v_o = mark(VE.engine_nop(), "ve")
                        wait(SY, "ve", v_o)
                        st_marks[g] = mark(SY.dma_start(
                            out_d[ts0 + tt * 128:ts0 + (tt + 1) * 128, :],
                            ost[g][:]), "st", 16)
                SY.wait_ge(sems["st"][0], sems["st"][1])

            quiesce(SY)
            PO.wait_ge(sems["dpo"][0], sems["dpo"][1])
            SY.wait_ge(sems["st"][0], sems["st"][1])

        _body(None)

        def _replay(stream):
            def f(_eng):
                for fn, a, k, h in stream:
                    inst = fn(*a, **k)
                    for sem, n in h.incs:
                        inst.then_inc(sem, n)
            return f

        block.sync(_replay(streams["sy"]))
        block.vector(_replay(streams["ve"]))
        block.scalar(_replay(streams["ac"]))
        block.tensor(_replay(streams["pe"]))
        block.gpsimd(_replay(streams["po"]))

    return nc, dbg_outs


def _shard_inputs(inputs):
    x = np.ascontiguousarray(np.asarray(inputs["L0_attn_x"], dtype=np.float32))
    conv_w = np.ascontiguousarray(
        np.asarray(inputs["L0_attn_conv_w"], dtype=np.float32)[:, 0, :])

    def f32(k):
        return np.ascontiguousarray(np.asarray(inputs[k], dtype=np.float32))

    common = {
        "ipw": f32("L0_attn_in_proj_w"), "ipb": f32("L0_attn_in_proj_b"),
        "cw": conv_w, "cb": f32("L0_attn_conv_b"),
        "bcw": f32("L0_attn_bc_w"), "bcb": f32("L0_attn_bc_b"),
        "dtw": f32("L0_attn_dt_w"), "dtb": f32("L0_attn_dt_b"),
        "alog": f32("L0_attn_A_log"), "dvec": f32("L0_attn_D"),
        "ow": f32("L0_attn_out_w"), "ob": f32("L0_attn_out_b"),
        "lng": f32("L0_attn_ln_g"), "lnb": f32("L0_attn_ln_b"),
        "w1": f32("L0_ffn_W1"), "b1": f32("L0_ffn_b1"),
        "w2": f32("L0_ffn_W2"), "b2": f32("L0_ffn_b2"),
        "g2": f32("L0_ffn_ln_gamma"), "be2": f32("L0_ffn_ln_beta"),
        "wr": f32("L0_ffn_Wr"), "br": f32("L0_ffn_br"),
    }
    in_maps = []
    for c in range(8):
        mm = dict(common)
        mm["x"] = np.ascontiguousarray(x[c])
        in_maps.append(mm)
    return in_maps


def kernel(**inputs):
    in_maps = _shard_inputs(inputs)
    nc, _ = build()
    res = run_bass_kernel_spmd(nc, in_maps, core_ids=list(range(8)))
    out = np.stack([np.asarray(res.results[c]["out"]) for c in range(8)], axis=0)
    return np.ascontiguousarray(out.astype(np.float32))
